# version 30
# baseline (speedup 1.0000x reference)
import contextlib
import numpy as np
import ml_dtypes
import concourse.bass as bass
import concourse.mybir as mybir
from concourse.bass_utils import run_bass_kernel_spmd

F32 = mybir.dt.float32
BF16 = mybir.dt.bfloat16
U8 = mybir.dt.uint8
AF = mybir.ActivationFunctionType
ALU = mybir.AluOpType

D, T, HD, NP, DC, OWNT = 1024, 4096, 64, 8, 8, 2048
DFF, RANK = 4096, 256
SCALE_QK = 0.125
VSCALE = 1e-3
EPS = 1e-5


def prod(xs):
    r = 1
    for x in xs:
        r *= x
    return r


class Build:
    ENGINES = ("sync", "tensor", "scalar", "vector", "gpsimd")

    def __init__(self, nc):
        self.nc = nc
        self.streams = {e: [] for e in self.ENGINES}
        self.sem_handles = {}
        self.cnt = {"T": 0, "S": 0, "V": 0, "P": 0}
        self.last_wait = {e: {s: 0 for s in self.cnt} for e in self.ENGINES}

    N_DMA_SEMS = 96

    def init_sems(self, stack):
        for s in self.cnt:
            self.sem_handles[s] = stack.enter_context(self.nc.semaphore("sem_" + s))
        self.dma_sems = [stack.enter_context(self.nc.semaphore(f"dsem{i}"))
                         for i in range(self.N_DMA_SEMS)]
        self.dma_n = 0

    ENG_SEM = {"tensor": "T", "scalar": "S", "vector": "V", "gpsimd": "P"}

    def op(self, eng, fn, inc=None, ss=False):
        inc = self.ENG_SEM[eng]
        h = self.sem_handles[inc]
        selfw = 0
        if ss:
            v = self.cnt[inc]
            if v > 0 and self.last_wait[eng][inc] < v:
                selfw = v
                self.last_wait[eng][inc] = v
        self.cnt[inc] += 1

        def wrapped(e, fn=fn, h=h, w=selfw):
            i = fn(e)
            if w:
                i.wait_op(h, w, "sem-ge")
            i.then_inc(h, 1)
        self.streams[eng].append(wrapped)
        return self.cnt[inc]

    def _dma_on(self, queue, dst, src, slot=None):
        key = f"dma{self.dma_n}"
        self.dma_n += 1
        if not hasattr(self, "dma_slots"):
            self.dma_slots = {}
        if slot is None:
            slot = key
        if slot not in self.dma_slots:
            assert len(self.dma_slots) < self.N_DMA_SEMS, "DMA sem pool exhausted"
            self.dma_slots[slot] = [self.dma_sems[len(self.dma_slots)], 0]
        ent = self.dma_slots[slot]
        ent[1] += 16
        h = ent[0]
        self.sem_handles[key] = h
        self.cnt[key] = ent[1]
        for e_ in self.ENGINES:
            self.last_wait[e_][key] = 0
        self.streams[queue].append(
            lambda e, d=dst, s=src, h=h: e.dma_start(out=d, in_=s).then_inc(h, 16))
        return key

    def dma(self, dst, src, slot=None):
        return self._dma_on("sync", dst, src, slot)

    def dma2(self, dst, src, slot=None):
        return self._dma_on("scalar", dst, src, slot)

    def wait(self, eng, sem, val):
        if not val or val <= 0:
            return
        if sem.startswith("dma"):
            val = self.cnt[sem]
        assert val <= self.cnt[sem], f"{eng} waits future {sem}={val} (now {self.cnt[sem]})"
        if self.last_wait[eng][sem] >= val:
            return
        self.last_wait[eng][sem] = val
        h = self.sem_handles[sem]
        self.streams[eng].append(lambda e, h=h, v=val: e.wait_ge(h, v))

    def barrier(self):
        return dict(self.cnt)

    def wait_all(self, eng, counts, sems=("T", "S", "V")):
        for s in sems:
            self.wait(eng, s, counts[s])

    def replay(self, block):
        for name in self.ENGINES:
            ops = self.streams[name]

            def body(eng, ops=ops):
                for f in ops:
                    f(eng)
            body.__name__ = "_" + name
            getattr(block, name)(body)


def view(pool, byte_off, dtype, shape):
    esz = mybir.dt.size(dtype)
    assert byte_off % 32 == 0 and byte_off % esz == 0
    n = prod(shape[1:])
    ap = pool.bitcast(dtype).ap()[0:shape[0], byte_off // esz: byte_off // esz + n]
    if len(shape) > 2:
        names = [f"a{i}" for i in range(len(shape) - 1)]
        spec = "p (" + " ".join(names) + ") -> p " + " ".join(names)
        ap = ap.rearrange(spec, **{names[i]: shape[i + 1] for i in range(len(shape) - 1)})
    return ap


def build_program(upto="full", dumps=()):
    nc = bass.Bass("TRN2", target_bir_lowering=False)

    x_t = nc.dram_tensor("x_t", [D, T], F32, kind="ExternalInput").ap()
    x_own_t = nc.dram_tensor("x_own_t", [D, OWNT], F32, kind="ExternalInput").ap()
    W_attn = nc.dram_tensor("W_attn", [D, 3 * D], F32, kind="ExternalInput").ap()
    W_projd = nc.dram_tensor("W_proj", [D, D], F32, kind="ExternalInput").ap()
    U1d = nc.dram_tensor("U1", [D, RANK], F32, kind="ExternalInput").ap()
    V1d = nc.dram_tensor("V1", [RANK, DFF], F32, kind="ExternalInput").ap()
    U2d = nc.dram_tensor("U2", [DFF, RANK], F32, kind="ExternalInput").ap()
    V2d = nc.dram_tensor("V2", [RANK, D], F32, kind="ExternalInput").ap()
    b1d = nc.dram_tensor("b1l", [128, DFF // 128], F32, kind="ExternalInput").ap()
    b2d = nc.dram_tensor("b2l", [128, D // 128], F32, kind="ExternalInput").ap()
    masks_d = nc.dram_tensor("masks", [128, 2, 2, 128], BF16, kind="ExternalInput").ap()
    identb_d = nc.dram_tensor("identb", [128, 128], BF16, kind="ExternalInput").ap()
    identf_d = nc.dram_tensor("identf", [128, 128], F32, kind="ExternalInput").ap()
    onescol_d = nc.dram_tensor("onescol", [128, 1], BF16, kind="ExternalInput").ap()
    onesrow_d = nc.dram_tensor("onesrow", [1, 128], F32, kind="ExternalInput").ap()
    y_out = nc.dram_tensor("y", [OWNT, D], F32, kind="ExternalOutput").ap()
    kf2_d = nc.dram_tensor("kf2scratch", [128, NP, 2048], BF16, kind="Internal").ap()

    dump_specs = {
        "lnxb": (BF16, 0), "kf": (BF16, 1), "vt": (BF16, 2), "qf": (BF16, 3),
        "msnap": (BF16, 4), "prefsnap": (F32, 5), "avpart": (BF16, 6), "resid": (F32, 7),
        "wxb": (BF16, 8),
    }
    dump_t = {}

    with contextlib.ExitStack() as stack:
        E_ = stack.enter_context
        poolA = E_(nc.sbuf_tensor("poolA", [128, 65536], U8))
        poolB = E_(nc.sbuf_tensor("poolB", [128, 32768], U8))
        poolC = E_(nc.sbuf_tensor("poolC", [128, 65536], U8))
        poolD = E_(nc.sbuf_tensor("poolD", [128, 28672], U8))
        identb = E_(nc.sbuf_tensor("sb_identb", [128, 128], BF16))
        identf = E_(nc.sbuf_tensor("sb_identf", [128, 128], F32))
        onescol = E_(nc.sbuf_tensor("sb_onescol", [128, 1], BF16))
        onesrow = E_(nc.sbuf_tensor("sb_onesrow", [1, 128], F32))
        masksb = E_(nc.sbuf_tensor("sb_masksb", [128, 2, 2, 128], BF16))
        consts = E_(nc.sbuf_tensor("consts", [128, 3], F32))
        msnap = E_(nc.sbuf_tensor("msnap", [128, 8, 512], BF16))
        prefsnap = E_(nc.sbuf_tensor("prefsnap", [128, 17, 8], F32))
        b1s = E_(nc.sbuf_tensor("sb_b1s", [128, DFF // 128], F32))
        msum_t = E_(nc.sbuf_tensor("sb_msum", [128, 512], F32))
        prefsum_t = E_(nc.sbuf_tensor("sb_prefsum", [128, 8], F32))
        b2s = E_(nc.sbuf_tensor("sb_b2s", [128, D // 128], F32))
        psT = E_(nc.psum_tensor("ps", [128, 4096], F32))

        b = Build(nc)
        b.init_sems(stack)

        def vsync():
            b.wait("vector", "V", b.cnt["V"])

        def ssync():
            b.wait("scalar", "S", b.cnt["S"])

        def tfence():
            return b.cnt["T"]

        lnxh = [view(poolC, 0, BF16, [128, DC, 2048]),
                view(poolC, 32768, BF16, [128, DC, 2048])]
        kf = view(poolA, 0, BF16, [128, NP, 2048])
        vt = view(poolA, 32768, BF16, [128, 16, 1024])
        resid = view(poolA, 0, F32, [128, DC, OWNT])
        qf = view(poolB, 0, BF16, [128, NP, OWNT])
        v1b = view(poolB, 0, BF16, [128, 2, DFF])
        u2b = view(poolB, 16384, BF16, [128, 32, RANK])
        avpart = view(poolC, 0, BF16, [128, NP, OWNT])
        x2stage = view(poolC, 32768, F32, [128, DC, 512])
        tmp4 = view(poolC, 57344, F32, [128, 512])
        ln2blk = view(poolC, 49152, BF16, [128, DC, 512])
        xown = view(poolC, 32768, F32, [128, DC, 512])
        ybuf = view(poolC, 0, F32, [128, 4, D])
        y_cv = view(poolC, 16384, F32, [128, DC, 512])
        lnx2b = view(poolC, 32768, BF16, [128, DC, 512])
        p1bv = view(poolC, 40960, BF16, [128, 2, 512])
        hb2 = view(poolC, 43008, BF16, [128, 2, 512])
        q2bv = view(poolC, 45056, BF16, [128, 2, 512])
        xstage2_ = [view(poolD, 0, F32, [128, DC, 512]),
                    view(poolA, 0, F32, [128, DC, 512])]
        rawb = view(poolD, 16384, BF16, [128, 512])
        sqb = view(poolD, 17408, BF16, [128, 512])
        rows0 = [view(poolD, 18432 + 2048 * i, F32, [1, 512]) for i in range(4)]
        r_rows = view(poolB, 0, F32, [1, T])
        nm_rows = view(poolB, 16384, F32, [1, T])
        tmp32 = view(poolD, 26624, F32, [128, 512])
        wstage = view(poolD, 0, F32, [128, 2, 1024])
        wxb = view(poolD, 8192, BF16, [128, DC, 1024])
        eb = view(poolD, 0, BF16, [128, 6, 512])
        gb = view(poolD, 6144, BF16, [128, 6, 512])
        ab = view(poolD, 12288, BF16, [128, 6, 512])
        zb = view(poolD, 18432, BF16, [128, 3, 2, 128])
        zmb = view(poolD, 12288, BF16, [128, 3, 2, 128])
        abh = view(poolD, 13824, BF16, [128, 2, 128])
        K_ILV = False
        kstage4 = view(poolD, 20480, F32, [128, 8, 128])
        wkvh = [view(poolD, 14336, BF16, [128, 8, 128]),
                view(poolD, 16384, BF16, [128, 8, 128])]
        kevac2 = [view(poolD, 24576, BF16, [128, 512]),
                  view(poolD, 25600, BF16, [128, 512])]
        ab2 = [view(poolD, 14336, BF16, [128, 2, 512]),
               view(poolD, 16384, BF16, [128, 2, 512])]
        ktb2_ = [view(poolD, 24576, BF16, [128, 1024]),
                 view(poolD, 26624, BF16, [128, 1024])]
        wkvb = view(poolD, 0, BF16, [128, DC, 1024])
        wstage4 = view(poolD, 16384, F32, [128, 512])
        raw4 = view(poolD, 18432, BF16, [128, 512])
        sq4 = view(poolD, 19456, BF16, [128, 512])
        rows4 = [view(poolD, 20480 + 2048 * i, F32, [1, 512]) for i in range(4)]
        wprojb = view(poolD, 0, BF16, [128, NP, 1024])
        wstage6 = view(poolD, 16384, F32, [128, 1024])
        l2_rstd = view(poolD, 20480, F32, [1, 2048])
        l2_nm = view(poolD, 8192, F32, [1, 2048])
        row6_mean = view(poolD, 2048, F32, [1, 512])
        row6_var = view(poolD, 4096, F32, [1, 512])
        raw6 = view(poolD, 0, BF16, [128, 512])
        sq6 = view(poolD, 1024, BF16, [128, 512])
        tmp6 = view(poolD, 0, F32, [128, 512])
        u1b = view(poolC, 55296, BF16, [128, DC, RANK])
        v2b = view(poolC, 59392, BF16, [128, 2, D])

        msum_sb = msum_t.ap()
        prefsum_sb = prefsum_t.ap()
        ps = psT.ap()
        psbf = psT.bitcast(BF16).ap()

        def bank(i, lo=0, hi=512):
            return ps[:, i * 512 + lo: i * 512 + hi]
        ps8 = ps.rearrange("p (k c) -> p k c", k=8)
        tp_ps = psbf[:, 0:1024]

        for name in dumps:
            dt, _ = dump_specs[name]
            shp = {"lnxb": DC * T, "kf": NP * 2048, "vt": 16 * 1024, "qf": NP * OWNT,
                   "msnap": 8 * 512, "prefsnap": 17 * 8, "avpart": NP * OWNT,
                   "resid": DC * OWNT, "wxb": DC * 1024}[name]
            dump_t[name] = nc.dram_tensor("dump_" + name, [128, shp], dt,
                                          kind="ExternalOutput").ap()
        dump_src = {}

        d_misc_keys = []
        for dst, src in ((identb[:, :], identb_d[:, :]), (identf[:, :], identf_d[:, :]),
                         (onescol[:, :], onescol_d[:, :]), (onesrow[:, :], onesrow_d[:, :]),
                         (masksb[:, :, :, :], masks_d[:, :, :, :]),
                         (b1s[:, :], b1d[:, :]), (b2s[:, :], b2d[:, :])):
            d_misc_keys.append(b.dma(dst, src))

        def wait_misc(eng):
            for k in d_misc_keys:
                b.wait(eng, k, 16)
        b.op("vector", lambda e: e.memset(consts[:, 0:1], EPS))
        b.op("vector", lambda e: e.memset(consts[:, 1:2], 1.0))
        cV = b.op("vector", lambda e: e.memset(consts[:, 2:3], 1.0), inc="V")
        onescol_f = consts[:, 2:3]
        eps_ap = consts[:, 0:1]

        def emit_ln_stats(src_f32, rows, co, raw, sq, src_ready_d):
            rm = rows[0][0:1, co:co + 512]
            rv = rows[1][0:1, co:co + 512]
            rr = rows[2][0:1, co:co + 512]
            rn = rows[3][0:1, co:co + 512]
            b.wait("vector", "d", src_ready_d)
            b.wait("vector", "V", cV)
            b.wait("vector", "T", b.cnt["T"])
            raw2 = [raw, view(poolD, 6144, BF16, [128, 512])]
            sq2 = [sq, view(poolD, 7168, BF16, [128, 512])]
            t_done = {}
            for dc in range(DC):
                if dc >= 2:
                    b.wait("vector", "T", t_done[dc - 2])
                b.op("vector", lambda e, a=raw2[dc % 2], s=src_f32[:, dc, :]:
                     e.tensor_copy(a, s))
                b.op("vector", lambda e, a=sq2[dc % 2], r=raw2[dc % 2]:
                     e.tensor_mul(a, r, r), inc="V", ss=True)
                b.wait("tensor", "V", b.cnt["V"])
                b.op("tensor", lambda e, o=bank(0)[0:1, :], l=onescol[:, :], r=raw2[dc % 2],
                     st=(dc == 0), sp=(dc == DC - 1): e.matmul(o, l, r, skip_group_check=True, start=st, stop=sp))
                b.op("tensor", lambda e, o=bank(1)[0:1, :], l=onescol[:, :], r=sq2[dc % 2],
                     st=(dc == 0), sp=(dc == DC - 1): e.matmul(o, l, r, skip_group_check=True, start=st, stop=sp),
                     inc="T")
                t_done[dc] = b.cnt["T"]
            b.wait("scalar", "T", tfence())
            b.op("scalar", lambda e, o=rm, i=bank(0)[0:1, :]: e.mul(o, i, 1.0 / D))
            b.op("scalar", lambda e, o=rv, i=bank(1)[0:1, :]: e.mul(o, i, 1.0 / D),
                 inc="S")
            b.wait("vector", "S", b.cnt["S"])
            b.op("vector", lambda e, o=rn, m=rm: e.tensor_mul(o, m, m))
            b.op("vector", lambda e, o=rv, v=rv,
                 m=rn: e.tensor_sub(o, v, m), inc="V", ss=True)
            b.wait("scalar", "V", b.cnt["V"])
            b.op("scalar", lambda e, o=rr, i=rv:
                 e.activation(o, i, AF.Ln, bias=eps_ap[0:1, :]))
            b.op("scalar", lambda e, o=rr, i=rr:
                 e.activation(o, i, AF.Exp, scale=-0.5), inc="S", ss=True)
            b.wait("vector", "S", b.cnt["S"])
            b.op("vector", lambda e, o=rn, m=rm,
                 r=rr: e.scalar_tensor_tensor(o, m, -1.0, r, ALU.mult, ALU.mult),
                 inc="V")

        def emit_ln_normalize(src_f32, dst_slices, rows, co, tmpn):
            rr = rows[2][0:1, co:co + 512]
            rn = rows[3][0:1, co:co + 512]
            b.wait("tensor", "V", b.cnt["V"])
            b.op("tensor", lambda e, o=bank(2), l=onesrow[0:1, :], r=rr:
                 e.matmul(o, l, r, skip_group_check=True, start=True, stop=True))
            b.op("tensor", lambda e, o=bank(3), l=onesrow[0:1, :], r=rn:
                 e.matmul(o, l, r, skip_group_check=True, start=True, stop=True), inc="T")
            b.wait("vector", "T", tfence())
            for dc in range(DC):
                b.op("vector", lambda e, o=tmpn, s=src_f32[:, dc, :], r=bank(2):
                     e.tensor_mul(o, s, r), ss=True)
                b.op("vector", lambda e, o=dst_slices[dc], t=tmpn, n=bank(3):
                     e.tensor_add(o, t, n), inc="V", ss=True)
            return b.cnt["V"]

        def emit_ln_stats_mixed(src_f32, rows, co_rs, raw, sq):
            rm = rows[0][0:1, 0:512]
            rv = rows[1][0:1, 0:512]
            rr = rows[2][0:1, co_rs:co_rs + 512]
            rn = rows[3][0:1, co_rs:co_rs + 512]
            b.wait("vector", "V", cV)
            t_entry = b.cnt["T"]
            raw2_ = [raw, view(poolD, 6144, BF16, [128, 512])]
            sq2_ = [sq, view(poolD, 7168, BF16, [128, 512])]
            t_done6 = {}
            c6 = {}

            def p6cpy(dc):
                b.wait("vector", "T", t_done6[dc - 2] if dc >= 2 else t_entry)
                c6[dc] = b.op("vector", lambda e, a=raw2_[dc % 2], s=src_f32[:, dc, :]:
                              e.tensor_copy(a, s), inc="V")
            p6cpy(0)
            for dc in range(DC):
                if dc + 1 < DC:
                    p6cpy(dc + 1)
                b.wait("vector", "V", c6[dc])
                sq_v = b.op("vector", lambda e, a=sq2_[dc % 2], r=raw2_[dc % 2]:
                            e.tensor_mul(a, r, r), inc="V")
                b.wait("tensor", "V", sq_v)
                b.op("tensor", lambda e, o=bank(0)[0:1, :], l=onescol[:, :], r=raw2_[dc % 2],
                     st=(dc == 0), sp=(dc == DC - 1): e.matmul(o, l, r, skip_group_check=True, start=st, stop=sp))
                b.op("tensor", lambda e, o=bank(1)[0:1, :], l=onescol[:, :], r=sq2_[dc % 2],
                     st=(dc == 0), sp=(dc == DC - 1): e.matmul(o, l, r, skip_group_check=True, start=st, stop=sp),
                     inc="T")
                t_done6[dc] = b.cnt["T"]
            b.wait("scalar", "T", tfence())
            b.op("scalar", lambda e, o=rm, i=bank(0)[0:1, :]: e.mul(o, i, 1.0 / D))
            b.op("scalar", lambda e, o=rv, i=bank(1)[0:1, :]: e.mul(o, i, 1.0 / D), inc="S")
            b.wait("vector", "S", b.cnt["S"])
            b.op("vector", lambda e, o=rn, m=rm: e.tensor_mul(o, m, m))
            b.op("vector", lambda e, o=rv, v=rv, m=rn: e.tensor_sub(o, v, m), inc="V",
                 ss=True)
            b.wait("scalar", "V", b.cnt["V"])
            b.op("scalar", lambda e, o=rr, i=rv:
                 e.activation(o, i, AF.Ln, bias=eps_ap[0:1, :]))
            b.op("scalar", lambda e, o=rr, i=rr:
                 e.activation(o, i, AF.Exp, scale=-0.5), inc="S", ss=True)
            b.wait("vector", "S", b.cnt["S"])
            b.op("vector", lambda e, o=rn, m=rm, r=rr:
                 e.scalar_tensor_tensor(o, m, -1.0, r, ALU.mult, ALU.mult), inc="V")

        x_src = x_t.rearrange("(dc p) t -> p dc t", p=128)
        wait_misc("tensor")
        wait_misc("vector")
        wait_misc("scalar")
        wait_misc("gpsimd")
        relVs = [0, 0]
        relTs = [0, 0]
        for tb in range(8):
            xs = xstage2_[tb % 2]
            b.wait("sync", "V", relVs[tb % 2])
            b.wait("sync", "T", relTs[tb % 2])
            dd = b.dma(xs[:, :, :], x_src[:, :, tb * 512:(tb + 1) * 512])
            co = tb * 512
            rm = rows0[0][0:1, 0:512]
            rv = rows0[1][0:1, 0:512]
            rr = r_rows[0:1, co:co + 512]
            rn = nm_rows[0:1, co:co + 512]
            b.wait("vector", dd, 16)
            b.wait("vector", "V", cV)
            b.wait("vector", "T", b.cnt["T"])
            rawx = [rawb, view(poolD, 26624, BF16, [128, 512]),
                    view(poolD, 22528, BF16, [128, 512]),
                    view(poolD, 24576, BF16, [128, 512])]
            sqx = [sqb, view(poolD, 27648, BF16, [128, 512]),
                   view(poolD, 23552, BF16, [128, 512]),
                   view(poolD, 25600, BF16, [128, 512])]
            t_done = {}
            ccnt = {}

            def p1cpy(dc):
                if dc >= 4:
                    b.wait("vector", "T", t_done[dc - 4])
                ccnt[dc] = b.op("vector", lambda e, a=rawx[dc % 4], x_=xs[:, dc, :]:
                                e.tensor_copy(a, x_), inc="V")
            p1cpy(0)
            p1cpy(1)
            for dc in range(DC):
                if dc + 2 < DC:
                    p1cpy(dc + 2)
                b.wait("vector", "V", ccnt[dc])
                sq_v = b.op("vector", lambda e, a=sqx[dc % 4], r=rawx[dc % 4]:
                            e.tensor_mul(a, r, r), inc="V")
                b.wait("tensor", "V", sq_v)
                b.op("tensor", lambda e, o=bank(0)[0:1, :], l=onescol[:, :], r=rawx[dc % 4],
                     st=(dc == 0), sp=(dc == DC - 1): e.matmul(o, l, r, skip_group_check=True, start=st, stop=sp))
                b.op("tensor", lambda e, o=bank(1)[0:1, :], l=onescol[:, :], r=sqx[dc % 4],
                     st=(dc == 0), sp=(dc == DC - 1): e.matmul(o, l, r, skip_group_check=True, start=st, stop=sp),
                     inc="T")
                t_done[dc] = b.cnt["T"]
            relVs[tb % 2] = b.cnt["V"]
            relTs[tb % 2] = b.cnt["T"]
            b.wait("scalar", "T", tfence())
            b.op("scalar", lambda e, o=rm, i=bank(0)[0:1, :]: e.mul(o, i, 1.0 / D))
            b.op("scalar", lambda e, o=rv, i=bank(1)[0:1, :]: e.mul(o, i, 1.0 / D), inc="S")
            b.wait("vector", "S", b.cnt["S"])
            b.op("vector", lambda e, o=rn, m=rm: e.tensor_mul(o, m, m))
            b.op("vector", lambda e, o=rv, v=rv, m=rn: e.tensor_sub(o, v, m), inc="V",
                 ss=True)
            b.wait("scalar", "V", b.cnt["V"])
            b.op("scalar", lambda e, o=rr, i=rv:
                 e.activation(o, i, AF.Ln, bias=eps_ap[0:1, :]))
            b.op("scalar", lambda e, o=rr, i=rr:
                 e.activation(o, i, AF.Exp, scale=-0.5), inc="S", ss=True)
            b.wait("vector", "S", b.cnt["S"])
            b.op("vector", lambda e, o=rn, m=rm, r=rr:
                 e.scalar_tensor_tensor(o, m, -1.0, r, ALU.mult, ALU.mult), inc="V")
        relVs = [b.cnt["V"], b.cnt["V"]]
        bc_free = 0
        p2_prev = {}
        for tb in range(8):
            xs = xstage2_[tb % 2]
            b.wait("sync", "V", relVs[tb % 2])
            dd = b.dma(xs[:, :, :], x_src[:, :, tb * 512:(tb + 1) * 512])
            co = tb * 512
            bka, bkb = (2, 3) if tb % 2 == 0 else (4, 5)
            b.wait("tensor", "V", bc_free)
            b.op("tensor", lambda e, o=bank(bka), l=onesrow[0:1, :], r=r_rows[0:1, co:co + 512]:
                 e.matmul(o, l, r, skip_group_check=True, start=True, stop=True))
            b.op("tensor", lambda e, o=bank(bkb), l=onesrow[0:1, :], r=nm_rows[0:1, co:co + 512]:
                 e.matmul(o, l, r, skip_group_check=True, start=True, stop=True), inc="T")
            tb_T = tfence()
            b.wait("vector", "T", tb_T)
            b.wait("vector", dd, 16)
            tmp2 = [tmp32, view(poolD, 16384, F32, [128, 512]),
                    view(poolD, 22528, F32, [128, 512]),
                    view(poolD, 24576, F32, [128, 512])]
            mcnt2 = {}
            acnt2 = {}

            def p2mul(dc):
                b.wait("vector", "V",
                       acnt2[dc - 4] if dc >= 4 else p2_prev.get(dc + 4, 0))
                mcnt2[dc] = b.op("vector", lambda e, o=tmp2[dc % 4], s=xs[:, dc, :],
                                 r=bank(bka): e.tensor_mul(o, s, r), inc="V")
            p2mul(0)
            p2mul(1)
            for dc in range(DC):
                if dc + 2 < DC:
                    p2mul(dc + 2)
                b.wait("vector", "V", mcnt2[dc])
                acnt2[dc] = b.op("vector", lambda e,
                                 o=lnxh[tb // 4][:, dc, (tb % 4) * 512:(tb % 4 + 1) * 512],
                                 t=tmp2[dc % 4], n=bank(bkb): e.tensor_add(o, t, n),
                                 inc="V")
            p2_prev = acnt2
            relVs[tb % 2] = b.cnt["V"]
            if tb % 2 == 1:
                bc_free = b.cnt["V"]
        p0_done = b.barrier()

        s_rel = [0, 0, 0]
        z_rel = [0, 0, 0]
        e_rel = [0, 0, 0]
        e_relP = [0, 0, 0]
        ab_rel = [0, 0, 0]
        zw_rel = [0, 0, 0]
        g_rel = [0, 0, 0]
        SBK = (0, 2, 6)
        av_free = {4: 0, 5: 0}
        av_freeS = {4: 0, 5: 0}
        sweep_state = {"tpV0": 0, "tpV1": 0, "snapV": 0, "mprefT": 0}

        def emit_sweep_kt(ktg, half, vt_ready_V=0):
            if True:
                ktloc = ktg - 16 * half
                b.wait("tensor", "V", vt_ready_V)
                par2 = ktloc % 2
                tpx = psbf[:, par2 * 1024:(par2 + 1) * 1024]
                ktb = ktb2_[par2]
                b.wait("tensor", "V", sweep_state["tpV" + str(par2)])
                b.wait("tensor", "V", sweep_state["snapV"])
                if K_ILV and half == 1:
                    b.wait("tensor", kfload[ktloc // 4], 16)
                for p in range(NP):
                    b.op("tensor", lambda e, o=tpx[:, p * 128:(p + 1) * 128],
                         i=kf[:, p, ktloc * 128:(ktloc + 1) * 128]:
                         e.transpose(o, i, identb[:, :]), inc="T")
                b.wait("vector", "T", tfence())
                sweep_state["tpV" + str(par2)] = b.op(
                    "vector", lambda e, k=ktb, t=tpx: e.tensor_scalar_mul(k[:, :], t, -0.01),
                    inc="V")
                b.wait("tensor", "V", sweep_state["tpV" + str(par2)])
                for p in range(NP):
                    for par in range(2):
                        b.op("tensor", lambda e,
                             o=ps[par * 64:(par + 1) * 64, 6 * 512 + p * 64: 6 * 512 + (p + 1) * 64],
                             l=ktb[:, p * 128 + par * 64: p * 128 + (par + 1) * 64],
                             r=vt[:, ktloc, (2 * p + par) * 64:(2 * p + par + 1) * 64],
                             st=(ktg % 16 == 0 and p == 0), tp=(0, par * 64):
                             e.matmul(o, l, r, skip_group_check=True, start=st, stop=True, tile_position=tp), inc="T")
                for dc in range(DC):
                    b.op("tensor", lambda e, o=bank(7, 2 * dc, 2 * dc + 1),
                         l=vt[:, ktloc, dc * 128:(dc + 1) * 128], r=onescol[:, :],
                         st=(ktg % 16 == 0 and dc == 0): e.matmul(o, l, r, skip_group_check=True, start=st, stop=True),
                         inc="T")
                sweep_state["mprefT"] = b.cnt["T"]
                if ktg % 2 == 1:
                    j = (ktg + 3) // 2
                    pref_src = bank(7, 0, 16).rearrange("p (a b) -> p a b", b=2)[:, :, 0]
                    if 2 <= j <= 16:
                        b.wait("vector", "T", tfence())
                        if half == 0:
                            b.op("vector", lambda e, o=msnap[:, j % 8, :], s=bank(6):
                                 e.tensor_copy(o, s))
                            sweep_state["snapV"] = b.op(
                                "vector", lambda e, o=prefsnap[:, j, :], s=pref_src:
                                e.tensor_copy(o, s), inc="V")
                        else:
                            b.op("vector", lambda e, o=msnap[:, j % 8, :], s=bank(6),
                                 m=msum_sb[:, :]: e.tensor_add(o, s, m))
                            sweep_state["snapV"] = b.op(
                                "vector", lambda e, o=prefsnap[:, j, :], s=pref_src,
                                m=prefsum_sb[:, :]: e.tensor_add(o, s, m), inc="V")
                    if ktg == 15:
                        b.wait("vector", "T", tfence())
                        b.op("vector", lambda e: e.tensor_copy(msum_sb[:, :], bank(6)))
                        sweep_state["snapV"] = b.op(
                            "vector", lambda e, o=prefsum_sb[:, :], s=pref_src:
                            e.tensor_copy(o, s), inc="V")

        b.wait("sync", "V", p0_done["V"])
        b.wait("vector", "T", p0_done["T"])
        b.wait("vector", "S", p0_done["S"])

        w_src2 = W_attn.rearrange("(a p) c -> p a c", p=128)

        def stream_w(colbase, dst):
            b.wait("vector", "T", b.cnt["T"])
            casts = []
            for dp in range(4):
                b.wait("sync", "V", casts[-1] if casts else b.cnt["V"])
                dd = b.dma(wstage[:, :, :],
                           w_src2[:, 2 * dp:2 * dp + 2, colbase:colbase + 1024])
                b.wait("vector", dd, 16)
                for k in range(2):
                    casts.append(b.op("vector", lambda e, o=dst[:, 2 * dp + k, :],
                                      s=wstage[:, k, :]: e.tensor_copy(o, s), inc="V"))
            return b.cnt["V"]

        evacV = {4: 0, 5: 0}

        def proj_loop(tiles, emit_mms, emit_evac):
            for i, tile in enumerate(tiles):
                bk = 4 + i % 2
                b.wait("tensor", "V", evacV[bk])
                emit_mms(bk, tile)
                b.wait("vector", "T", tfence())
                evacV[bk] = emit_evac(bk, tile)

        wV = stream_w(D, wxb)
        b.wait("tensor", "V", wV)

        def k_mms(bk, tile):
            hc, tb2 = tile
            for dc in range(DC):
                b.op("tensor", lambda e, o=bank(bk), l=wxb[:, dc, hc * 128:(hc + 1) * 128],
                     r=lnxh[0][:, dc, tb2 * 512:(tb2 + 1) * 512], st=(dc == 0),
                     sp=(dc == DC - 1): e.matmul(o, l, r, skip_group_check=True, start=st, stop=sp), inc="T")

        def k_evac(bk, tile):
            hc, tb2 = tile
            return b.op("vector", lambda e, o=kf[:, hc, tb2 * 512:(tb2 + 1) * 512],
                        s=bank(bk): e.tensor_copy(o, s), inc="V")
        proj_loop([(hc, tb2) for hc in range(NP) for tb2 in range(4)], k_mms, k_evac)

        if upto == "k":
            dump_src.update(kf=kf, wxb=wxb)
            return finish(nc, b, stack, dump_t, dump_src, b.barrier())

        wV = stream_w(2 * D, wxb)
        b.wait("tensor", "V", wV)

        def v_mms(bk, tile):
            tt, vc2 = tile
            for dc in range(DC):
                b.op("tensor", lambda e, o=bank(bk), l=lnxh[0][:, dc, tt * 128:(tt + 1) * 128],
                     r=wxb[:, dc, vc2 * 512:(vc2 + 1) * 512], st=(dc == 0),
                     sp=(dc == DC - 1): e.matmul(o, l, r, skip_group_check=True, start=st, stop=sp), inc="T")

        def v_evac(bk, tile):
            tt, vc2 = tile
            return b.op("vector", lambda e, o=vt[:, tt, vc2 * 512:(vc2 + 1) * 512],
                        s=bank(bk): e.tensor_scalar_mul(o, s, VSCALE), inc="V")
        proj_loop([(tt, vc2) for tt in range(16) for vc2 in range(2)], v_mms, v_evac)

        if upto == "kv":
            dump_src.update(kf=kf, vt=vt)
            return finish(nc, b, stack, dump_t, dump_src, b.barrier())

        wV = stream_w(0, wxb)
        b.wait("tensor", "V", wV)
        lnx_gh = [h.rearrange("p dc (g c) -> p dc g c", c=128) for h in lnxh]

        def q_mms(bk, tile):
            hc, qb4 = tile
            for dc in range(DC):
                rhs = lnx_gh[qb4 // 2][:, dc, 0::2, :][:, (qb4 % 2) * 4:(qb4 % 2 + 1) * 4, :]
                b.op("tensor", lambda e, o=bank(bk), l=wxb[:, dc, hc * 128:(hc + 1) * 128],
                     r=rhs, st=(dc == 0), sp=(dc == DC - 1):
                     e.matmul(o, l, r, skip_group_check=True, start=st, stop=sp), inc="T")

        def q_evac(bk, tile):
            hc, qb4 = tile
            return b.op("vector", lambda e, o=qf[:, hc, qb4 * 512:(qb4 + 1) * 512],
                        s=bank(bk): e.tensor_scalar_mul(o, s, SCALE_QK), inc="V")

        q_tiles = [(hc, qb4) for hc in range(NP) for qb4 in range(4)]
        for i, tile in enumerate(q_tiles):
            bk = 4 + i % 2
            b.wait("tensor", "V", evacV[bk])
            q_mms(bk, tile)
            qT = tfence()
            if i < 16:
                emit_sweep_kt(i, 0)
            b.wait("vector", "T", qT)
            evacV[bk] = q_evac(bk, tile)
        p1_done = b.barrier()

        dump_src.update(lnxb=lnxh[0], kf=kf, vt=vt, qf=qf)
        if upto == "p1":
            return finish(nc, b, stack, dump_t, dump_src, b.barrier())

        def kts_of(c, half):
            lo, hi = 16 * half, 16 * half + 16
            return list(range(lo, min(8 * c + 8, hi)))

        kblk = [0]
        kcast = {}
        kwT = [0, 0]
        ke_dma = {0: 0, 1: 0}
        kfload = []

        def emit_attention(half):
            b.wait("tensor", "V", b.cnt["V"])
            if K_ILV and half == 0:
                b.wait("sync", "T", p1_done["T"])
            if K_ILV and half == 1:
                for k in kfload:
                    b.wait("tensor", k, 16)
            for c in range(4):
                kts = kts_of(c, half)
                if not kts:
                    continue
                for p in range(NP):
                    avbk = 4 + p % 2
                    av_done = 0
                    nk = len(kts)
                    hc, tb2 = kblk[0] // 4, kblk[0] % 4
                    idle_bk = 4 + (1 - p % 2)
                    if K_ILV and half == 0 and tb2 == 0:
                        b.wait("sync", "V", kcast[hc - 1] if hc > 0 else p1_done["V"])
                        kdd = b.dma(kstage4[:, :, :],
                                    w_src2[:, :, D + hc * 128: D + (hc + 1) * 128],
                                    slot="wk")
                        b.wait("vector", kdd, 16)
                        b.wait("vector", "T", kwT[hc % 2])
                        kcast[hc] = b.op("vector", lambda e, o=wkvh[hc % 2][:, :, :],
                                         s=kstage4[:, :, :]: e.tensor_copy(o, s),
                                         inc="V")

                    def emit_qk(idx):
                        ktg = kts[idx]
                        ktloc = ktg - 16 * half
                        sb = idx % 3
                        jm = ktg // 2 + 1
                        j0 = max(4 * c + 1, jm)
                        coloff = (j0 - (4 * c + 1)) * 128
                        N = 512 - coloff
                        qcols = qf[:, p, c * 512 + coloff: (c + 1) * 512]
                        b.wait("tensor", "S", s_rel[sb])
                        b.wait("tensor", "V", z_rel[sb])
                        for hh in range(2):
                            b.op("tensor", lambda e,
                                 o=bank(SBK[sb] + hh, 0, N),
                                 l=kf[hh * 64:(hh + 1) * 64, p, ktloc * 128:(ktloc + 1) * 128],
                                 r=qcols[hh * 64:(hh + 1) * 64, :], tp=(hh * 64, 0):
                                 e.matmul(o, l, r, skip_group_check=True, start=True, stop=True, tile_position=tp),
                                 inc="T")
                        return b.cnt["T"]

                    def is_m(idx):
                        return 8 * c <= kts[idx] <= 8 * c + 7

                    gcnt, mcnt, pcnt = {}, {}, {}

                    def emit_eg(idx, qkT):
                        ktg = kts[idx]
                        sb = idx % 3
                        masked = is_m(idx)
                        j0 = max(4 * c + 1, ktg // 2 + 1)
                        coloff = (j0 - (4 * c + 1)) * 128
                        N = 512 - coloff
                        b.wait("scalar", "T", qkT)
                        b.wait("scalar", "V", e_rel[sb])
                        b.wait("scalar", "P", e_relP[sb])
                        b.op("scalar", lambda e, o=eb[:, 2 * sb:2 * sb + 2, 0:N],
                             i=ps8[:, SBK[sb]:SBK[sb] + 2, 0:N]:
                             e.activation(o, i, AF.Relu), inc="S")
                        if masked:
                            b.wait("scalar", "V", zw_rel[sb])
                            b.op("scalar", lambda e, o=zb[:, sb, :, :],
                                 i=ps8[:, SBK[sb]:SBK[sb] + 2, 0:128]:
                                 e.activation(o, i, AF.Identity, scale=-0.01,
                                              bias=consts[:, 1:2]), inc="S")
                        s_rel[sb] = b.cnt["S"]
                        b.wait("vector", "S", s_rel[sb])
                        if idx >= 2:
                            b.wait("vector", "V", gcnt[idx - 2])
                        acnt = b.op("vector",
                                    lambda e, o=ab2[idx % 2][:, :, 0:N],
                                    i=eb[:, 2 * sb:2 * sb + 2, 0:N]:
                                    e.tensor_scalar_add(o, i, 0.01), inc="V")
                        b.wait("vector", "T", g_rel[sb])
                        b.wait("vector", "V", acnt)
                        gcnt[idx] = b.op("vector",
                                         lambda e, o=gb[:, 2 * sb:2 * sb + 2, 0:N],
                                         a=ab2[idx % 2][:, :, 0:N],
                                         i=eb[:, 2 * sb:2 * sb + 2, 0:N]:
                                         e.tensor_mul(o, a, i), inc="V")
                        pcnt[idx] = 0
                        if masked:
                            midx = ktg % 2
                            zw_rel[sb] = b.op("vector", lambda e, o=zmb[:, sb, :, :],
                                              z=zb[:, sb, :, :], m=masksb[:, midx, :, :]:
                                              e.tensor_mul(o, z, m), inc="V")
                            b.wait("vector", "V", gcnt[idx])
                            mcnt[idx] = b.op("vector",
                                             lambda e, o=gb[:, 2 * sb:2 * sb + 2, 0:128],
                                             m=masksb[:, midx, :, :]:
                                             e.tensor_mul(o, o, m), inc="V")
                        e_rel[sb] = b.cnt["V"]

                    def emit_av(idx):
                        ktg = kts[idx]
                        ktloc = ktg - 16 * half
                        sb = idx % 3
                        jm = ktg // 2 + 1
                        j0 = max(4 * c + 1, jm)
                        coloff = (j0 - (4 * c + 1)) * 128
                        N = 512 - coloff
                        b.wait("tensor", "V", mcnt[idx] if is_m(idx) else gcnt[idx])
                        b.wait("tensor", "P", pcnt[idx])
                        if idx == 0:
                            b.wait("tensor", "V", av_free[avbk])
                            b.wait("tensor", "S", av_freeS[avbk])
                        for hh in range(2):
                            b.op("tensor", lambda e, o=ps[hh * 64:(hh + 1) * 64,
                                 avbk * 512 + coloff: (avbk + 1) * 512],
                                 l=vt[:, ktloc, (2 * p + hh) * 64:(2 * p + hh + 1) * 64],
                                 r=gb[:, 2 * sb + hh, 0:N], st=(idx == 0),
                                 tp=(0, hh * 64):
                                 e.matmul(o, l, r, skip_group_check=True, start=st, stop=True, tile_position=tp),
                                 inc="T")
                        if is_m(idx):
                            for hh in range(2):
                                b.op("tensor", lambda e, o=ps[hh * 64:(hh + 1) * 64,
                                     avbk * 512 + coloff: avbk * 512 + coloff + 128],
                                     l=vt[:, ktloc, (2 * p + hh) * 64:(2 * p + hh + 1) * 64],
                                     r=zmb[:, sb, hh, :], tp=(0, hh * 64):
                                     e.matmul(o, l, r, skip_group_check=True, start=False,
                                              stop=True, tile_position=tp), inc="T")
                        g_rel[sb] = b.cnt["T"]
                        if idx == 0:
                            for i4 in range(4):
                                j = 4 * c + 1 + i4
                                hj = 0 if j <= 9 else 1
                                if hj != half or j < 2:
                                    continue
                                for par in range(2):
                                    b.op("tensor", lambda e,
                                         o=ps[par * 64:(par + 1) * 64,
                                              avbk * 512 + i4 * 128: avbk * 512 + (i4 + 1) * 128],
                                         l=msnap[par * 64:(par + 1) * 64, j % 8,
                                                 p * 64:(p + 1) * 64],
                                         r=qf[par * 64:(par + 1) * 64, p,
                                              (c * 4 + i4) * 128:(c * 4 + i4 + 1) * 128],
                                         tp=(par * 64, par * 64):
                                         e.matmul(o, l, r, skip_group_check=True, start=False, stop=True,
                                                  tile_position=tp), inc="T")
                            g_rel[sb] = b.cnt["T"]

                    fences = {}
                    fences[0] = emit_qk(0)
                    if nk > 1:
                        fences[1] = emit_qk(1)
                    if nk > 2:
                        fences[2] = emit_qk(2)
                    for idx in range(nk):
                        emit_eg(idx, fences[idx])
                        if idx + 3 < nk:
                            fences[idx + 3] = emit_qk(idx + 3)
                        emit_av(idx)
                        if K_ILV and half == 0 and idx < 8:
                            b.wait("tensor", "V", kcast[hc])
                            if idx == 0:
                                b.wait("tensor", "V", av_free[idle_bk])
                                b.wait("tensor", "S", av_freeS[idle_bk])
                            b.op("tensor", lambda e, o=bank(idle_bk),
                                 l=wkvh[hc % 2][:, idx, :],
                                 r=lnxh[1][:, idx, tb2 * 512:(tb2 + 1) * 512],
                                 st=(idx == 0), sp=(idx == 7):
                                 e.matmul(o, l, r, skip_group_check=True, start=st,
                                          stop=sp), inc="T")
                    av_done = b.cnt["T"]
                    b.wait("vector", "T", tfence())
                    if K_ILV and half == 0:
                        if ke_dma[kblk[0] % 2]:
                            b.wait("vector", ke_dma[kblk[0] % 2], 16)
                        kv = b.op("vector", lambda e, o=kevac2[kblk[0] % 2][:, :],
                                  s=bank(idle_bk): e.tensor_copy(o, s), inc="V")
                        av_free[idle_bk] = kv
                        b.wait("sync", "V", kv)
                        ke_dma[kblk[0] % 2] = b.dma(
                            kf2_d[:, hc, tb2 * 512:(tb2 + 1) * 512],
                            kevac2[kblk[0] % 2][:, :], slot=f"ke{kblk[0] % 2}")
                        if tb2 == 3:
                            kwT[hc % 2] = b.cnt["T"]
                        kblk[0] += 1
                    chunk_final = (c <= 2 * half + 1)
                    for i4 in range(4):
                        j = 4 * c + 1 + i4
                        dst = avpart[:, p, (j - 1) * 128: j * 128]
                        src = ps[:, avbk * 512 + i4 * 128: avbk * 512 + (i4 + 1) * 128]
                        if not chunk_final:
                            b.op("vector", lambda e, o=dst, s=src: e.tensor_copy(o, s),
                                 inc="V")
                        elif c <= 1:
                            if j == 1:
                                b.op("vector", lambda e, o=dst, s=src: e.tensor_copy(o, s),
                                     inc="V")
                            else:
                                b.op("vector", lambda e, o=dst, s=src,
                                     pv=prefsnap[:, j, p:p + 1]:
                                     e.tensor_scalar(o, s, pv, None, ALU.add), inc="V")
                        else:
                            b.op("vector", lambda e, o=dst, s=src, pv=prefsnap[:, j, p:p + 1]:
                                 e.scalar_tensor_tensor(o, s, pv, o, ALU.add, ALU.add),
                                 inc="V")
                    av_free[avbk] = b.cnt["V"]
                    av_freeS[avbk] = b.cnt["S"]

        b.wait_all("tensor", p1_done)
        if upto == "sweep":
            dump_src.update(msnap=msnap.ap(), prefsnap=prefsnap.ap())
            return finish(nc, b, stack, dump_t, dump_src, b.barrier())
        emit_attention(0)
        h0_done = b.barrier()

        def p4_sweep(colbase, is_v):
            entry_T = b.cnt["T"]
            casts = []
            wstage4b = view(poolD, 16384, F32, [128, 1024])
            for dc in range(DC):
                b.wait("sync", "V", casts[-1] if casts else b.cnt["V"])
                dd = b.dma(wstage4b[:, :],
                           W_attn[dc * 128:(dc + 1) * 128, colbase: colbase + 1024])
                b.wait("vector", dd, 16)
                b.wait("vector", "T", entry_T)
                casts.append(b.op("vector", lambda e, o=wkvb[:, dc, :],
                                  s=wstage4b[:, :]: e.tensor_copy(o, s), inc="V"))
            wready = b.cnt["V"]
            b.wait("tensor", "V", wready)
            if not is_v:
                for tb in range(4):
                    for hc in range(NP):
                        bk = 4 + hc % 2
                        b.wait("tensor", "V", evacV[bk])
                        for dc in range(DC):
                            b.op("tensor", lambda e, o=bank(bk),
                                 l=wkvb[:, dc, hc * 128:(hc + 1) * 128],
                                 r=lnxh[1][:, dc, tb * 512:(tb + 1) * 512],
                                 st=(dc == 0), sp=(dc == DC - 1):
                                 e.matmul(o, l, r, skip_group_check=True, start=st, stop=sp), inc="T")
                        b.wait("vector", "T", tfence())
                        evacV[bk] = b.op("vector", lambda e,
                                         o=kf[:, hc, tb * 512:(tb + 1) * 512], s=bank(bk):
                                         e.tensor_copy(o, s), inc="V")
            else:
                for tt in range(16):
                    for vc2 in range(2):
                        bk = 4 + (tt * 2 + vc2) % 2
                        b.wait("tensor", "V", evacV[bk])
                        for dc in range(DC):
                            b.op("tensor", lambda e, o=bank(bk),
                                 l=lnxh[1][:, dc, tt * 128:(tt + 1) * 128],
                                 r=wkvb[:, dc, vc2 * 512:(vc2 + 1) * 512],
                                 st=(dc == 0), sp=(dc == DC - 1):
                                 e.matmul(o, l, r, skip_group_check=True, start=st, stop=sp), inc="T")
                        b.wait("vector", "T", tfence())
                        evacV[bk] = b.op("vector", lambda e,
                                         o=vt[:, tt, vc2 * 512:(vc2 + 1) * 512],
                                         s=bank(bk): e.tensor_scalar_mul(o, s, VSCALE),
                                         inc="V")
                    emit_sweep_kt(16 + tt, 1, vt_ready_V=evacV[4 + (tt * 2 + 1) % 2])

        b.wait("sync", "T", h0_done["T"])
        b.wait("sync", "S", h0_done["S"])
        b.wait_all("vector", h0_done, sems=("T", "S"))
        if K_ILV:
            b.wait("sync", ke_dma[0], 16)
            b.wait("sync", ke_dma[1], 16)
            for i in range(4):
                kfload.append(b.dma(kf[:, :, 512 * i:512 * (i + 1)],
                                    kf2_d[:, :, 512 * i:512 * (i + 1)]))
            b.wait("vector", ke_dma[0], 16)
            b.wait("vector", ke_dma[1], 16)
        else:
            p4_sweep(D, is_v=False)
        p4_sweep(2 * D, is_v=True)
        p4_done = b.barrier()

        b.wait_all("tensor", p4_done)
        emit_attention(1)
        attn_done = b.barrier()

        dump_src.update(msnap=msnap.ap(), prefsnap=prefsnap.ap(), avpart=avpart)
        if upto == "attn":
            return finish(nc, b, stack, dump_t, dump_src, b.barrier())

        b.wait("sync", "T", attn_done["T"])
        b.wait("sync", "S", attn_done["S"])
        b.wait("sync", "V", attn_done["V"])
        casts = []
        for p in range(NP):
            if p >= 1:
                b.wait("sync", "V", casts[p - 1])
            dd = b.dma(wstage6[:, :], W_projd[p * 128:(p + 1) * 128, :])
            b.wait("vector", dd, 16)
            b.wait("vector", "T", attn_done["T"])
            casts.append(b.op("vector", lambda e, o=wprojb[:, p, :], s=wstage6[:, :]:
                              e.tensor_copy(o, s), inc="V"))
        wproj_V = b.cnt["V"]
        b.wait("tensor", "V", wproj_V)

        wstage6b = view(poolC, 49152, F32, [128, 1024])

        def stream_ffn(pairs):
            cc = [wproj_V, wproj_V]
            stages = [wstage6, wstage6b]
            for i, (dsl, ssl) in enumerate(pairs):
                n = prod(ssl.shape[1:])
                assert n <= 1024
                stage = stages[i % 2][:, 0:n]
                if len(ssl.shape) == 3:
                    stage = stage.rearrange("p (a c) -> p a c", a=ssl.shape[1])
                b.wait("scalar", "V", cc[-2])
                dd = b.dma2(stage, ssl)
                b.wait("vector", dd, 16)
                cc.append(b.op("vector", lambda e, o=dsl, s=stage: e.tensor_copy(o, s),
                               inc="V"))
            return b.cnt["V"]

        u1_src = U1d.rearrange("(dc p) r -> p dc r", p=128)
        v1_src = V1d.rearrange("(rc p) f -> p rc f", p=128)
        u2_src = U2d.rearrange("(fc p) r -> p fc r", p=128)
        v2_src = V2d.rearrange("(rc p) d -> p rc d", p=128)
        pairs = []
        for i in range(2):
            pairs.append((u1b[:, 4 * i:4 * (i + 1), :], u1_src[:, 4 * i:4 * (i + 1), :]))
        for rc in range(2):
            for sseg in range(4):
                pairs.append((v1b[:, rc, sseg * 1024:(sseg + 1) * 1024],
                              v1_src[:, rc, sseg * 1024:(sseg + 1) * 1024]))
        for i in range(8):
            pairs.append((u2b[:, 4 * i:4 * (i + 1), :], u2_src[:, 4 * i:4 * (i + 1), :]))
        for rc in range(2):
            pairs.append((v2b[:, rc, :], v2_src[:, rc, :]))

        evac2 = {0: 0, 1: 0}
        x_own_src = x_own_t.rearrange("(dc p) t -> p dc t", p=128)
        xdd = {}

        def xo_dma(c, h):
            xdd[(c, h)] = b.dma(xown[:, 4 * h:4 * h + 4, :],
                                x_own_src[:, 4 * h:4 * h + 4, c * 512:(c + 1) * 512])
        xo_dma(0, 0)
        xo_dma(0, 1)
        b.wait("tensor", "S", attn_done["S"])
        for c in range(4):
            for dcol in range(DC):
                bk = dcol % 2
                if dcol % 4 == 0:
                    b.wait("vector", xdd[(c, dcol // 4)], 16)
                b.wait("tensor", "V", evac2[bk])
                b.wait("tensor", "V", attn_done["V"])
                for p in range(NP):
                    b.op("tensor", lambda e, o=bank(bk), l=wprojb[:, p, dcol * 128:(dcol + 1) * 128],
                         r=avpart[:, p, c * 512:(c + 1) * 512], st=(p == 0), sp=(p == NP - 1):
                         e.matmul(o, l, r, skip_group_check=True, start=st, stop=sp), inc="T")
                b.wait("vector", "T", tfence())
                evac2[bk] = b.op("vector", lambda e, o=resid[:, dcol, c * 512:(c + 1) * 512],
                                 x=xown[:, dcol, :], s=bank(bk): e.tensor_add(o, x, s), inc="V")
                if dcol in (3, 7) and c + 1 < 4:
                    b.wait("sync", "V", b.cnt["V"])
                    xo_dma(c + 1, dcol // 4)
        wffn_V = stream_ffn(pairs)
        p6a_done = b.barrier()

        dump_src.update(resid=resid)
        if upto == "p6a":
            return finish(nc, b, stack, dump_t, dump_src, b.barrier())

        rows6 = (row6_mean, row6_var, l2_rstd, l2_nm)
        b.wait("vector", "T", p6a_done["T"])
        for c in range(4):
            src = resid[:, :, c * 512:(c + 1) * 512]
            rows_c = (row6_mean, row6_var,
                      l2_rstd[0:1, c * 512:(c + 1) * 512].rearrange("p c -> p c") if False
                      else l2_rstd, l2_nm)
            emit_ln_stats_mixed(src, rows_c, c * 512, raw6, sq6)
        p6b_done = b.barrier()

        b.wait("tensor", "V", wffn_V)

        ydma = []
        lnx2b_ = [lnx2b, view(poolC, 47104, BF16, [128, DC, 512])]
        ln_done = [0, 0, 0, 0]
        tmp6_ = [tmp6] + [view(poolD, k, F32, [128, 512]) for k in (2048, 4096, 6144)]
        ln2_prev = {}

        def emit_ln2_normalize(c):
            nonlocal ln2_prev
            ccols = slice(c * 512, (c + 1) * 512)
            b.wait("tensor", "S", b.cnt["S"])
            b.wait("tensor", "V", tail_ybufV[0])
            b.op("tensor", lambda e, o=bank(6), l=onesrow[0:1, :],
                 r=l2_rstd[0:1, c * 512:(c + 1) * 512]:
                 e.matmul(o, l, r, skip_group_check=True, start=True, stop=True))
            b.op("tensor", lambda e, o=bank(7), l=onesrow[0:1, :],
                 r=l2_nm[0:1, c * 512:(c + 1) * 512]:
                 e.matmul(o, l, r, skip_group_check=True, start=True, stop=True), inc="T")
            b.wait("vector", "T", tfence())
            m6 = {}
            a6 = {}

            def n6mul(dc):
                b.wait("vector", "V",
                       a6[dc - 4] if dc >= 4 else ln2_prev.get(dc + 4, 0))
                m6[dc] = b.op("vector", lambda e, o=tmp6_[dc % 4], s=resid[:, dc, ccols],
                              r=bank(6): e.tensor_mul(o, s, r), inc="V")
            n6mul(0)
            n6mul(1)
            for dc in range(DC):
                if dc + 2 < DC:
                    n6mul(dc + 2)
                b.wait("vector", "V", m6[dc])
                a6[dc] = b.op("vector", lambda e, o=lnx2b_[c % 2][:, dc, :],
                              t=tmp6_[dc % 4], n=bank(7): e.tensor_add(o, t, n), inc="V")
            ln2_prev = a6
            ln_done[c] = b.cnt["V"]

        b.wait("vector", "V", p6b_done["V"])
        pending_tail = [None]
        prev_ycV = [0]
        tail_ybufV = [0]
        emit_ln2_normalize(0)
        for c in range(4):
            ccols = slice(c * 512, (c + 1) * 512)
            b.wait("tensor", "V", ln_done[c])
            b.wait("tensor", "V", prev_ycV[0])
            for rc2 in range(2):
                for dc in range(DC):
                    b.op("tensor", lambda e, o=bank(rc2), l=u1b[:, dc, rc2 * 128:(rc2 + 1) * 128],
                         r=lnx2b_[c % 2][:, dc, :], st=(dc == 0), sp=(dc == DC - 1):
                         e.matmul(o, l, r, skip_group_check=True, start=st, stop=sp), inc="T")
            b.wait("vector", "T", tfence())
            p1V = b.op("vector", lambda e: e.tensor_copy(p1bv[:, 0, :], bank(0)))
            p1V = b.op("vector", lambda e: e.tensor_copy(p1bv[:, 1, :], bank(1)), inc="V")
            if pending_tail[0] is not None:
                pending_tail[0]()
                pending_tail[0] = None
            b.wait("tensor", "V", p1V)
            gelu_S = [0, 0]
            q2_T = [0, 0]
            hfence = {}

            def emit_h(fc):
                hbk = 2 + fc % 2
                b.wait("tensor", "S", gelu_S[fc % 2])
                for rc2 in range(2):
                    b.op("tensor", lambda e, o=bank(hbk), l=v1b[:, rc2, fc * 128:(fc + 1) * 128],
                         r=p1bv[:, rc2, :], st=(rc2 == 0), sp=(rc2 == 1):
                         e.matmul(o, l, r, skip_group_check=True, start=st, stop=sp), inc="T")
                return tfence()

            hfence[0] = emit_h(0)
            hfence[1] = emit_h(1)
            if c + 1 < 4:
                emit_ln2_normalize(c + 1)
            for fc in range(32):
                b.wait("scalar", "T", hfence[fc])
                b.wait("scalar", "T", q2_T[fc % 2])
                gelu_S[fc % 2] = b.op("scalar", lambda e, o=hb2[:, fc % 2, :],
                                      i=bank(2 + fc % 2), bb=b1s[:, fc:fc + 1]:
                                      e.activation(o, i, AF.Gelu_apprx_tanh, bias=bb), inc="S")
                if fc + 2 < 32:
                    hfence[fc + 2] = emit_h(fc + 2)
                b.wait("tensor", "S", gelu_S[fc % 2])
                for rc2 in range(2):
                    b.op("tensor", lambda e, o=bank(4 + rc2),
                         l=u2b[:, fc, rc2 * 128:(rc2 + 1) * 128],
                         r=hb2[:, fc % 2, :], st=(fc == 0), sp=(fc == 31):
                         e.matmul(o, l, r, skip_group_check=True, start=st, stop=sp), inc="T")
                q2_T[fc % 2] = b.cnt["T"]
            b.wait("vector", "T", tfence())
            q2V = b.op("vector", lambda e: e.tensor_copy(q2bv[:, 0, :], bank(4)))
            q2V = b.op("vector", lambda e: e.tensor_copy(q2bv[:, 1, :], bank(5)), inc="V")
            b.wait("tensor", "V", q2V)
            oev = {0: 0, 1: 0}
            for dcol in range(DC):
                obk = dcol % 2
                b.wait("tensor", "V", oev[obk])
                for rc2 in range(2):
                    b.op("tensor", lambda e, o=bank(obk), l=v2b[:, rc2, dcol * 128:(dcol + 1) * 128],
                         r=q2bv[:, rc2, :], st=(rc2 == 0), sp=(rc2 == 1):
                         e.matmul(o, l, r, skip_group_check=True, start=st, stop=sp), inc="T")
                b.wait("vector", "T", tfence())
                oev[obk] = b.op("vector", lambda e, o=y_cv[:, dcol, :], s=bank(obk),
                                bb=b2s[:, dcol:dcol + 1], rr=resid[:, dcol, ccols]:
                                e.scalar_tensor_tensor(o, s, bb, rr, ALU.add, ALU.add), inc="V")
            ycV = b.cnt["V"]
            prev_ycV[0] = ycV

            def make_tail(c=c, ycV=ycV):
                def tail():
                    tpos_done = ycV
                    if ydma:
                        b.wait("vector", ydma[-1], 16)
                    for tt4 in range(4):
                        b.wait("tensor", "V", tpos_done)
                        for dcol in range(DC):
                            tb_ = 6 + dcol // 4
                            b.op("tensor", lambda e,
                                 o=bank(tb_, (dcol % 4) * 128, (dcol % 4 + 1) * 128),
                                 i=y_cv[:, dcol, tt4 * 128:(tt4 + 1) * 128]:
                                 e.transpose(o, i, identf[:, :]), inc="T")
                        b.wait("vector", "T", tfence())
                        b.op("vector", lambda e, o=ybuf[:, tt4, 0:512], s=bank(6):
                             e.tensor_copy(o, s))
                        tpos_done = b.op("vector", lambda e, o=ybuf[:, tt4, 512:1024],
                                         s=bank(7): e.tensor_copy(o, s), inc="V")
                    tail_ybufV[0] = tpos_done
                    b.wait("sync", "V", tpos_done)
                    ydst = y_out.rearrange("(c tt p) d -> c p tt d", p=128, tt=4)[c]
                    ydma.append(b.dma(ydst, ybuf[:, :, :]))
                return tail
            pending_tail[0] = make_tail()

        pending_tail[0]()
        return finish(nc, b, stack, dump_t, dump_src, b.barrier())


def finish(nc, b, stack, dump_t, dump_src, counts):
    b.wait_all("sync", counts, sems=("T", "S", "V", "P"))
    for name, ap in dump_t.items():
        src = dump_src[name]
        flat = src
        nd = len(flat.shape)
        if nd == 3:
            flat = flat.rearrange("p a b -> p (a b)")
        elif nd == 4:
            flat = flat.rearrange("p a b c -> p (a b c)")
        b.dma(ap, flat)
    for i in range(b.dma_n):
        b.wait("sync", f"dma{i}", 16)
    block = stack.enter_context(nc.Block())
    b.replay(block)
    return nc


def _host_inputs(core, x, W_attn, W_proj, U1, V1, b1, U2, V2, b2):
    seq, cls = core // 2, core % 2
    xt = np.ascontiguousarray(x[seq].T)
    perm = np.arange(32)
    if cls == 1:
        perm = perm.reshape(16, 2)[:, ::-1].reshape(32)
    xt_p = np.ascontiguousarray(
        xt.reshape(D, 32, 128)[:, perm, :].reshape(D, T))
    own_tiles = perm[0::2]
    x_own = np.ascontiguousarray(
        xt.reshape(D, 32, 128)[:, own_tiles, :].reshape(D, OWNT))
    tril = np.tril(np.ones((128, 128), np.float32))
    trilT = tril.T
    m0 = trilT
    m1 = np.zeros((128, 128), np.float32) if cls == 0 else np.ones((128, 128), np.float32)
    masks = np.stack([np.stack([m0, m0]), np.stack([m1, m1])])
    masks = np.ascontiguousarray(masks.transpose(2, 0, 1, 3)).astype(ml_dtypes.bfloat16)
    return {
        "x_t": xt_p, "x_own_t": x_own,
        "W_attn": W_attn, "W_proj": W_proj, "U1": U1, "V1": V1,
        "U2": U2, "V2": V2,
        "b1l": np.ascontiguousarray(b1.reshape(DFF // 128, 128).T),
        "b2l": np.ascontiguousarray(b2.reshape(D // 128, 128).T),
        "masks": masks,
        "identb": np.eye(128, dtype=np.float32).astype(ml_dtypes.bfloat16),
        "identf": np.eye(128, dtype=np.float32),
        "onescol": np.ones((128, 1), np.float32).astype(ml_dtypes.bfloat16),
        "onesrow": np.ones((1, 128), np.float32),
    }, own_tiles


_NC_CACHE = {}


def kernel(L0_attn_x, L0_attn_W_attn, L0_attn_W_proj, L0_ffn_U1, L0_ffn_V1,
           L0_ffn_b1, L0_ffn_U2, L0_ffn_V2, L0_ffn_b2):
    x = np.asarray(L0_attn_x, dtype=np.float32)
    args = tuple(np.ascontiguousarray(np.asarray(a, dtype=np.float32)) for a in
                 (L0_attn_W_attn, L0_attn_W_proj, L0_ffn_U1, L0_ffn_V1,
                  L0_ffn_U2, L0_ffn_V2))
    b1 = np.asarray(L0_ffn_b1, np.float32)
    b2 = np.asarray(L0_ffn_b2, np.float32)
    if "nc" not in _NC_CACHE:
        _NC_CACHE["nc"] = build_program("full")
    nc = _NC_CACHE["nc"]
    in_maps = []
    owns = []
    for core in range(8):
        m, own_tiles = _host_inputs(core, x, args[0], args[1], args[2], args[3],
                                    b1, args[4], args[5], b2)
        in_maps.append(m)
        owns.append(own_tiles)
    res = run_bass_kernel_spmd(nc, in_maps, core_ids=list(range(8)))
    out = np.empty((4, T, D), np.float32)
    for core in range(8):
        seq = core // 2
        yc = res.results[core]["y"]
        rows = (owns[core][:, None] * 128 + np.arange(128)[None, :]).reshape(-1)
        out[seq, rows, :] = yc
    return out


# version 31
# speedup vs baseline: 1.0202x; 1.0202x over previous
import contextlib
import numpy as np
import ml_dtypes
import concourse.bass as bass
import concourse.mybir as mybir
from concourse.bass_utils import run_bass_kernel_spmd

F32 = mybir.dt.float32
BF16 = mybir.dt.bfloat16
U8 = mybir.dt.uint8
AF = mybir.ActivationFunctionType
ALU = mybir.AluOpType

D, T, HD, NP, DC, OWNT = 1024, 4096, 64, 8, 8, 2048
DFF, RANK = 4096, 256
SCALE_QK = 0.125
VSCALE = 1e-3
EPS = 1e-5


def prod(xs):
    r = 1
    for x in xs:
        r *= x
    return r


class Build:
    ENGINES = ("sync", "tensor", "scalar", "vector", "gpsimd")

    def __init__(self, nc):
        self.nc = nc
        self.streams = {e: [] for e in self.ENGINES}
        self.sem_handles = {}
        self.cnt = {"T": 0, "S": 0, "V": 0, "P": 0}
        self.last_wait = {e: {s: 0 for s in self.cnt} for e in self.ENGINES}

    N_DMA_SEMS = 96

    def init_sems(self, stack):
        for s in self.cnt:
            self.sem_handles[s] = stack.enter_context(self.nc.semaphore("sem_" + s))
        self.dma_sems = [stack.enter_context(self.nc.semaphore(f"dsem{i}"))
                         for i in range(self.N_DMA_SEMS)]
        self.dma_n = 0

    ENG_SEM = {"tensor": "T", "scalar": "S", "vector": "V", "gpsimd": "P"}

    def op(self, eng, fn, inc=None, ss=False):
        inc = self.ENG_SEM[eng]
        h = self.sem_handles[inc]
        selfw = 0
        if ss:
            v = self.cnt[inc]
            if v > 0 and self.last_wait[eng][inc] < v:
                selfw = v
                self.last_wait[eng][inc] = v
        self.cnt[inc] += 1

        def wrapped(e, fn=fn, h=h, w=selfw):
            i = fn(e)
            if w:
                i.wait_op(h, w, "sem-ge")
            i.then_inc(h, 1)
        self.streams[eng].append(wrapped)
        return self.cnt[inc]

    def _dma_on(self, queue, dst, src, slot=None):
        key = f"dma{self.dma_n}"
        self.dma_n += 1
        if not hasattr(self, "dma_slots"):
            self.dma_slots = {}
        if slot is None:
            slot = key
        if slot not in self.dma_slots:
            assert len(self.dma_slots) < self.N_DMA_SEMS, "DMA sem pool exhausted"
            self.dma_slots[slot] = [self.dma_sems[len(self.dma_slots)], 0]
        ent = self.dma_slots[slot]
        ent[1] += 16
        h = ent[0]
        self.sem_handles[key] = h
        self.cnt[key] = ent[1]
        for e_ in self.ENGINES:
            self.last_wait[e_][key] = 0
        self.streams[queue].append(
            lambda e, d=dst, s=src, h=h: e.dma_start(out=d, in_=s).then_inc(h, 16))
        return key

    def dma(self, dst, src, slot=None):
        return self._dma_on("sync", dst, src, slot)

    def dma2(self, dst, src, slot=None):
        return self._dma_on("scalar", dst, src, slot)

    def wait(self, eng, sem, val):
        if not val or val <= 0:
            return
        if sem.startswith("dma"):
            val = self.cnt[sem]
        assert val <= self.cnt[sem], f"{eng} waits future {sem}={val} (now {self.cnt[sem]})"
        if self.last_wait[eng][sem] >= val:
            return
        self.last_wait[eng][sem] = val
        h = self.sem_handles[sem]
        self.streams[eng].append(lambda e, h=h, v=val: e.wait_ge(h, v))

    def barrier(self):
        return dict(self.cnt)

    def wait_all(self, eng, counts, sems=("T", "S", "V")):
        for s in sems:
            self.wait(eng, s, counts[s])

    def replay(self, block):
        for name in self.ENGINES:
            ops = self.streams[name]

            def body(eng, ops=ops):
                for f in ops:
                    f(eng)
            body.__name__ = "_" + name
            getattr(block, name)(body)


def view(pool, byte_off, dtype, shape):
    esz = mybir.dt.size(dtype)
    assert byte_off % 32 == 0 and byte_off % esz == 0
    n = prod(shape[1:])
    ap = pool.bitcast(dtype).ap()[0:shape[0], byte_off // esz: byte_off // esz + n]
    if len(shape) > 2:
        names = [f"a{i}" for i in range(len(shape) - 1)]
        spec = "p (" + " ".join(names) + ") -> p " + " ".join(names)
        ap = ap.rearrange(spec, **{names[i]: shape[i + 1] for i in range(len(shape) - 1)})
    return ap


def build_program(upto="full", dumps=()):
    nc = bass.Bass("TRN2", target_bir_lowering=False)

    x_t = nc.dram_tensor("x_t", [D, T], F32, kind="ExternalInput").ap()
    x_own_t = nc.dram_tensor("x_own_t", [D, OWNT], F32, kind="ExternalInput").ap()
    W_attn = nc.dram_tensor("W_attn", [D, 3 * D], F32, kind="ExternalInput").ap()
    W_projd = nc.dram_tensor("W_proj", [D, D], F32, kind="ExternalInput").ap()
    U1d = nc.dram_tensor("U1", [D, RANK], F32, kind="ExternalInput").ap()
    V1d = nc.dram_tensor("V1", [RANK, DFF], F32, kind="ExternalInput").ap()
    U2d = nc.dram_tensor("U2", [DFF, RANK], F32, kind="ExternalInput").ap()
    V2d = nc.dram_tensor("V2", [RANK, D], F32, kind="ExternalInput").ap()
    b1d = nc.dram_tensor("b1l", [128, DFF // 128], F32, kind="ExternalInput").ap()
    b2d = nc.dram_tensor("b2l", [128, D // 128], F32, kind="ExternalInput").ap()
    masks_d = nc.dram_tensor("masks", [128, 2, 2, 128], BF16, kind="ExternalInput").ap()
    identb_d = nc.dram_tensor("identb", [128, 128], BF16, kind="ExternalInput").ap()
    identf_d = nc.dram_tensor("identf", [128, 128], F32, kind="ExternalInput").ap()
    onescol_d = nc.dram_tensor("onescol", [128, 1], BF16, kind="ExternalInput").ap()
    onesrow_d = nc.dram_tensor("onesrow", [1, 128], F32, kind="ExternalInput").ap()
    y_out = nc.dram_tensor("y", [OWNT, D], F32, kind="ExternalOutput").ap()
    kf2_d = nc.dram_tensor("kf2scratch", [128, NP, 2048], BF16, kind="Internal").ap()

    dump_specs = {
        "lnxb": (BF16, 0), "kf": (BF16, 1), "vt": (BF16, 2), "qf": (BF16, 3),
        "msnap": (BF16, 4), "prefsnap": (F32, 5), "avpart": (BF16, 6), "resid": (F32, 7),
        "wxb": (BF16, 8),
    }
    dump_t = {}

    with contextlib.ExitStack() as stack:
        E_ = stack.enter_context
        poolA = E_(nc.sbuf_tensor("poolA", [128, 65536], U8))
        poolB = E_(nc.sbuf_tensor("poolB", [128, 32768], U8))
        poolC = E_(nc.sbuf_tensor("poolC", [128, 65536], U8))
        poolD = E_(nc.sbuf_tensor("poolD", [128, 28672], U8))
        identb = E_(nc.sbuf_tensor("sb_identb", [128, 128], BF16))
        identf = E_(nc.sbuf_tensor("sb_identf", [128, 128], F32))
        onescol = E_(nc.sbuf_tensor("sb_onescol", [128, 1], BF16))
        onesrow = E_(nc.sbuf_tensor("sb_onesrow", [1, 128], F32))
        masksb = E_(nc.sbuf_tensor("sb_masksb", [128, 2, 2, 128], BF16))
        consts = E_(nc.sbuf_tensor("consts", [128, 3], F32))
        msnap = E_(nc.sbuf_tensor("msnap", [128, 8, 512], BF16))
        prefsnap = E_(nc.sbuf_tensor("prefsnap", [128, 17, 8], F32))
        b1s = E_(nc.sbuf_tensor("sb_b1s", [128, DFF // 128], F32))
        msum_t = E_(nc.sbuf_tensor("sb_msum", [128, 512], F32))
        prefsum_t = E_(nc.sbuf_tensor("sb_prefsum", [128, 8], F32))
        b2s = E_(nc.sbuf_tensor("sb_b2s", [128, D // 128], F32))
        psT = E_(nc.psum_tensor("ps", [128, 4096], F32))

        b = Build(nc)
        b.init_sems(stack)

        def vsync():
            b.wait("vector", "V", b.cnt["V"])

        def ssync():
            b.wait("scalar", "S", b.cnt["S"])

        def tfence():
            return b.cnt["T"]

        lnxh = [view(poolC, 0, BF16, [128, DC, 2048]),
                view(poolC, 32768, BF16, [128, DC, 2048])]
        kf = view(poolA, 0, BF16, [128, NP, 2048])
        vt = view(poolA, 32768, BF16, [128, 16, 1024])
        resid = view(poolA, 0, F32, [128, DC, OWNT])
        qf = view(poolB, 0, BF16, [128, NP, OWNT])
        v1b = view(poolB, 0, BF16, [128, 2, DFF])
        u2b = view(poolB, 16384, BF16, [128, 32, RANK])
        avpart = view(poolC, 0, BF16, [128, NP, OWNT])
        x2stage = view(poolC, 32768, F32, [128, DC, 512])
        tmp4 = view(poolC, 57344, F32, [128, 512])
        ln2blk = view(poolC, 49152, BF16, [128, DC, 512])
        xown = view(poolC, 32768, F32, [128, DC, 512])
        ybuf = view(poolC, 0, F32, [128, 4, D])
        y_cv = view(poolC, 16384, F32, [128, DC, 512])
        lnx2b = view(poolC, 32768, BF16, [128, DC, 512])
        p1bv = view(poolC, 40960, BF16, [128, 2, 512])
        hb2 = view(poolC, 43008, BF16, [128, 2, 512])
        q2bv = view(poolC, 45056, BF16, [128, 2, 512])
        xstage2_ = [view(poolD, 0, F32, [128, DC, 512]),
                    view(poolA, 0, F32, [128, DC, 512])]
        rawb = view(poolD, 16384, BF16, [128, 512])
        sqb = view(poolD, 17408, BF16, [128, 512])
        rows0 = [view(poolD, 18432 + 2048 * i, F32, [1, 512]) for i in range(4)]
        r_rows = view(poolB, 0, F32, [1, T])
        nm_rows = view(poolB, 16384, F32, [1, T])
        tmp32 = view(poolD, 26624, F32, [128, 512])
        wstage = view(poolD, 0, F32, [128, 2, 1024])
        wxb = view(poolD, 8192, BF16, [128, DC, 1024])
        eb = view(poolD, 0, BF16, [128, 6, 512])
        gb = view(poolD, 6144, BF16, [128, 6, 512])
        ab = view(poolD, 12288, BF16, [128, 6, 512])
        zb = view(poolD, 18432, BF16, [128, 3, 2, 128])
        zmb = view(poolD, 12288, BF16, [128, 3, 2, 128])
        abh = view(poolD, 13824, BF16, [128, 2, 128])
        K_ILV = False
        kstage4 = view(poolD, 20480, F32, [128, 8, 128])
        wkvh = [view(poolD, 14336, BF16, [128, 8, 128]),
                view(poolD, 16384, BF16, [128, 8, 128])]
        kevac2 = [view(poolD, 24576, BF16, [128, 512]),
                  view(poolD, 25600, BF16, [128, 512])]
        ab2 = [view(poolD, 14336, BF16, [128, 2, 512]),
               view(poolD, 16384, BF16, [128, 2, 512])]
        ktb2_ = [view(poolD, 24576, BF16, [128, 1024]),
                 view(poolD, 26624, BF16, [128, 1024])]
        wkvb = view(poolD, 0, BF16, [128, DC, 1024])
        wstage4 = view(poolD, 16384, F32, [128, 512])
        raw4 = view(poolD, 18432, BF16, [128, 512])
        sq4 = view(poolD, 19456, BF16, [128, 512])
        rows4 = [view(poolD, 20480 + 2048 * i, F32, [1, 512]) for i in range(4)]
        wprojb = view(poolD, 0, BF16, [128, NP, 1024])
        wstage6 = view(poolD, 16384, F32, [128, 1024])
        l2_rstd = view(poolD, 20480, F32, [1, 2048])
        l2_nm = view(poolD, 8192, F32, [1, 2048])
        row6_mean = view(poolD, 2048, F32, [1, 512])
        row6_var = view(poolD, 4096, F32, [1, 512])
        raw6 = view(poolD, 0, BF16, [128, 512])
        sq6 = view(poolD, 1024, BF16, [128, 512])
        tmp6 = view(poolD, 0, F32, [128, 512])
        u1b = view(poolC, 55296, BF16, [128, DC, RANK])
        v2b = view(poolC, 59392, BF16, [128, 2, D])

        msum_sb = msum_t.ap()
        prefsum_sb = prefsum_t.ap()
        ps = psT.ap()
        psbf = psT.bitcast(BF16).ap()

        def bank(i, lo=0, hi=512):
            return ps[:, i * 512 + lo: i * 512 + hi]
        ps8 = ps.rearrange("p (k c) -> p k c", k=8)
        tp_ps = psbf[:, 0:1024]

        for name in dumps:
            dt, _ = dump_specs[name]
            shp = {"lnxb": DC * T, "kf": NP * 2048, "vt": 16 * 1024, "qf": NP * OWNT,
                   "msnap": 8 * 512, "prefsnap": 17 * 8, "avpart": NP * OWNT,
                   "resid": DC * OWNT, "wxb": DC * 1024}[name]
            dump_t[name] = nc.dram_tensor("dump_" + name, [128, shp], dt,
                                          kind="ExternalOutput").ap()
        dump_src = {}

        d_misc_keys = []
        for dst, src in ((identb[:, :], identb_d[:, :]), (identf[:, :], identf_d[:, :]),
                         (onescol[:, :], onescol_d[:, :]), (onesrow[:, :], onesrow_d[:, :]),
                         (masksb[:, :, :, :], masks_d[:, :, :, :]),
                         (b1s[:, :], b1d[:, :]), (b2s[:, :], b2d[:, :])):
            d_misc_keys.append(b.dma(dst, src))

        def wait_misc(eng):
            for k in d_misc_keys:
                b.wait(eng, k, 16)
        b.op("vector", lambda e: e.memset(consts[:, 0:1], EPS))
        b.op("vector", lambda e: e.memset(consts[:, 1:2], 1.0))
        cV = b.op("vector", lambda e: e.memset(consts[:, 2:3], 1.0), inc="V")
        onescol_f = consts[:, 2:3]
        eps_ap = consts[:, 0:1]

        def emit_ln_stats(src_f32, rows, co, raw, sq, src_ready_d):
            rm = rows[0][0:1, co:co + 512]
            rv = rows[1][0:1, co:co + 512]
            rr = rows[2][0:1, co:co + 512]
            rn = rows[3][0:1, co:co + 512]
            b.wait("vector", "d", src_ready_d)
            b.wait("vector", "V", cV)
            b.wait("vector", "T", b.cnt["T"])
            raw2 = [raw, view(poolD, 6144, BF16, [128, 512])]
            sq2 = [sq, view(poolD, 7168, BF16, [128, 512])]
            t_done = {}
            for dc in range(DC):
                if dc >= 2:
                    b.wait("vector", "T", t_done[dc - 2])
                b.op("vector", lambda e, a=raw2[dc % 2], s=src_f32[:, dc, :]:
                     e.tensor_copy(a, s))
                b.op("vector", lambda e, a=sq2[dc % 2], r=raw2[dc % 2]:
                     e.tensor_mul(a, r, r), inc="V", ss=True)
                b.wait("tensor", "V", b.cnt["V"])
                b.op("tensor", lambda e, o=bank(0)[0:1, :], l=onescol[:, :], r=raw2[dc % 2],
                     st=(dc == 0), sp=(dc == DC - 1): e.matmul(o, l, r, skip_group_check=True, start=st, stop=sp))
                b.op("tensor", lambda e, o=bank(1)[0:1, :], l=onescol[:, :], r=sq2[dc % 2],
                     st=(dc == 0), sp=(dc == DC - 1): e.matmul(o, l, r, skip_group_check=True, start=st, stop=sp),
                     inc="T")
                t_done[dc] = b.cnt["T"]
            b.wait("scalar", "T", tfence())
            b.op("scalar", lambda e, o=rm, i=bank(0)[0:1, :]: e.mul(o, i, 1.0 / D))
            b.op("scalar", lambda e, o=rv, i=bank(1)[0:1, :]: e.mul(o, i, 1.0 / D),
                 inc="S")
            b.wait("vector", "S", b.cnt["S"])
            b.op("vector", lambda e, o=rn, m=rm: e.tensor_mul(o, m, m))
            b.op("vector", lambda e, o=rv, v=rv,
                 m=rn: e.tensor_sub(o, v, m), inc="V", ss=True)
            b.wait("scalar", "V", b.cnt["V"])
            b.op("scalar", lambda e, o=rr, i=rv:
                 e.activation(o, i, AF.Ln, bias=eps_ap[0:1, :]))
            b.op("scalar", lambda e, o=rr, i=rr:
                 e.activation(o, i, AF.Exp, scale=-0.5), inc="S", ss=True)
            b.wait("vector", "S", b.cnt["S"])
            b.op("vector", lambda e, o=rn, m=rm,
                 r=rr: e.scalar_tensor_tensor(o, m, -1.0, r, ALU.mult, ALU.mult),
                 inc="V")

        def emit_ln_normalize(src_f32, dst_slices, rows, co, tmpn):
            rr = rows[2][0:1, co:co + 512]
            rn = rows[3][0:1, co:co + 512]
            b.wait("tensor", "V", b.cnt["V"])
            b.op("tensor", lambda e, o=bank(2), l=onesrow[0:1, :], r=rr:
                 e.matmul(o, l, r, skip_group_check=True, start=True, stop=True))
            b.op("tensor", lambda e, o=bank(3), l=onesrow[0:1, :], r=rn:
                 e.matmul(o, l, r, skip_group_check=True, start=True, stop=True), inc="T")
            b.wait("vector", "T", tfence())
            for dc in range(DC):
                b.op("vector", lambda e, o=tmpn, s=src_f32[:, dc, :], r=bank(2):
                     e.tensor_mul(o, s, r), ss=True)
                b.op("vector", lambda e, o=dst_slices[dc], t=tmpn, n=bank(3):
                     e.tensor_add(o, t, n), inc="V", ss=True)
            return b.cnt["V"]

        def emit_ln_stats_mixed(src_f32, rows, co_rs, raw, sq):
            rm = rows[0][0:1, 0:512]
            rv = rows[1][0:1, 0:512]
            rr = rows[2][0:1, co_rs:co_rs + 512]
            rn = rows[3][0:1, co_rs:co_rs + 512]
            b.wait("vector", "V", cV)
            t_entry = b.cnt["T"]
            raw2_ = [raw, view(poolD, 6144, BF16, [128, 512])]
            sq2_ = [sq, view(poolD, 7168, BF16, [128, 512])]
            t_done6 = {}
            c6 = {}

            def p6cpy(dc):
                b.wait("vector", "T", t_done6[dc - 2] if dc >= 2 else t_entry)
                c6[dc] = b.op("vector", lambda e, a=raw2_[dc % 2], s=src_f32[:, dc, :]:
                              e.tensor_copy(a, s), inc="V")
            p6cpy(0)
            for dc in range(DC):
                if dc + 1 < DC:
                    p6cpy(dc + 1)
                b.wait("vector", "V", c6[dc])
                sq_v = b.op("vector", lambda e, a=sq2_[dc % 2], r=raw2_[dc % 2]:
                            e.tensor_mul(a, r, r), inc="V")
                b.wait("tensor", "V", sq_v)
                b.op("tensor", lambda e, o=bank(0)[0:1, :], l=onescol[:, :], r=raw2_[dc % 2],
                     st=(dc == 0), sp=(dc == DC - 1): e.matmul(o, l, r, skip_group_check=True, start=st, stop=sp))
                b.op("tensor", lambda e, o=bank(1)[0:1, :], l=onescol[:, :], r=sq2_[dc % 2],
                     st=(dc == 0), sp=(dc == DC - 1): e.matmul(o, l, r, skip_group_check=True, start=st, stop=sp),
                     inc="T")
                t_done6[dc] = b.cnt["T"]
            b.wait("scalar", "T", tfence())
            b.op("scalar", lambda e, o=rm, i=bank(0)[0:1, :]: e.mul(o, i, 1.0 / D))
            b.op("scalar", lambda e, o=rv, i=bank(1)[0:1, :]: e.mul(o, i, 1.0 / D), inc="S")
            b.wait("vector", "S", b.cnt["S"])
            b.op("vector", lambda e, o=rn, m=rm: e.tensor_mul(o, m, m))
            b.op("vector", lambda e, o=rv, v=rv, m=rn: e.tensor_sub(o, v, m), inc="V",
                 ss=True)
            b.wait("scalar", "V", b.cnt["V"])
            b.op("scalar", lambda e, o=rr, i=rv:
                 e.activation(o, i, AF.Ln, bias=eps_ap[0:1, :]))
            b.op("scalar", lambda e, o=rr, i=rr:
                 e.activation(o, i, AF.Exp, scale=-0.5), inc="S", ss=True)
            b.wait("vector", "S", b.cnt["S"])
            b.op("vector", lambda e, o=rn, m=rm, r=rr:
                 e.scalar_tensor_tensor(o, m, -1.0, r, ALU.mult, ALU.mult), inc="V")

        x_src = x_t.rearrange("(dc p) t -> p dc t", p=128)
        wait_misc("tensor")
        wait_misc("vector")
        wait_misc("scalar")
        wait_misc("gpsimd")
        relVs = [0, 0]
        relTs = [0, 0]
        for tb in range(8):
            xs = xstage2_[tb % 2]
            b.wait("sync", "V", relVs[tb % 2])
            b.wait("sync", "T", relTs[tb % 2])
            dd = b.dma(xs[:, :, :], x_src[:, :, tb * 512:(tb + 1) * 512])
            co = tb * 512
            rm = rows0[0][0:1, 0:512]
            rv = rows0[1][0:1, 0:512]
            rr = r_rows[0:1, co:co + 512]
            rn = nm_rows[0:1, co:co + 512]
            b.wait("vector", dd, 16)
            b.wait("vector", "V", cV)
            b.wait("vector", "T", b.cnt["T"])
            rawx = [rawb, view(poolD, 26624, BF16, [128, 512]),
                    view(poolD, 22528, BF16, [128, 512]),
                    view(poolD, 24576, BF16, [128, 512])]
            sqx = [sqb, view(poolD, 27648, BF16, [128, 512]),
                   view(poolD, 23552, BF16, [128, 512]),
                   view(poolD, 25600, BF16, [128, 512])]
            t_done = {}
            ccnt = {}

            def p1cpy(dc):
                if dc >= 4:
                    b.wait("vector", "T", t_done[dc - 4])
                ccnt[dc] = b.op("vector", lambda e, a=rawx[dc % 4], x_=xs[:, dc, :]:
                                e.tensor_copy(a, x_), inc="V")
            p1cpy(0)
            p1cpy(1)
            for dc in range(DC):
                if dc + 2 < DC:
                    p1cpy(dc + 2)
                b.wait("vector", "V", ccnt[dc])
                sq_v = b.op("vector", lambda e, a=sqx[dc % 4], r=rawx[dc % 4]:
                            e.tensor_mul(a, r, r), inc="V")
                b.wait("tensor", "V", sq_v)
                b.op("tensor", lambda e, o=bank(0)[0:1, :], l=onescol[:, :], r=rawx[dc % 4],
                     st=(dc == 0), sp=(dc == DC - 1): e.matmul(o, l, r, skip_group_check=True, start=st, stop=sp))
                b.op("tensor", lambda e, o=bank(1)[0:1, :], l=onescol[:, :], r=sqx[dc % 4],
                     st=(dc == 0), sp=(dc == DC - 1): e.matmul(o, l, r, skip_group_check=True, start=st, stop=sp),
                     inc="T")
                t_done[dc] = b.cnt["T"]
            relVs[tb % 2] = b.cnt["V"]
            relTs[tb % 2] = b.cnt["T"]
            b.wait("scalar", "T", tfence())
            b.op("scalar", lambda e, o=rm, i=bank(0)[0:1, :]: e.mul(o, i, 1.0 / D))
            b.op("scalar", lambda e, o=rv, i=bank(1)[0:1, :]: e.mul(o, i, 1.0 / D), inc="S")
            b.wait("vector", "S", b.cnt["S"])
            b.op("vector", lambda e, o=rn, m=rm: e.tensor_mul(o, m, m))
            b.op("vector", lambda e, o=rv, v=rv, m=rn: e.tensor_sub(o, v, m), inc="V",
                 ss=True)
            b.wait("scalar", "V", b.cnt["V"])
            b.op("scalar", lambda e, o=rr, i=rv:
                 e.activation(o, i, AF.Ln, bias=eps_ap[0:1, :]))
            b.op("scalar", lambda e, o=rr, i=rr:
                 e.activation(o, i, AF.Exp, scale=-0.5), inc="S", ss=True)
            b.wait("vector", "S", b.cnt["S"])
            b.op("vector", lambda e, o=rn, m=rm, r=rr:
                 e.scalar_tensor_tensor(o, m, -1.0, r, ALU.mult, ALU.mult), inc="V")
        relVs = [b.cnt["V"], b.cnt["V"]]
        bc_free = 0
        p2_prev = {}
        for tb in range(8):
            xs = xstage2_[tb % 2]
            b.wait("sync", "V", relVs[tb % 2])
            dd = b.dma(xs[:, :, :], x_src[:, :, tb * 512:(tb + 1) * 512])
            co = tb * 512
            bka, bkb = (2, 3) if tb % 2 == 0 else (4, 5)
            b.wait("tensor", "V", bc_free)
            b.op("tensor", lambda e, o=bank(bka), l=onesrow[0:1, :], r=r_rows[0:1, co:co + 512]:
                 e.matmul(o, l, r, skip_group_check=True, start=True, stop=True))
            b.op("tensor", lambda e, o=bank(bkb), l=onesrow[0:1, :], r=nm_rows[0:1, co:co + 512]:
                 e.matmul(o, l, r, skip_group_check=True, start=True, stop=True), inc="T")
            tb_T = tfence()
            b.wait("vector", "T", tb_T)
            b.wait("vector", dd, 16)
            tmp2 = [tmp32, view(poolD, 16384, F32, [128, 512]),
                    view(poolD, 22528, F32, [128, 512]),
                    view(poolD, 24576, F32, [128, 512])]
            mcnt2 = {}
            acnt2 = {}

            def p2mul(dc):
                b.wait("vector", "V",
                       acnt2[dc - 4] if dc >= 4 else p2_prev.get(dc + 4, 0))
                mcnt2[dc] = b.op("vector", lambda e, o=tmp2[dc % 4], s=xs[:, dc, :],
                                 r=bank(bka): e.tensor_mul(o, s, r), inc="V")
            p2mul(0)
            p2mul(1)
            for dc in range(DC):
                if dc + 2 < DC:
                    p2mul(dc + 2)
                b.wait("vector", "V", mcnt2[dc])
                acnt2[dc] = b.op("vector", lambda e,
                                 o=lnxh[tb // 4][:, dc, (tb % 4) * 512:(tb % 4 + 1) * 512],
                                 t=tmp2[dc % 4], n=bank(bkb): e.tensor_add(o, t, n),
                                 inc="V")
            p2_prev = acnt2
            relVs[tb % 2] = b.cnt["V"]
            if tb % 2 == 1:
                bc_free = b.cnt["V"]
        p0_done = b.barrier()

        s_rel = [0, 0, 0]
        z_rel = [0, 0, 0]
        e_rel = [0, 0, 0]
        e_relP = [0, 0, 0]
        ab_rel = [0, 0, 0]
        zw_rel = [0, 0, 0]
        g_rel = [0, 0, 0]
        SBK = (0, 2, 6)
        av_free = {4: 0, 5: 0}
        av_freeS = {4: 0, 5: 0}
        sweep_state = {"tpV0": 0, "tpV1": 0, "snapV": 0, "mprefT": 0}

        def emit_sweep_kt(ktg, half, vt_ready_V=0):
            if True:
                ktloc = ktg - 16 * half
                b.wait("tensor", "V", vt_ready_V)
                par2 = ktloc % 2
                tpx = psbf[:, par2 * 1024:(par2 + 1) * 1024]
                ktb = ktb2_[par2]
                b.wait("tensor", "V", sweep_state["tpV" + str(par2)])
                b.wait("tensor", "V", sweep_state["snapV"])
                if K_ILV and half == 1:
                    b.wait("tensor", kfload[ktloc // 4], 16)
                for p in range(NP):
                    b.op("tensor", lambda e, o=tpx[:, p * 128:(p + 1) * 128],
                         i=kf[:, p, ktloc * 128:(ktloc + 1) * 128]:
                         e.transpose(o, i, identb[:, :]), inc="T")
                b.wait("vector", "T", tfence())
                sweep_state["tpV" + str(par2)] = b.op(
                    "vector", lambda e, k=ktb, t=tpx: e.tensor_scalar_mul(k[:, :], t, -0.01),
                    inc="V")
                b.wait("tensor", "V", sweep_state["tpV" + str(par2)])
                for p in range(NP):
                    for par in range(2):
                        b.op("tensor", lambda e,
                             o=ps[par * 64:(par + 1) * 64, 6 * 512 + p * 64: 6 * 512 + (p + 1) * 64],
                             l=ktb[:, p * 128 + par * 64: p * 128 + (par + 1) * 64],
                             r=vt[:, ktloc, (2 * p + par) * 64:(2 * p + par + 1) * 64],
                             st=(ktg % 16 == 0 and p == 0), tp=(0, par * 64):
                             e.matmul(o, l, r, skip_group_check=True, start=st, stop=True, tile_position=tp), inc="T")
                for dc in range(DC):
                    b.op("tensor", lambda e, o=bank(7, 2 * dc, 2 * dc + 1),
                         l=vt[:, ktloc, dc * 128:(dc + 1) * 128], r=onescol[:, :],
                         st=(ktg % 16 == 0 and dc == 0): e.matmul(o, l, r, skip_group_check=True, start=st, stop=True),
                         inc="T")
                sweep_state["mprefT"] = b.cnt["T"]
                if ktg % 2 == 1:
                    j = (ktg + 3) // 2
                    pref_src = bank(7, 0, 16).rearrange("p (a b) -> p a b", b=2)[:, :, 0]
                    if 2 <= j <= 16:
                        b.wait("vector", "T", tfence())
                        if half == 0:
                            b.op("vector", lambda e, o=msnap[:, j % 8, :], s=bank(6):
                                 e.tensor_copy(o, s))
                            sweep_state["snapV"] = b.op(
                                "vector", lambda e, o=prefsnap[:, j, :], s=pref_src:
                                e.tensor_copy(o, s), inc="V")
                        else:
                            b.op("vector", lambda e, o=msnap[:, j % 8, :], s=bank(6),
                                 m=msum_sb[:, :]: e.tensor_add(o, s, m))
                            sweep_state["snapV"] = b.op(
                                "vector", lambda e, o=prefsnap[:, j, :], s=pref_src,
                                m=prefsum_sb[:, :]: e.tensor_add(o, s, m), inc="V")
                    if ktg == 15:
                        b.wait("vector", "T", tfence())
                        b.op("vector", lambda e: e.tensor_copy(msum_sb[:, :], bank(6)))
                        sweep_state["snapV"] = b.op(
                            "vector", lambda e, o=prefsum_sb[:, :], s=pref_src:
                            e.tensor_copy(o, s), inc="V")

        b.wait("sync", "V", p0_done["V"])
        b.wait("vector", "T", p0_done["T"])
        b.wait("vector", "S", p0_done["S"])

        w_src2 = W_attn.rearrange("(a p) c -> p a c", p=128)

        def stream_w(colbase, dst):
            b.wait("vector", "T", b.cnt["T"])
            casts = []
            for dp in range(4):
                b.wait("sync", "V", casts[-1] if casts else b.cnt["V"])
                dd = b.dma(wstage[:, :, :],
                           w_src2[:, 2 * dp:2 * dp + 2, colbase:colbase + 1024])
                b.wait("vector", dd, 16)
                for k in range(2):
                    casts.append(b.op("vector", lambda e, o=dst[:, 2 * dp + k, :],
                                      s=wstage[:, k, :]: e.tensor_copy(o, s), inc="V"))
            return b.cnt["V"]

        evacV = {4: 0, 5: 0}

        def proj_loop(tiles, emit_mms, emit_evac):
            for i, tile in enumerate(tiles):
                bk = 4 + i % 2
                b.wait("tensor", "V", evacV[bk])
                emit_mms(bk, tile)
                b.wait("vector", "T", tfence())
                evacV[bk] = emit_evac(bk, tile)

        wV = stream_w(D, wxb)
        b.wait("tensor", "V", wV)

        def k_mms(bk, tile):
            hc, tb2 = tile
            for dc in range(DC):
                b.op("tensor", lambda e, o=bank(bk), l=wxb[:, dc, hc * 128:(hc + 1) * 128],
                     r=lnxh[0][:, dc, tb2 * 512:(tb2 + 1) * 512], st=(dc == 0),
                     sp=(dc == DC - 1): e.matmul(o, l, r, skip_group_check=True, start=st, stop=sp), inc="T")

        def k_evac(bk, tile):
            hc, tb2 = tile
            return b.op("vector", lambda e, o=kf[:, hc, tb2 * 512:(tb2 + 1) * 512],
                        s=bank(bk): e.tensor_copy(o, s), inc="V")
        proj_loop([(hc, tb2) for hc in range(NP) for tb2 in range(4)], k_mms, k_evac)

        if upto == "k":
            dump_src.update(kf=kf, wxb=wxb)
            return finish(nc, b, stack, dump_t, dump_src, b.barrier())

        wV = stream_w(2 * D, wxb)
        b.wait("tensor", "V", wV)

        def v_mms(bk, tile):
            tt, vc2 = tile
            for dc in range(DC):
                b.op("tensor", lambda e, o=bank(bk), l=lnxh[0][:, dc, tt * 128:(tt + 1) * 128],
                     r=wxb[:, dc, vc2 * 512:(vc2 + 1) * 512], st=(dc == 0),
                     sp=(dc == DC - 1): e.matmul(o, l, r, skip_group_check=True, start=st, stop=sp), inc="T")

        def v_evac(bk, tile):
            tt, vc2 = tile
            return b.op("vector", lambda e, o=vt[:, tt, vc2 * 512:(vc2 + 1) * 512],
                        s=bank(bk): e.tensor_scalar_mul(o, s, VSCALE), inc="V")
        proj_loop([(tt, vc2) for tt in range(16) for vc2 in range(2)], v_mms, v_evac)

        if upto == "kv":
            dump_src.update(kf=kf, vt=vt)
            return finish(nc, b, stack, dump_t, dump_src, b.barrier())

        wV = stream_w(0, wxb)
        b.wait("tensor", "V", wV)
        lnx_gh = [h.rearrange("p dc (g c) -> p dc g c", c=128) for h in lnxh]

        def q_mms(bk, tile):
            hc, qb4 = tile
            for dc in range(DC):
                rhs = lnx_gh[qb4 // 2][:, dc, 0::2, :][:, (qb4 % 2) * 4:(qb4 % 2 + 1) * 4, :]
                b.op("tensor", lambda e, o=bank(bk), l=wxb[:, dc, hc * 128:(hc + 1) * 128],
                     r=rhs, st=(dc == 0), sp=(dc == DC - 1):
                     e.matmul(o, l, r, skip_group_check=True, start=st, stop=sp), inc="T")

        def q_evac(bk, tile):
            hc, qb4 = tile
            return b.op("vector", lambda e, o=qf[:, hc, qb4 * 512:(qb4 + 1) * 512],
                        s=bank(bk): e.tensor_scalar_mul(o, s, SCALE_QK), inc="V")

        q_tiles = [(hc, qb4) for hc in range(NP) for qb4 in range(4)]
        for i, tile in enumerate(q_tiles):
            bk = 4 + i % 2
            b.wait("tensor", "V", evacV[bk])
            q_mms(bk, tile)
            qT = tfence()
            if i < 16:
                emit_sweep_kt(i, 0)
            b.wait("vector", "T", qT)
            evacV[bk] = q_evac(bk, tile)
        p1_done = b.barrier()

        dump_src.update(lnxb=lnxh[0], kf=kf, vt=vt, qf=qf)
        if upto == "p1":
            return finish(nc, b, stack, dump_t, dump_src, b.barrier())

        def kts_of(c, half):
            lo, hi = 16 * half, 16 * half + 16
            return list(range(lo, min(8 * c + 8, hi)))

        kblk = [0]
        kcast = {}
        kwT = [0, 0]
        ke_dma = {0: 0, 1: 0}
        kfload = []

        def emit_attention(half):
            b.wait("tensor", "V", b.cnt["V"])
            if K_ILV and half == 0:
                b.wait("sync", "T", p1_done["T"])
            if K_ILV and half == 1:
                for k in kfload:
                    b.wait("tensor", k, 16)
            for c in range(4):
                kts = kts_of(c, half)
                if not kts:
                    continue
                for p in range(NP):
                    avbk = 4 + p % 2
                    av_done = 0
                    nk = len(kts)
                    hc, tb2 = kblk[0] // 4, kblk[0] % 4
                    idle_bk = 4 + (1 - p % 2)
                    if K_ILV and half == 0 and tb2 == 0:
                        b.wait("sync", "V", kcast[hc - 1] if hc > 0 else p1_done["V"])
                        kdd = b.dma(kstage4[:, :, :],
                                    w_src2[:, :, D + hc * 128: D + (hc + 1) * 128],
                                    slot="wk")
                        b.wait("vector", kdd, 16)
                        b.wait("vector", "T", kwT[hc % 2])
                        kcast[hc] = b.op("vector", lambda e, o=wkvh[hc % 2][:, :, :],
                                         s=kstage4[:, :, :]: e.tensor_copy(o, s),
                                         inc="V")

                    def emit_qk(idx):
                        ktg = kts[idx]
                        ktloc = ktg - 16 * half
                        sb = idx % 3
                        jm = ktg // 2 + 1
                        j0 = max(4 * c + 1, jm)
                        coloff = (j0 - (4 * c + 1)) * 128
                        N = 512 - coloff
                        qcols = qf[:, p, c * 512 + coloff: (c + 1) * 512]
                        b.wait("tensor", "S", s_rel[sb])
                        b.wait("tensor", "V", z_rel[sb])
                        for hh in range(2):
                            b.op("tensor", lambda e,
                                 o=bank(SBK[sb] + hh, 0, N),
                                 l=kf[hh * 64:(hh + 1) * 64, p, ktloc * 128:(ktloc + 1) * 128],
                                 r=qcols[hh * 64:(hh + 1) * 64, :], tp=(hh * 64, 0):
                                 e.matmul(o, l, r, skip_group_check=True, start=True, stop=True, tile_position=tp),
                                 inc="T")
                        return b.cnt["T"]

                    def is_m(idx):
                        return 8 * c <= kts[idx] <= 8 * c + 7

                    gcnt, mcnt, pcnt = {}, {}, {}

                    def emit_eg(idx, qkT):
                        ktg = kts[idx]
                        sb = idx % 3
                        masked = is_m(idx)
                        j0 = max(4 * c + 1, ktg // 2 + 1)
                        coloff = (j0 - (4 * c + 1)) * 128
                        N = 512 - coloff
                        b.wait("scalar", "T", qkT)
                        b.wait("scalar", "V", e_rel[sb])
                        b.wait("scalar", "P", e_relP[sb])
                        b.op("scalar", lambda e, o=eb[:, 2 * sb:2 * sb + 2, 0:N],
                             i=ps8[:, SBK[sb]:SBK[sb] + 2, 0:N]:
                             e.activation(o, i, AF.Relu), inc="S")
                        if masked:
                            b.wait("scalar", "V", zw_rel[sb])
                            b.op("scalar", lambda e, o=zb[:, sb, :, :],
                                 i=ps8[:, SBK[sb]:SBK[sb] + 2, 0:128]:
                                 e.activation(o, i, AF.Identity, scale=-0.01,
                                              bias=consts[:, 1:2]), inc="S")
                        s_rel[sb] = b.cnt["S"]
                        b.wait("vector", "S", s_rel[sb])
                        b.wait("vector", "T", g_rel[sb])
                        gcnt[idx] = b.op("vector",
                                         lambda e, o=gb[:, 2 * sb:2 * sb + 2, 0:N],
                                         i=eb[:, 2 * sb:2 * sb + 2, 0:N]:
                                         e.scalar_tensor_tensor(o, i, 0.01, i,
                                                                ALU.add, ALU.mult),
                                         inc="V")
                        pcnt[idx] = 0
                        if masked:
                            midx = ktg % 2
                            zw_rel[sb] = b.op("vector", lambda e, o=zmb[:, sb, :, :],
                                              z=zb[:, sb, :, :], m=masksb[:, midx, :, :]:
                                              e.tensor_mul(o, z, m), inc="V")
                            b.wait("vector", "V", gcnt[idx])
                            mcnt[idx] = b.op("vector",
                                             lambda e, o=gb[:, 2 * sb:2 * sb + 2, 0:128],
                                             m=masksb[:, midx, :, :]:
                                             e.tensor_mul(o, o, m), inc="V")
                        e_rel[sb] = b.cnt["V"]

                    def emit_av(idx):
                        ktg = kts[idx]
                        ktloc = ktg - 16 * half
                        sb = idx % 3
                        jm = ktg // 2 + 1
                        j0 = max(4 * c + 1, jm)
                        coloff = (j0 - (4 * c + 1)) * 128
                        N = 512 - coloff
                        b.wait("tensor", "V", mcnt[idx] if is_m(idx) else gcnt[idx])
                        b.wait("tensor", "P", pcnt[idx])
                        if idx == 0:
                            b.wait("tensor", "V", av_free[avbk])
                            b.wait("tensor", "S", av_freeS[avbk])
                        for hh in range(2):
                            b.op("tensor", lambda e, o=ps[hh * 64:(hh + 1) * 64,
                                 avbk * 512 + coloff: (avbk + 1) * 512],
                                 l=vt[:, ktloc, (2 * p + hh) * 64:(2 * p + hh + 1) * 64],
                                 r=gb[:, 2 * sb + hh, 0:N], st=(idx == 0),
                                 tp=(0, hh * 64):
                                 e.matmul(o, l, r, skip_group_check=True, start=st, stop=True, tile_position=tp),
                                 inc="T")
                        if is_m(idx):
                            for hh in range(2):
                                b.op("tensor", lambda e, o=ps[hh * 64:(hh + 1) * 64,
                                     avbk * 512 + coloff: avbk * 512 + coloff + 128],
                                     l=vt[:, ktloc, (2 * p + hh) * 64:(2 * p + hh + 1) * 64],
                                     r=zmb[:, sb, hh, :], tp=(0, hh * 64):
                                     e.matmul(o, l, r, skip_group_check=True, start=False,
                                              stop=True, tile_position=tp), inc="T")
                        g_rel[sb] = b.cnt["T"]
                        if idx == 0:
                            for i4 in range(4):
                                j = 4 * c + 1 + i4
                                hj = 0 if j <= 9 else 1
                                if hj != half or j < 2:
                                    continue
                                for par in range(2):
                                    b.op("tensor", lambda e,
                                         o=ps[par * 64:(par + 1) * 64,
                                              avbk * 512 + i4 * 128: avbk * 512 + (i4 + 1) * 128],
                                         l=msnap[par * 64:(par + 1) * 64, j % 8,
                                                 p * 64:(p + 1) * 64],
                                         r=qf[par * 64:(par + 1) * 64, p,
                                              (c * 4 + i4) * 128:(c * 4 + i4 + 1) * 128],
                                         tp=(par * 64, par * 64):
                                         e.matmul(o, l, r, skip_group_check=True, start=False, stop=True,
                                                  tile_position=tp), inc="T")
                            g_rel[sb] = b.cnt["T"]

                    fences = {}
                    fences[0] = emit_qk(0)
                    if nk > 1:
                        fences[1] = emit_qk(1)
                    if nk > 2:
                        fences[2] = emit_qk(2)
                    for idx in range(nk):
                        emit_eg(idx, fences[idx])
                        if idx + 3 < nk:
                            fences[idx + 3] = emit_qk(idx + 3)
                        emit_av(idx)
                        if K_ILV and half == 0 and idx < 8:
                            b.wait("tensor", "V", kcast[hc])
                            if idx == 0:
                                b.wait("tensor", "V", av_free[idle_bk])
                                b.wait("tensor", "S", av_freeS[idle_bk])
                            b.op("tensor", lambda e, o=bank(idle_bk),
                                 l=wkvh[hc % 2][:, idx, :],
                                 r=lnxh[1][:, idx, tb2 * 512:(tb2 + 1) * 512],
                                 st=(idx == 0), sp=(idx == 7):
                                 e.matmul(o, l, r, skip_group_check=True, start=st,
                                          stop=sp), inc="T")
                    av_done = b.cnt["T"]
                    b.wait("vector", "T", tfence())
                    if K_ILV and half == 0:
                        if ke_dma[kblk[0] % 2]:
                            b.wait("vector", ke_dma[kblk[0] % 2], 16)
                        kv = b.op("vector", lambda e, o=kevac2[kblk[0] % 2][:, :],
                                  s=bank(idle_bk): e.tensor_copy(o, s), inc="V")
                        av_free[idle_bk] = kv
                        b.wait("sync", "V", kv)
                        ke_dma[kblk[0] % 2] = b.dma(
                            kf2_d[:, hc, tb2 * 512:(tb2 + 1) * 512],
                            kevac2[kblk[0] % 2][:, :], slot=f"ke{kblk[0] % 2}")
                        if tb2 == 3:
                            kwT[hc % 2] = b.cnt["T"]
                        kblk[0] += 1
                    chunk_final = (c <= 2 * half + 1)
                    for i4 in range(4):
                        j = 4 * c + 1 + i4
                        dst = avpart[:, p, (j - 1) * 128: j * 128]
                        src = ps[:, avbk * 512 + i4 * 128: avbk * 512 + (i4 + 1) * 128]
                        if not chunk_final:
                            b.op("vector", lambda e, o=dst, s=src: e.tensor_copy(o, s),
                                 inc="V")
                        elif c <= 1:
                            if j == 1:
                                b.op("vector", lambda e, o=dst, s=src: e.tensor_copy(o, s),
                                     inc="V")
                            else:
                                b.op("vector", lambda e, o=dst, s=src,
                                     pv=prefsnap[:, j, p:p + 1]:
                                     e.tensor_scalar(o, s, pv, None, ALU.add), inc="V")
                        else:
                            b.op("vector", lambda e, o=dst, s=src, pv=prefsnap[:, j, p:p + 1]:
                                 e.scalar_tensor_tensor(o, s, pv, o, ALU.add, ALU.add),
                                 inc="V")
                    av_free[avbk] = b.cnt["V"]
                    av_freeS[avbk] = b.cnt["S"]

        b.wait_all("tensor", p1_done)
        if upto == "sweep":
            dump_src.update(msnap=msnap.ap(), prefsnap=prefsnap.ap())
            return finish(nc, b, stack, dump_t, dump_src, b.barrier())
        emit_attention(0)
        h0_done = b.barrier()

        def p4_sweep(colbase, is_v):
            entry_T = b.cnt["T"]
            casts = []
            wstage4b = view(poolD, 16384, F32, [128, 1024])
            for dc in range(DC):
                b.wait("sync", "V", casts[-1] if casts else b.cnt["V"])
                dd = b.dma(wstage4b[:, :],
                           W_attn[dc * 128:(dc + 1) * 128, colbase: colbase + 1024])
                b.wait("vector", dd, 16)
                b.wait("vector", "T", entry_T)
                casts.append(b.op("vector", lambda e, o=wkvb[:, dc, :],
                                  s=wstage4b[:, :]: e.tensor_copy(o, s), inc="V"))
            wready = b.cnt["V"]
            b.wait("tensor", "V", wready)
            if not is_v:
                for tb in range(4):
                    for hc in range(NP):
                        bk = 4 + hc % 2
                        b.wait("tensor", "V", evacV[bk])
                        for dc in range(DC):
                            b.op("tensor", lambda e, o=bank(bk),
                                 l=wkvb[:, dc, hc * 128:(hc + 1) * 128],
                                 r=lnxh[1][:, dc, tb * 512:(tb + 1) * 512],
                                 st=(dc == 0), sp=(dc == DC - 1):
                                 e.matmul(o, l, r, skip_group_check=True, start=st, stop=sp), inc="T")
                        b.wait("vector", "T", tfence())
                        evacV[bk] = b.op("vector", lambda e,
                                         o=kf[:, hc, tb * 512:(tb + 1) * 512], s=bank(bk):
                                         e.tensor_copy(o, s), inc="V")
            else:
                for tt in range(16):
                    for vc2 in range(2):
                        bk = 4 + (tt * 2 + vc2) % 2
                        b.wait("tensor", "V", evacV[bk])
                        for dc in range(DC):
                            b.op("tensor", lambda e, o=bank(bk),
                                 l=lnxh[1][:, dc, tt * 128:(tt + 1) * 128],
                                 r=wkvb[:, dc, vc2 * 512:(vc2 + 1) * 512],
                                 st=(dc == 0), sp=(dc == DC - 1):
                                 e.matmul(o, l, r, skip_group_check=True, start=st, stop=sp), inc="T")
                        b.wait("vector", "T", tfence())
                        evacV[bk] = b.op("vector", lambda e,
                                         o=vt[:, tt, vc2 * 512:(vc2 + 1) * 512],
                                         s=bank(bk): e.tensor_scalar_mul(o, s, VSCALE),
                                         inc="V")
                    emit_sweep_kt(16 + tt, 1, vt_ready_V=evacV[4 + (tt * 2 + 1) % 2])

        b.wait("sync", "T", h0_done["T"])
        b.wait("sync", "S", h0_done["S"])
        b.wait_all("vector", h0_done, sems=("T", "S"))
        if K_ILV:
            b.wait("sync", ke_dma[0], 16)
            b.wait("sync", ke_dma[1], 16)
            for i in range(4):
                kfload.append(b.dma(kf[:, :, 512 * i:512 * (i + 1)],
                                    kf2_d[:, :, 512 * i:512 * (i + 1)]))
            b.wait("vector", ke_dma[0], 16)
            b.wait("vector", ke_dma[1], 16)
        else:
            p4_sweep(D, is_v=False)
        p4_sweep(2 * D, is_v=True)
        p4_done = b.barrier()

        b.wait_all("tensor", p4_done)
        emit_attention(1)
        attn_done = b.barrier()

        dump_src.update(msnap=msnap.ap(), prefsnap=prefsnap.ap(), avpart=avpart)
        if upto == "attn":
            return finish(nc, b, stack, dump_t, dump_src, b.barrier())

        b.wait("sync", "T", attn_done["T"])
        b.wait("sync", "S", attn_done["S"])
        b.wait("sync", "V", attn_done["V"])
        casts = []
        for p in range(NP):
            if p >= 1:
                b.wait("sync", "V", casts[p - 1])
            dd = b.dma(wstage6[:, :], W_projd[p * 128:(p + 1) * 128, :])
            b.wait("vector", dd, 16)
            b.wait("vector", "T", attn_done["T"])
            casts.append(b.op("vector", lambda e, o=wprojb[:, p, :], s=wstage6[:, :]:
                              e.tensor_copy(o, s), inc="V"))
        wproj_V = b.cnt["V"]
        b.wait("tensor", "V", wproj_V)

        wstage6b = view(poolC, 49152, F32, [128, 1024])

        def stream_ffn(pairs):
            cc = [wproj_V, wproj_V]
            stages = [wstage6, wstage6b]
            for i, (dsl, ssl) in enumerate(pairs):
                n = prod(ssl.shape[1:])
                assert n <= 1024
                stage = stages[i % 2][:, 0:n]
                if len(ssl.shape) == 3:
                    stage = stage.rearrange("p (a c) -> p a c", a=ssl.shape[1])
                b.wait("scalar", "V", cc[-2])
                dd = b.dma2(stage, ssl)
                b.wait("vector", dd, 16)
                cc.append(b.op("vector", lambda e, o=dsl, s=stage: e.tensor_copy(o, s),
                               inc="V"))
            return b.cnt["V"]

        u1_src = U1d.rearrange("(dc p) r -> p dc r", p=128)
        v1_src = V1d.rearrange("(rc p) f -> p rc f", p=128)
        u2_src = U2d.rearrange("(fc p) r -> p fc r", p=128)
        v2_src = V2d.rearrange("(rc p) d -> p rc d", p=128)
        pairs = []
        for i in range(2):
            pairs.append((u1b[:, 4 * i:4 * (i + 1), :], u1_src[:, 4 * i:4 * (i + 1), :]))
        for rc in range(2):
            for sseg in range(4):
                pairs.append((v1b[:, rc, sseg * 1024:(sseg + 1) * 1024],
                              v1_src[:, rc, sseg * 1024:(sseg + 1) * 1024]))
        for i in range(8):
            pairs.append((u2b[:, 4 * i:4 * (i + 1), :], u2_src[:, 4 * i:4 * (i + 1), :]))
        for rc in range(2):
            pairs.append((v2b[:, rc, :], v2_src[:, rc, :]))

        evac2 = {0: 0, 1: 0}
        x_own_src = x_own_t.rearrange("(dc p) t -> p dc t", p=128)
        xdd = {}

        def xo_dma(c, h):
            xdd[(c, h)] = b.dma(xown[:, 4 * h:4 * h + 4, :],
                                x_own_src[:, 4 * h:4 * h + 4, c * 512:(c + 1) * 512])
        xo_dma(0, 0)
        xo_dma(0, 1)
        b.wait("tensor", "S", attn_done["S"])
        for c in range(4):
            for dcol in range(DC):
                bk = dcol % 2
                if dcol % 4 == 0:
                    b.wait("vector", xdd[(c, dcol // 4)], 16)
                b.wait("tensor", "V", evac2[bk])
                b.wait("tensor", "V", attn_done["V"])
                for p in range(NP):
                    b.op("tensor", lambda e, o=bank(bk), l=wprojb[:, p, dcol * 128:(dcol + 1) * 128],
                         r=avpart[:, p, c * 512:(c + 1) * 512], st=(p == 0), sp=(p == NP - 1):
                         e.matmul(o, l, r, skip_group_check=True, start=st, stop=sp), inc="T")
                b.wait("vector", "T", tfence())
                evac2[bk] = b.op("vector", lambda e, o=resid[:, dcol, c * 512:(c + 1) * 512],
                                 x=xown[:, dcol, :], s=bank(bk): e.tensor_add(o, x, s), inc="V")
                if dcol in (3, 7) and c + 1 < 4:
                    b.wait("sync", "V", b.cnt["V"])
                    xo_dma(c + 1, dcol // 4)
        wffn_V = stream_ffn(pairs)
        p6a_done = b.barrier()

        dump_src.update(resid=resid)
        if upto == "p6a":
            return finish(nc, b, stack, dump_t, dump_src, b.barrier())

        rows6 = (row6_mean, row6_var, l2_rstd, l2_nm)
        b.wait("vector", "T", p6a_done["T"])
        for c in range(4):
            src = resid[:, :, c * 512:(c + 1) * 512]
            rows_c = (row6_mean, row6_var,
                      l2_rstd[0:1, c * 512:(c + 1) * 512].rearrange("p c -> p c") if False
                      else l2_rstd, l2_nm)
            emit_ln_stats_mixed(src, rows_c, c * 512, raw6, sq6)
        p6b_done = b.barrier()

        b.wait("tensor", "V", wffn_V)

        ydma = []
        lnx2b_ = [lnx2b, view(poolC, 47104, BF16, [128, DC, 512])]
        ln_done = [0, 0, 0, 0]
        tmp6_ = [tmp6] + [view(poolD, k, F32, [128, 512]) for k in (2048, 4096, 6144)]
        ln2_prev = {}

        def emit_ln2_normalize(c):
            nonlocal ln2_prev
            ccols = slice(c * 512, (c + 1) * 512)
            b.wait("tensor", "S", b.cnt["S"])
            b.wait("tensor", "V", tail_ybufV[0])
            b.op("tensor", lambda e, o=bank(6), l=onesrow[0:1, :],
                 r=l2_rstd[0:1, c * 512:(c + 1) * 512]:
                 e.matmul(o, l, r, skip_group_check=True, start=True, stop=True))
            b.op("tensor", lambda e, o=bank(7), l=onesrow[0:1, :],
                 r=l2_nm[0:1, c * 512:(c + 1) * 512]:
                 e.matmul(o, l, r, skip_group_check=True, start=True, stop=True), inc="T")
            b.wait("vector", "T", tfence())
            m6 = {}
            a6 = {}

            def n6mul(dc):
                b.wait("vector", "V",
                       a6[dc - 4] if dc >= 4 else ln2_prev.get(dc + 4, 0))
                m6[dc] = b.op("vector", lambda e, o=tmp6_[dc % 4], s=resid[:, dc, ccols],
                              r=bank(6): e.tensor_mul(o, s, r), inc="V")
            n6mul(0)
            n6mul(1)
            for dc in range(DC):
                if dc + 2 < DC:
                    n6mul(dc + 2)
                b.wait("vector", "V", m6[dc])
                a6[dc] = b.op("vector", lambda e, o=lnx2b_[c % 2][:, dc, :],
                              t=tmp6_[dc % 4], n=bank(7): e.tensor_add(o, t, n), inc="V")
            ln2_prev = a6
            ln_done[c] = b.cnt["V"]

        b.wait("vector", "V", p6b_done["V"])
        pending_tail = [None]
        prev_ycV = [0]
        tail_ybufV = [0]
        emit_ln2_normalize(0)
        for c in range(4):
            ccols = slice(c * 512, (c + 1) * 512)
            b.wait("tensor", "V", ln_done[c])
            b.wait("tensor", "V", prev_ycV[0])
            for rc2 in range(2):
                for dc in range(DC):
                    b.op("tensor", lambda e, o=bank(rc2), l=u1b[:, dc, rc2 * 128:(rc2 + 1) * 128],
                         r=lnx2b_[c % 2][:, dc, :], st=(dc == 0), sp=(dc == DC - 1):
                         e.matmul(o, l, r, skip_group_check=True, start=st, stop=sp), inc="T")
            b.wait("vector", "T", tfence())
            p1V = b.op("vector", lambda e: e.tensor_copy(p1bv[:, 0, :], bank(0)))
            p1V = b.op("vector", lambda e: e.tensor_copy(p1bv[:, 1, :], bank(1)), inc="V")
            if pending_tail[0] is not None:
                pending_tail[0]()
                pending_tail[0] = None
            b.wait("tensor", "V", p1V)
            gelu_S = [0, 0]
            q2_T = [0, 0]
            hfence = {}

            def emit_h(fc):
                hbk = 2 + fc % 2
                b.wait("tensor", "S", gelu_S[fc % 2])
                for rc2 in range(2):
                    b.op("tensor", lambda e, o=bank(hbk), l=v1b[:, rc2, fc * 128:(fc + 1) * 128],
                         r=p1bv[:, rc2, :], st=(rc2 == 0), sp=(rc2 == 1):
                         e.matmul(o, l, r, skip_group_check=True, start=st, stop=sp), inc="T")
                return tfence()

            hfence[0] = emit_h(0)
            hfence[1] = emit_h(1)
            if c + 1 < 4:
                emit_ln2_normalize(c + 1)
            for fc in range(32):
                b.wait("scalar", "T", hfence[fc])
                b.wait("scalar", "T", q2_T[fc % 2])
                gelu_S[fc % 2] = b.op("scalar", lambda e, o=hb2[:, fc % 2, :],
                                      i=bank(2 + fc % 2), bb=b1s[:, fc:fc + 1]:
                                      e.activation(o, i, AF.Gelu_apprx_tanh, bias=bb), inc="S")
                if fc + 2 < 32:
                    hfence[fc + 2] = emit_h(fc + 2)
                b.wait("tensor", "S", gelu_S[fc % 2])
                for rc2 in range(2):
                    b.op("tensor", lambda e, o=bank(4 + rc2),
                         l=u2b[:, fc, rc2 * 128:(rc2 + 1) * 128],
                         r=hb2[:, fc % 2, :], st=(fc == 0), sp=(fc == 31):
                         e.matmul(o, l, r, skip_group_check=True, start=st, stop=sp), inc="T")
                q2_T[fc % 2] = b.cnt["T"]
            b.wait("vector", "T", tfence())
            q2V = b.op("vector", lambda e: e.tensor_copy(q2bv[:, 0, :], bank(4)))
            q2V = b.op("vector", lambda e: e.tensor_copy(q2bv[:, 1, :], bank(5)), inc="V")
            b.wait("tensor", "V", q2V)
            oev = {0: 0, 1: 0}
            for dcol in range(DC):
                obk = dcol % 2
                b.wait("tensor", "V", oev[obk])
                for rc2 in range(2):
                    b.op("tensor", lambda e, o=bank(obk), l=v2b[:, rc2, dcol * 128:(dcol + 1) * 128],
                         r=q2bv[:, rc2, :], st=(rc2 == 0), sp=(rc2 == 1):
                         e.matmul(o, l, r, skip_group_check=True, start=st, stop=sp), inc="T")
                b.wait("vector", "T", tfence())
                oev[obk] = b.op("vector", lambda e, o=y_cv[:, dcol, :], s=bank(obk),
                                bb=b2s[:, dcol:dcol + 1], rr=resid[:, dcol, ccols]:
                                e.scalar_tensor_tensor(o, s, bb, rr, ALU.add, ALU.add), inc="V")
            ycV = b.cnt["V"]
            prev_ycV[0] = ycV

            def make_tail(c=c, ycV=ycV):
                def tail():
                    tpos_done = ycV
                    if ydma:
                        b.wait("vector", ydma[-1], 16)
                    for tt4 in range(4):
                        b.wait("tensor", "V", tpos_done)
                        for dcol in range(DC):
                            tb_ = 6 + dcol // 4
                            b.op("tensor", lambda e,
                                 o=bank(tb_, (dcol % 4) * 128, (dcol % 4 + 1) * 128),
                                 i=y_cv[:, dcol, tt4 * 128:(tt4 + 1) * 128]:
                                 e.transpose(o, i, identf[:, :]), inc="T")
                        b.wait("vector", "T", tfence())
                        b.op("vector", lambda e, o=ybuf[:, tt4, 0:512], s=bank(6):
                             e.tensor_copy(o, s))
                        tpos_done = b.op("vector", lambda e, o=ybuf[:, tt4, 512:1024],
                                         s=bank(7): e.tensor_copy(o, s), inc="V")
                    tail_ybufV[0] = tpos_done
                    b.wait("sync", "V", tpos_done)
                    ydst = y_out.rearrange("(c tt p) d -> c p tt d", p=128, tt=4)[c]
                    ydma.append(b.dma(ydst, ybuf[:, :, :]))
                return tail
            pending_tail[0] = make_tail()

        pending_tail[0]()
        return finish(nc, b, stack, dump_t, dump_src, b.barrier())


def finish(nc, b, stack, dump_t, dump_src, counts):
    b.wait_all("sync", counts, sems=("T", "S", "V", "P"))
    for name, ap in dump_t.items():
        src = dump_src[name]
        flat = src
        nd = len(flat.shape)
        if nd == 3:
            flat = flat.rearrange("p a b -> p (a b)")
        elif nd == 4:
            flat = flat.rearrange("p a b c -> p (a b c)")
        b.dma(ap, flat)
    for i in range(b.dma_n):
        b.wait("sync", f"dma{i}", 16)
    block = stack.enter_context(nc.Block())
    b.replay(block)
    return nc


def _host_inputs(core, x, W_attn, W_proj, U1, V1, b1, U2, V2, b2):
    seq, cls = core // 2, core % 2
    xt = np.ascontiguousarray(x[seq].T)
    perm = np.arange(32)
    if cls == 1:
        perm = perm.reshape(16, 2)[:, ::-1].reshape(32)
    xt_p = np.ascontiguousarray(
        xt.reshape(D, 32, 128)[:, perm, :].reshape(D, T))
    own_tiles = perm[0::2]
    x_own = np.ascontiguousarray(
        xt.reshape(D, 32, 128)[:, own_tiles, :].reshape(D, OWNT))
    tril = np.tril(np.ones((128, 128), np.float32))
    trilT = tril.T
    m0 = trilT
    m1 = np.zeros((128, 128), np.float32) if cls == 0 else np.ones((128, 128), np.float32)
    masks = np.stack([np.stack([m0, m0]), np.stack([m1, m1])])
    masks = np.ascontiguousarray(masks.transpose(2, 0, 1, 3)).astype(ml_dtypes.bfloat16)
    return {
        "x_t": xt_p, "x_own_t": x_own,
        "W_attn": W_attn, "W_proj": W_proj, "U1": U1, "V1": V1,
        "U2": U2, "V2": V2,
        "b1l": np.ascontiguousarray(b1.reshape(DFF // 128, 128).T),
        "b2l": np.ascontiguousarray(b2.reshape(D // 128, 128).T),
        "masks": masks,
        "identb": np.eye(128, dtype=np.float32).astype(ml_dtypes.bfloat16),
        "identf": np.eye(128, dtype=np.float32),
        "onescol": np.ones((128, 1), np.float32).astype(ml_dtypes.bfloat16),
        "onesrow": np.ones((1, 128), np.float32),
    }, own_tiles


_NC_CACHE = {}


def kernel(L0_attn_x, L0_attn_W_attn, L0_attn_W_proj, L0_ffn_U1, L0_ffn_V1,
           L0_ffn_b1, L0_ffn_U2, L0_ffn_V2, L0_ffn_b2):
    x = np.asarray(L0_attn_x, dtype=np.float32)
    args = tuple(np.ascontiguousarray(np.asarray(a, dtype=np.float32)) for a in
                 (L0_attn_W_attn, L0_attn_W_proj, L0_ffn_U1, L0_ffn_V1,
                  L0_ffn_U2, L0_ffn_V2))
    b1 = np.asarray(L0_ffn_b1, np.float32)
    b2 = np.asarray(L0_ffn_b2, np.float32)
    if "nc" not in _NC_CACHE:
        _NC_CACHE["nc"] = build_program("full")
    nc = _NC_CACHE["nc"]
    in_maps = []
    owns = []
    for core in range(8):
        m, own_tiles = _host_inputs(core, x, args[0], args[1], args[2], args[3],
                                    b1, args[4], args[5], b2)
        in_maps.append(m)
        owns.append(own_tiles)
    res = run_bass_kernel_spmd(nc, in_maps, core_ids=list(range(8)))
    out = np.empty((4, T, D), np.float32)
    for core in range(8):
        seq = core // 2
        yc = res.results[core]["y"]
        rows = (owns[core][:, None] * 128 + np.arange(128)[None, :]).reshape(-1)
        out[seq, rows, :] = yc
    return out


# version 32
# speedup vs baseline: 1.0335x; 1.0130x over previous
import contextlib
import numpy as np
import ml_dtypes
import concourse.bass as bass
import concourse.mybir as mybir
from concourse.bass_utils import run_bass_kernel_spmd

F32 = mybir.dt.float32
BF16 = mybir.dt.bfloat16
U8 = mybir.dt.uint8
AF = mybir.ActivationFunctionType
ALU = mybir.AluOpType

D, T, HD, NP, DC, OWNT = 1024, 4096, 64, 8, 8, 2048
DFF, RANK = 4096, 256
SCALE_QK = 0.125
VSCALE = 1e-3
EPS = 1e-5


def prod(xs):
    r = 1
    for x in xs:
        r *= x
    return r


class Build:
    ENGINES = ("sync", "tensor", "scalar", "vector", "gpsimd")

    def __init__(self, nc):
        self.nc = nc
        self.streams = {e: [] for e in self.ENGINES}
        self.sem_handles = {}
        self.cnt = {"T": 0, "S": 0, "V": 0, "P": 0}
        self.last_wait = {e: {s: 0 for s in self.cnt} for e in self.ENGINES}

    N_DMA_SEMS = 96

    def init_sems(self, stack):
        for s in self.cnt:
            self.sem_handles[s] = stack.enter_context(self.nc.semaphore("sem_" + s))
        self.dma_sems = [stack.enter_context(self.nc.semaphore(f"dsem{i}"))
                         for i in range(self.N_DMA_SEMS)]
        self.dma_n = 0

    ENG_SEM = {"tensor": "T", "scalar": "S", "vector": "V", "gpsimd": "P"}

    def op(self, eng, fn, inc=None, ss=False):
        inc = self.ENG_SEM[eng]
        h = self.sem_handles[inc]
        selfw = 0
        if ss:
            v = self.cnt[inc]
            if v > 0 and self.last_wait[eng][inc] < v:
                selfw = v
                self.last_wait[eng][inc] = v
        self.cnt[inc] += 1

        def wrapped(e, fn=fn, h=h, w=selfw):
            i = fn(e)
            if w:
                i.wait_op(h, w, "sem-ge")
            i.then_inc(h, 1)
        self.streams[eng].append(wrapped)
        return self.cnt[inc]

    def _dma_on(self, queue, dst, src, slot=None):
        key = f"dma{self.dma_n}"
        self.dma_n += 1
        if not hasattr(self, "dma_slots"):
            self.dma_slots = {}
        if slot is None:
            slot = key
        if slot not in self.dma_slots:
            assert len(self.dma_slots) < self.N_DMA_SEMS, "DMA sem pool exhausted"
            self.dma_slots[slot] = [self.dma_sems[len(self.dma_slots)], 0]
        ent = self.dma_slots[slot]
        ent[1] += 16
        h = ent[0]
        self.sem_handles[key] = h
        self.cnt[key] = ent[1]
        for e_ in self.ENGINES:
            self.last_wait[e_][key] = 0
        self.streams[queue].append(
            lambda e, d=dst, s=src, h=h: e.dma_start(out=d, in_=s).then_inc(h, 16))
        return key

    def dma(self, dst, src, slot=None):
        return self._dma_on("sync", dst, src, slot)

    def dma2(self, dst, src, slot=None):
        return self._dma_on("scalar", dst, src, slot)

    def wait(self, eng, sem, val):
        if not val or val <= 0:
            return
        if sem.startswith("dma"):
            val = self.cnt[sem]
        assert val <= self.cnt[sem], f"{eng} waits future {sem}={val} (now {self.cnt[sem]})"
        if self.last_wait[eng][sem] >= val:
            return
        self.last_wait[eng][sem] = val
        h = self.sem_handles[sem]
        self.streams[eng].append(lambda e, h=h, v=val: e.wait_ge(h, v))

    def barrier(self):
        return dict(self.cnt)

    def wait_all(self, eng, counts, sems=("T", "S", "V")):
        for s in sems:
            self.wait(eng, s, counts[s])

    def replay(self, block):
        for name in self.ENGINES:
            ops = self.streams[name]

            def body(eng, ops=ops):
                for f in ops:
                    f(eng)
            body.__name__ = "_" + name
            getattr(block, name)(body)


def view(pool, byte_off, dtype, shape):
    esz = mybir.dt.size(dtype)
    assert byte_off % 32 == 0 and byte_off % esz == 0
    n = prod(shape[1:])
    ap = pool.bitcast(dtype).ap()[0:shape[0], byte_off // esz: byte_off // esz + n]
    if len(shape) > 2:
        names = [f"a{i}" for i in range(len(shape) - 1)]
        spec = "p (" + " ".join(names) + ") -> p " + " ".join(names)
        ap = ap.rearrange(spec, **{names[i]: shape[i + 1] for i in range(len(shape) - 1)})
    return ap


def build_program(upto="full", dumps=()):
    nc = bass.Bass("TRN2", target_bir_lowering=False)

    x_t = nc.dram_tensor("x_t", [D, T], F32, kind="ExternalInput").ap()
    x_own_t = nc.dram_tensor("x_own_t", [D, OWNT], F32, kind="ExternalInput").ap()
    W_attn = nc.dram_tensor("W_attn", [D, 3 * D], F32, kind="ExternalInput").ap()
    W_projd = nc.dram_tensor("W_proj", [D, D], F32, kind="ExternalInput").ap()
    U1d = nc.dram_tensor("U1", [D, RANK], F32, kind="ExternalInput").ap()
    V1d = nc.dram_tensor("V1", [RANK, DFF], F32, kind="ExternalInput").ap()
    U2d = nc.dram_tensor("U2", [DFF, RANK], F32, kind="ExternalInput").ap()
    V2d = nc.dram_tensor("V2", [RANK, D], F32, kind="ExternalInput").ap()
    b1d = nc.dram_tensor("b1l", [128, DFF // 128], F32, kind="ExternalInput").ap()
    b2d = nc.dram_tensor("b2l", [128, D // 128], F32, kind="ExternalInput").ap()
    masks_d = nc.dram_tensor("masks", [128, 2, 2, 128], BF16, kind="ExternalInput").ap()
    identb_d = nc.dram_tensor("identb", [128, 128], BF16, kind="ExternalInput").ap()
    identf_d = nc.dram_tensor("identf", [128, 128], F32, kind="ExternalInput").ap()
    onescol_d = nc.dram_tensor("onescol", [128, 1], BF16, kind="ExternalInput").ap()
    onesrow_d = nc.dram_tensor("onesrow", [1, 128], F32, kind="ExternalInput").ap()
    y_out = nc.dram_tensor("y", [OWNT, D], F32, kind="ExternalOutput").ap()
    kf2_d = nc.dram_tensor("kf2scratch", [128, NP, 2048], BF16, kind="Internal").ap()

    dump_specs = {
        "lnxb": (BF16, 0), "kf": (BF16, 1), "vt": (BF16, 2), "qf": (BF16, 3),
        "msnap": (BF16, 4), "prefsnap": (F32, 5), "avpart": (BF16, 6), "resid": (F32, 7),
        "wxb": (BF16, 8),
    }
    dump_t = {}

    with contextlib.ExitStack() as stack:
        E_ = stack.enter_context
        poolA = E_(nc.sbuf_tensor("poolA", [128, 65536], U8))
        poolB = E_(nc.sbuf_tensor("poolB", [128, 32768], U8))
        poolC = E_(nc.sbuf_tensor("poolC", [128, 65536], U8))
        poolD = E_(nc.sbuf_tensor("poolD", [128, 28672], U8))
        identb = E_(nc.sbuf_tensor("sb_identb", [128, 128], BF16))
        identf = E_(nc.sbuf_tensor("sb_identf", [128, 128], F32))
        onescol = E_(nc.sbuf_tensor("sb_onescol", [128, 1], BF16))
        onesrow = E_(nc.sbuf_tensor("sb_onesrow", [1, 128], F32))
        masksb = E_(nc.sbuf_tensor("sb_masksb", [128, 2, 2, 128], BF16))
        consts = E_(nc.sbuf_tensor("consts", [128, 3], F32))
        msnap = E_(nc.sbuf_tensor("msnap", [128, 8, 512], BF16))
        prefsnap = E_(nc.sbuf_tensor("prefsnap", [128, 17, 8], F32))
        b1s = E_(nc.sbuf_tensor("sb_b1s", [128, DFF // 128], F32))
        msum_t = E_(nc.sbuf_tensor("sb_msum", [128, 512], F32))
        prefsum_t = E_(nc.sbuf_tensor("sb_prefsum", [128, 8], F32))
        b2s = E_(nc.sbuf_tensor("sb_b2s", [128, D // 128], F32))
        psT = E_(nc.psum_tensor("ps", [128, 4096], F32))

        b = Build(nc)
        b.init_sems(stack)

        def vsync():
            b.wait("vector", "V", b.cnt["V"])

        def ssync():
            b.wait("scalar", "S", b.cnt["S"])

        def tfence():
            return b.cnt["T"]

        lnxh = [view(poolC, 0, BF16, [128, DC, 2048]),
                view(poolC, 32768, BF16, [128, DC, 2048])]
        kf = view(poolA, 0, BF16, [128, NP, 2048])
        vt = view(poolA, 32768, BF16, [128, 16, 1024])
        resid = view(poolA, 0, F32, [128, DC, OWNT])
        qf = view(poolB, 0, BF16, [128, NP, OWNT])
        v1b = view(poolB, 0, BF16, [128, 2, DFF])
        u2b = view(poolB, 16384, BF16, [128, 32, RANK])
        avpart = view(poolC, 0, BF16, [128, NP, OWNT])
        x2stage = view(poolC, 32768, F32, [128, DC, 512])
        tmp4 = view(poolC, 57344, F32, [128, 512])
        ln2blk = view(poolC, 49152, BF16, [128, DC, 512])
        xown = view(poolC, 32768, F32, [128, DC, 512])
        ybuf = view(poolC, 0, F32, [128, 4, D])
        y_cv = view(poolC, 16384, F32, [128, DC, 512])
        lnx2b = view(poolC, 32768, BF16, [128, DC, 512])
        p1bv = view(poolC, 40960, BF16, [128, 2, 512])
        hb2 = view(poolC, 43008, BF16, [128, 2, 512])
        q2bv = view(poolC, 45056, BF16, [128, 2, 512])
        xstage2_ = [view(poolD, 0, F32, [128, DC, 512]),
                    view(poolA, 0, F32, [128, DC, 512])]
        rawb = view(poolD, 16384, BF16, [128, 512])
        sqb = view(poolD, 17408, BF16, [128, 512])
        rows0 = [view(poolD, 18432 + 2048 * i, F32, [1, 512]) for i in range(4)]
        r_rows = view(poolB, 0, F32, [1, T])
        nm_rows = view(poolB, 16384, F32, [1, T])
        tmp32 = view(poolD, 26624, F32, [128, 512])
        wstage = view(poolD, 0, F32, [128, 2, 1024])
        wxb = view(poolD, 8192, BF16, [128, DC, 1024])
        eb = view(poolD, 0, BF16, [128, 6, 512])
        gb = view(poolD, 6144, BF16, [128, 6, 512])
        ab = view(poolD, 12288, BF16, [128, 6, 512])
        zb = view(poolD, 18432, BF16, [128, 3, 2, 128])
        zmb = view(poolD, 12288, BF16, [128, 3, 2, 128])
        abh = view(poolD, 13824, BF16, [128, 2, 128])
        K_ILV = False
        kstage4 = view(poolD, 20480, F32, [128, 8, 128])
        wkvh = [view(poolD, 14336, BF16, [128, 8, 128]),
                view(poolD, 16384, BF16, [128, 8, 128])]
        kevac2 = [view(poolD, 24576, BF16, [128, 512]),
                  view(poolD, 25600, BF16, [128, 512])]
        ab2 = [view(poolD, 14336, BF16, [128, 2, 512]),
               view(poolD, 16384, BF16, [128, 2, 512])]
        ktb2_ = [view(poolD, 24576, BF16, [128, 1024]),
                 view(poolD, 26624, BF16, [128, 1024])]
        wkvb = view(poolD, 0, BF16, [128, DC, 1024])
        wstage4 = view(poolD, 16384, F32, [128, 512])
        raw4 = view(poolD, 18432, BF16, [128, 512])
        sq4 = view(poolD, 19456, BF16, [128, 512])
        rows4 = [view(poolD, 20480 + 2048 * i, F32, [1, 512]) for i in range(4)]
        wprojb = view(poolD, 0, BF16, [128, NP, 1024])
        wstage6 = view(poolD, 16384, F32, [128, 1024])
        l2_rstd = view(poolD, 20480, F32, [1, 2048])
        l2_nm = view(poolD, 8192, F32, [1, 2048])
        row6_mean = view(poolD, 2048, F32, [1, 512])
        row6_var = view(poolD, 4096, F32, [1, 512])
        raw6 = view(poolD, 0, BF16, [128, 512])
        sq6 = view(poolD, 1024, BF16, [128, 512])
        tmp6 = view(poolD, 0, F32, [128, 512])
        u1b = view(poolC, 55296, BF16, [128, DC, RANK])
        v2b = view(poolC, 59392, BF16, [128, 2, D])

        msum_sb = msum_t.ap()
        prefsum_sb = prefsum_t.ap()
        ps = psT.ap()
        psbf = psT.bitcast(BF16).ap()

        def bank(i, lo=0, hi=512):
            return ps[:, i * 512 + lo: i * 512 + hi]
        ps8 = ps.rearrange("p (k c) -> p k c", k=8)
        tp_ps = psbf[:, 0:1024]

        for name in dumps:
            dt, _ = dump_specs[name]
            shp = {"lnxb": DC * T, "kf": NP * 2048, "vt": 16 * 1024, "qf": NP * OWNT,
                   "msnap": 8 * 512, "prefsnap": 17 * 8, "avpart": NP * OWNT,
                   "resid": DC * OWNT, "wxb": DC * 1024}[name]
            dump_t[name] = nc.dram_tensor("dump_" + name, [128, shp], dt,
                                          kind="ExternalOutput").ap()
        dump_src = {}

        d_misc_keys = []
        for dst, src in ((identb[:, :], identb_d[:, :]), (identf[:, :], identf_d[:, :]),
                         (onescol[:, :], onescol_d[:, :]), (onesrow[:, :], onesrow_d[:, :]),
                         (masksb[:, :, :, :], masks_d[:, :, :, :]),
                         (b1s[:, :], b1d[:, :]), (b2s[:, :], b2d[:, :])):
            d_misc_keys.append(b.dma(dst, src))

        def wait_misc(eng):
            for k in d_misc_keys:
                b.wait(eng, k, 16)
        b.op("vector", lambda e: e.memset(consts[:, 0:1], EPS))
        b.op("vector", lambda e: e.memset(consts[:, 1:2], 1.0))
        cV = b.op("vector", lambda e: e.memset(consts[:, 2:3], 1.0), inc="V")
        onescol_f = consts[:, 2:3]
        eps_ap = consts[:, 0:1]

        def emit_ln_stats(src_f32, rows, co, raw, sq, src_ready_d):
            rm = rows[0][0:1, co:co + 512]
            rv = rows[1][0:1, co:co + 512]
            rr = rows[2][0:1, co:co + 512]
            rn = rows[3][0:1, co:co + 512]
            b.wait("vector", "d", src_ready_d)
            b.wait("vector", "V", cV)
            b.wait("vector", "T", b.cnt["T"])
            raw2 = [raw, view(poolD, 6144, BF16, [128, 512])]
            sq2 = [sq, view(poolD, 7168, BF16, [128, 512])]
            t_done = {}
            for dc in range(DC):
                if dc >= 2:
                    b.wait("vector", "T", t_done[dc - 2])
                b.op("vector", lambda e, a=raw2[dc % 2], s=src_f32[:, dc, :]:
                     e.tensor_copy(a, s))
                b.op("vector", lambda e, a=sq2[dc % 2], r=raw2[dc % 2]:
                     e.tensor_mul(a, r, r), inc="V", ss=True)
                b.wait("tensor", "V", b.cnt["V"])
                b.op("tensor", lambda e, o=bank(0)[0:1, :], l=onescol[:, :], r=raw2[dc % 2],
                     st=(dc == 0), sp=(dc == DC - 1): e.matmul(o, l, r, skip_group_check=True, start=st, stop=sp))
                b.op("tensor", lambda e, o=bank(1)[0:1, :], l=onescol[:, :], r=sq2[dc % 2],
                     st=(dc == 0), sp=(dc == DC - 1): e.matmul(o, l, r, skip_group_check=True, start=st, stop=sp),
                     inc="T")
                t_done[dc] = b.cnt["T"]
            b.wait("scalar", "T", tfence())
            b.op("scalar", lambda e, o=rm, i=bank(0)[0:1, :]: e.mul(o, i, 1.0 / D))
            b.op("scalar", lambda e, o=rv, i=bank(1)[0:1, :]: e.mul(o, i, 1.0 / D),
                 inc="S")
            b.wait("vector", "S", b.cnt["S"])
            b.op("vector", lambda e, o=rn, m=rm: e.tensor_mul(o, m, m))
            b.op("vector", lambda e, o=rv, v=rv,
                 m=rn: e.tensor_sub(o, v, m), inc="V", ss=True)
            b.wait("scalar", "V", b.cnt["V"])
            b.op("scalar", lambda e, o=rr, i=rv:
                 e.activation(o, i, AF.Ln, bias=eps_ap[0:1, :]))
            b.op("scalar", lambda e, o=rr, i=rr:
                 e.activation(o, i, AF.Exp, scale=-0.5), inc="S", ss=True)
            b.wait("vector", "S", b.cnt["S"])
            b.op("vector", lambda e, o=rn, m=rm,
                 r=rr: e.scalar_tensor_tensor(o, m, -1.0, r, ALU.mult, ALU.mult),
                 inc="V")

        def emit_ln_normalize(src_f32, dst_slices, rows, co, tmpn):
            rr = rows[2][0:1, co:co + 512]
            rn = rows[3][0:1, co:co + 512]
            b.wait("tensor", "V", b.cnt["V"])
            b.op("tensor", lambda e, o=bank(2), l=onesrow[0:1, :], r=rr:
                 e.matmul(o, l, r, skip_group_check=True, start=True, stop=True))
            b.op("tensor", lambda e, o=bank(3), l=onesrow[0:1, :], r=rn:
                 e.matmul(o, l, r, skip_group_check=True, start=True, stop=True), inc="T")
            b.wait("vector", "T", tfence())
            for dc in range(DC):
                b.op("vector", lambda e, o=tmpn, s=src_f32[:, dc, :], r=bank(2):
                     e.tensor_mul(o, s, r), ss=True)
                b.op("vector", lambda e, o=dst_slices[dc], t=tmpn, n=bank(3):
                     e.tensor_add(o, t, n), inc="V", ss=True)
            return b.cnt["V"]

        def emit_ln_stats_mixed(src_f32, rows, co_rs, raw, sq):
            rm = rows[0][0:1, 0:512]
            rv = rows[1][0:1, 0:512]
            rr = rows[2][0:1, co_rs:co_rs + 512]
            rn = rows[3][0:1, co_rs:co_rs + 512]
            b.wait("vector", "V", cV)
            t_entry = b.cnt["T"]
            raw2_ = [raw, view(poolD, 6144, BF16, [128, 512])]
            sq2_ = [sq, view(poolD, 7168, BF16, [128, 512])]
            t_done6 = {}
            c6 = {}

            def p6cpy(dc):
                b.wait("vector", "T", t_done6[dc - 2] if dc >= 2 else t_entry)
                c6[dc] = b.op("vector", lambda e, a=raw2_[dc % 2], s=src_f32[:, dc, :]:
                              e.tensor_copy(a, s), inc="V")
            p6cpy(0)
            for dc in range(DC):
                if dc + 1 < DC:
                    p6cpy(dc + 1)
                b.wait("vector", "V", c6[dc])
                sq_v = b.op("vector", lambda e, a=sq2_[dc % 2], r=raw2_[dc % 2]:
                            e.tensor_mul(a, r, r), inc="V")
                b.wait("tensor", "V", sq_v)
                b.op("tensor", lambda e, o=bank(0)[0:1, :], l=onescol[:, :], r=raw2_[dc % 2],
                     st=(dc == 0), sp=(dc == DC - 1): e.matmul(o, l, r, skip_group_check=True, start=st, stop=sp))
                b.op("tensor", lambda e, o=bank(1)[0:1, :], l=onescol[:, :], r=sq2_[dc % 2],
                     st=(dc == 0), sp=(dc == DC - 1): e.matmul(o, l, r, skip_group_check=True, start=st, stop=sp),
                     inc="T")
                t_done6[dc] = b.cnt["T"]
            b.wait("scalar", "T", tfence())
            b.op("scalar", lambda e, o=rm, i=bank(0)[0:1, :]: e.mul(o, i, 1.0 / D))
            b.op("scalar", lambda e, o=rv, i=bank(1)[0:1, :]: e.mul(o, i, 1.0 / D), inc="S")
            b.wait("vector", "S", b.cnt["S"])
            b.op("vector", lambda e, o=rn, m=rm: e.tensor_mul(o, m, m))
            b.op("vector", lambda e, o=rv, v=rv, m=rn: e.tensor_sub(o, v, m), inc="V",
                 ss=True)
            b.wait("scalar", "V", b.cnt["V"])
            b.op("scalar", lambda e, o=rr, i=rv:
                 e.activation(o, i, AF.Ln, bias=eps_ap[0:1, :]))
            b.op("scalar", lambda e, o=rr, i=rr:
                 e.activation(o, i, AF.Exp, scale=-0.5), inc="S", ss=True)
            b.wait("vector", "S", b.cnt["S"])
            b.op("vector", lambda e, o=rn, m=rm, r=rr:
                 e.scalar_tensor_tensor(o, m, -1.0, r, ALU.mult, ALU.mult), inc="V")

        x_src = x_t.rearrange("(dc p) t -> p dc t", p=128)
        wait_misc("tensor")
        wait_misc("vector")
        wait_misc("scalar")
        wait_misc("gpsimd")
        relVs = [0, 0]
        relTs = [0, 0]
        for tb in range(8):
            xs = xstage2_[tb % 2]
            b.wait("sync", "V", relVs[tb % 2])
            b.wait("sync", "T", relTs[tb % 2])
            dd = b.dma(xs[:, :, :], x_src[:, :, tb * 512:(tb + 1) * 512])
            co = tb * 512
            rm = rows0[0][0:1, 0:512]
            rv = rows0[1][0:1, 0:512]
            rr = r_rows[0:1, co:co + 512]
            rn = nm_rows[0:1, co:co + 512]
            b.wait("vector", dd, 16)
            b.wait("vector", "V", cV)
            b.wait("vector", "T", b.cnt["T"])
            rawx = [rawb, view(poolD, 26624, BF16, [128, 512]),
                    view(poolD, 22528, BF16, [128, 512]),
                    view(poolD, 24576, BF16, [128, 512])]
            sqx = [sqb, view(poolD, 27648, BF16, [128, 512]),
                   view(poolD, 23552, BF16, [128, 512]),
                   view(poolD, 25600, BF16, [128, 512])]
            t_done = {}
            ccnt = {}

            def p1cpy(dc):
                if dc >= 4:
                    b.wait("vector", "T", t_done[dc - 4])
                ccnt[dc] = b.op("vector", lambda e, a=rawx[dc % 4], x_=xs[:, dc, :]:
                                e.tensor_copy(a, x_), inc="V")
            p1cpy(0)
            p1cpy(1)
            for dc in range(DC):
                if dc + 2 < DC:
                    p1cpy(dc + 2)
                b.wait("vector", "V", ccnt[dc])
                sq_v = b.op("vector", lambda e, a=sqx[dc % 4], r=rawx[dc % 4]:
                            e.tensor_mul(a, r, r), inc="V")
                b.wait("tensor", "V", sq_v)
                b.op("tensor", lambda e, o=bank(0)[0:1, :], l=onescol[:, :], r=rawx[dc % 4],
                     st=(dc == 0), sp=(dc == DC - 1): e.matmul(o, l, r, skip_group_check=True, start=st, stop=sp))
                b.op("tensor", lambda e, o=bank(1)[0:1, :], l=onescol[:, :], r=sqx[dc % 4],
                     st=(dc == 0), sp=(dc == DC - 1): e.matmul(o, l, r, skip_group_check=True, start=st, stop=sp),
                     inc="T")
                t_done[dc] = b.cnt["T"]
            relVs[tb % 2] = b.cnt["V"]
            relTs[tb % 2] = b.cnt["T"]
            b.wait("scalar", "T", tfence())
            b.op("scalar", lambda e, o=rm, i=bank(0)[0:1, :]: e.mul(o, i, 1.0 / D))
            b.op("scalar", lambda e, o=rv, i=bank(1)[0:1, :]: e.mul(o, i, 1.0 / D), inc="S")
            b.wait("vector", "S", b.cnt["S"])
            b.op("vector", lambda e, o=rn, m=rm: e.tensor_mul(o, m, m))
            b.op("vector", lambda e, o=rv, v=rv, m=rn: e.tensor_sub(o, v, m), inc="V",
                 ss=True)
            b.wait("scalar", "V", b.cnt["V"])
            b.op("scalar", lambda e, o=rr, i=rv:
                 e.activation(o, i, AF.Ln, bias=eps_ap[0:1, :]))
            b.op("scalar", lambda e, o=rr, i=rr:
                 e.activation(o, i, AF.Exp, scale=-0.5), inc="S", ss=True)
            b.wait("vector", "S", b.cnt["S"])
            b.op("vector", lambda e, o=rn, m=rm, r=rr:
                 e.scalar_tensor_tensor(o, m, -1.0, r, ALU.mult, ALU.mult), inc="V")
        relVs = [b.cnt["V"], b.cnt["V"]]
        bc_free = 0
        p2_prev = {}
        for tb in range(8):
            xs = xstage2_[tb % 2]
            b.wait("sync", "V", relVs[tb % 2])
            dd = b.dma(xs[:, :, :], x_src[:, :, tb * 512:(tb + 1) * 512])
            co = tb * 512
            bka, bkb = (2, 3) if tb % 2 == 0 else (4, 5)
            b.wait("tensor", "V", bc_free)
            b.op("tensor", lambda e, o=bank(bka), l=onesrow[0:1, :], r=r_rows[0:1, co:co + 512]:
                 e.matmul(o, l, r, skip_group_check=True, start=True, stop=True))
            b.op("tensor", lambda e, o=bank(bkb), l=onesrow[0:1, :], r=nm_rows[0:1, co:co + 512]:
                 e.matmul(o, l, r, skip_group_check=True, start=True, stop=True), inc="T")
            tb_T = tfence()
            b.wait("vector", "T", tb_T)
            b.wait("vector", dd, 16)
            tmp2 = [tmp32, view(poolD, 16384, F32, [128, 512]),
                    view(poolD, 22528, F32, [128, 512]),
                    view(poolD, 24576, F32, [128, 512])]
            mcnt2 = {}
            acnt2 = {}

            def p2mul(dc):
                b.wait("vector", "V",
                       acnt2[dc - 4] if dc >= 4 else p2_prev.get(dc + 4, 0))
                mcnt2[dc] = b.op("vector", lambda e, o=tmp2[dc % 4], s=xs[:, dc, :],
                                 r=bank(bka): e.tensor_mul(o, s, r), inc="V")
            p2mul(0)
            p2mul(1)
            for dc in range(DC):
                if dc + 2 < DC:
                    p2mul(dc + 2)
                b.wait("vector", "V", mcnt2[dc])
                acnt2[dc] = b.op("vector", lambda e,
                                 o=lnxh[tb // 4][:, dc, (tb % 4) * 512:(tb % 4 + 1) * 512],
                                 t=tmp2[dc % 4], n=bank(bkb): e.tensor_add(o, t, n),
                                 inc="V")
            p2_prev = acnt2
            relVs[tb % 2] = b.cnt["V"]
            if tb % 2 == 1:
                bc_free = b.cnt["V"]
        p0_done = b.barrier()

        s_rel = [0, 0, 0]
        z_rel = [0, 0, 0]
        e_rel = [0, 0, 0]
        e_relP = [0, 0, 0]
        ab_rel = [0, 0, 0]
        zw_rel = [0, 0, 0]
        g_rel = [0, 0, 0]
        SBK = (0, 2, 6)
        av_free = {4: 0, 5: 0}
        av_freeS = {4: 0, 5: 0}
        sweep_state = {"tpV0": 0, "tpV1": 0, "snapV": 0, "mprefT": 0}

        def emit_sweep_kt(ktg, half, vt_ready_V=0):
            if True:
                ktloc = ktg - 16 * half
                b.wait("tensor", "V", vt_ready_V)
                par2 = ktloc % 2
                tpx = psbf[:, par2 * 1024:(par2 + 1) * 1024]
                ktb = ktb2_[par2]
                b.wait("tensor", "V", sweep_state["tpV" + str(par2)])
                b.wait("tensor", "V", sweep_state["snapV"])
                if K_ILV and half == 1:
                    b.wait("tensor", kfload[ktloc // 4], 16)
                for p in range(NP):
                    b.op("tensor", lambda e, o=tpx[:, p * 128:(p + 1) * 128],
                         i=kf[:, p, ktloc * 128:(ktloc + 1) * 128]:
                         e.transpose(o, i, identb[:, :]), inc="T")
                b.wait("vector", "T", tfence())
                sweep_state["tpV" + str(par2)] = b.op(
                    "vector", lambda e, k=ktb, t=tpx: e.tensor_scalar_mul(k[:, :], t, -0.01),
                    inc="V")
                b.wait("tensor", "V", sweep_state["tpV" + str(par2)])
                for p in range(NP):
                    for par in range(2):
                        b.op("tensor", lambda e,
                             o=ps[par * 64:(par + 1) * 64, 6 * 512 + p * 64: 6 * 512 + (p + 1) * 64],
                             l=ktb[:, p * 128 + par * 64: p * 128 + (par + 1) * 64],
                             r=vt[:, ktloc, (2 * p + par) * 64:(2 * p + par + 1) * 64],
                             st=(ktg % 16 == 0 and p == 0), tp=(0, par * 64):
                             e.matmul(o, l, r, skip_group_check=True, start=st, stop=True, tile_position=tp), inc="T")
                for dc in range(DC):
                    b.op("tensor", lambda e, o=bank(7, 2 * dc, 2 * dc + 1),
                         l=vt[:, ktloc, dc * 128:(dc + 1) * 128], r=onescol[:, :],
                         st=(ktg % 16 == 0 and dc == 0): e.matmul(o, l, r, skip_group_check=True, start=st, stop=True),
                         inc="T")
                sweep_state["mprefT"] = b.cnt["T"]
                if ktg % 2 == 1:
                    j = (ktg + 3) // 2
                    pref_src = bank(7, 0, 16).rearrange("p (a b) -> p a b", b=2)[:, :, 0]
                    if 2 <= j <= 16:
                        b.wait("vector", "T", tfence())
                        if half == 0:
                            b.op("vector", lambda e, o=msnap[:, j % 8, :], s=bank(6):
                                 e.tensor_copy(o, s))
                            sweep_state["snapV"] = b.op(
                                "vector", lambda e, o=prefsnap[:, j, :], s=pref_src:
                                e.tensor_copy(o, s), inc="V")
                        else:
                            b.op("vector", lambda e, o=msnap[:, j % 8, :], s=bank(6),
                                 m=msum_sb[:, :]: e.tensor_add(o, s, m))
                            sweep_state["snapV"] = b.op(
                                "vector", lambda e, o=prefsnap[:, j, :], s=pref_src,
                                m=prefsum_sb[:, :]: e.tensor_add(o, s, m), inc="V")
                    if ktg == 15:
                        b.wait("vector", "T", tfence())
                        b.op("vector", lambda e: e.tensor_copy(msum_sb[:, :], bank(6)))
                        sweep_state["snapV"] = b.op(
                            "vector", lambda e, o=prefsum_sb[:, :], s=pref_src:
                            e.tensor_copy(o, s), inc="V")

        b.wait("sync", "V", p0_done["V"])
        b.wait("vector", "T", p0_done["T"])
        b.wait("vector", "S", p0_done["S"])

        w_src2 = W_attn.rearrange("(a p) c -> p a c", p=128)

        def stream_w(colbase, dst):
            b.wait("vector", "T", b.cnt["T"])
            casts = []
            for dp in range(4):
                b.wait("sync", "V", casts[-1] if casts else b.cnt["V"])
                dd = b.dma(wstage[:, :, :],
                           w_src2[:, 2 * dp:2 * dp + 2, colbase:colbase + 1024])
                b.wait("vector", dd, 16)
                for k in range(2):
                    casts.append(b.op("vector", lambda e, o=dst[:, 2 * dp + k, :],
                                      s=wstage[:, k, :]: e.tensor_copy(o, s), inc="V"))
            return b.cnt["V"]

        evacV = {4: 0, 5: 0}

        def proj_loop(tiles, emit_mms, emit_evac):
            for i, tile in enumerate(tiles):
                bk = 4 + i % 2
                b.wait("tensor", "V", evacV[bk])
                emit_mms(bk, tile)
                b.wait("vector", "T", tfence())
                evacV[bk] = emit_evac(bk, tile)

        wV = stream_w(D, wxb)
        b.wait("tensor", "V", wV)

        def k_mms(bk, tile):
            hc, tb2 = tile
            for dc in range(DC):
                b.op("tensor", lambda e, o=bank(bk), l=wxb[:, dc, hc * 128:(hc + 1) * 128],
                     r=lnxh[0][:, dc, tb2 * 512:(tb2 + 1) * 512], st=(dc == 0),
                     sp=(dc == DC - 1): e.matmul(o, l, r, skip_group_check=True, start=st, stop=sp), inc="T")

        def k_evac(bk, tile):
            hc, tb2 = tile
            return b.op("vector", lambda e, o=kf[:, hc, tb2 * 512:(tb2 + 1) * 512],
                        s=bank(bk): e.tensor_copy(o, s), inc="V")
        proj_loop([(hc, tb2) for hc in range(NP) for tb2 in range(4)], k_mms, k_evac)

        if upto == "k":
            dump_src.update(kf=kf, wxb=wxb)
            return finish(nc, b, stack, dump_t, dump_src, b.barrier())

        wV = stream_w(2 * D, wxb)
        b.wait("tensor", "V", wV)

        def v_mms(bk, tile):
            tt, vc2 = tile
            for dc in range(DC):
                b.op("tensor", lambda e, o=bank(bk), l=lnxh[0][:, dc, tt * 128:(tt + 1) * 128],
                     r=wxb[:, dc, vc2 * 512:(vc2 + 1) * 512], st=(dc == 0),
                     sp=(dc == DC - 1): e.matmul(o, l, r, skip_group_check=True, start=st, stop=sp), inc="T")

        def v_evac(bk, tile):
            tt, vc2 = tile
            return b.op("vector", lambda e, o=vt[:, tt, vc2 * 512:(vc2 + 1) * 512],
                        s=bank(bk): e.tensor_scalar_mul(o, s, VSCALE), inc="V")
        proj_loop([(tt, vc2) for tt in range(16) for vc2 in range(2)], v_mms, v_evac)

        if upto == "kv":
            dump_src.update(kf=kf, vt=vt)
            return finish(nc, b, stack, dump_t, dump_src, b.barrier())

        wV = stream_w(0, wxb)
        b.wait("tensor", "V", wV)
        lnx_gh = [h.rearrange("p dc (g c) -> p dc g c", c=128) for h in lnxh]

        def q_mms(bk, tile):
            hc, qb4 = tile
            for dc in range(DC):
                rhs = lnx_gh[qb4 // 2][:, dc, 0::2, :][:, (qb4 % 2) * 4:(qb4 % 2 + 1) * 4, :]
                b.op("tensor", lambda e, o=bank(bk), l=wxb[:, dc, hc * 128:(hc + 1) * 128],
                     r=rhs, st=(dc == 0), sp=(dc == DC - 1):
                     e.matmul(o, l, r, skip_group_check=True, start=st, stop=sp), inc="T")

        def q_evac(bk, tile):
            hc, qb4 = tile
            return b.op("vector", lambda e, o=qf[:, hc, qb4 * 512:(qb4 + 1) * 512],
                        s=bank(bk): e.tensor_scalar_mul(o, s, SCALE_QK), inc="V")

        q_tiles = [(hc, qb4) for hc in range(NP) for qb4 in range(4)]
        for i, tile in enumerate(q_tiles):
            bk = 4 + i % 2
            b.wait("tensor", "V", evacV[bk])
            q_mms(bk, tile)
            qT = tfence()
            if i < 16:
                emit_sweep_kt(i, 0)
            b.wait("vector", "T", qT)
            evacV[bk] = q_evac(bk, tile)
        p1_done = b.barrier()

        dump_src.update(lnxb=lnxh[0], kf=kf, vt=vt, qf=qf)
        if upto == "p1":
            return finish(nc, b, stack, dump_t, dump_src, b.barrier())

        def kts_of(c, half):
            lo, hi = 16 * half, 16 * half + 16
            return list(range(lo, min(8 * c + 8, hi)))

        kblk = [0]
        kcast = {}
        kwT = [0, 0]
        ke_dma = {0: 0, 1: 0}
        kfload = []

        def emit_attention(half):
            b.wait("tensor", "V", b.cnt["V"])
            if K_ILV and half == 0:
                b.wait("sync", "T", p1_done["T"])
            if K_ILV and half == 1:
                for k in kfload:
                    b.wait("tensor", k, 16)
            for c in range(4):
                kts = kts_of(c, half)
                if not kts:
                    continue
                for p in range(NP):
                    avbk = 4 + p % 2
                    av_done = 0
                    nk = len(kts)
                    hc, tb2 = kblk[0] // 4, kblk[0] % 4
                    idle_bk = 4 + (1 - p % 2)
                    if K_ILV and half == 0 and tb2 == 0:
                        b.wait("sync", "V", kcast[hc - 1] if hc > 0 else p1_done["V"])
                        kdd = b.dma(kstage4[:, :, :],
                                    w_src2[:, :, D + hc * 128: D + (hc + 1) * 128],
                                    slot="wk")
                        b.wait("vector", kdd, 16)
                        b.wait("vector", "T", kwT[hc % 2])
                        kcast[hc] = b.op("vector", lambda e, o=wkvh[hc % 2][:, :, :],
                                         s=kstage4[:, :, :]: e.tensor_copy(o, s),
                                         inc="V")

                    def emit_qk(idx):
                        ktg = kts[idx]
                        ktloc = ktg - 16 * half
                        sb = idx % 3
                        jm = ktg // 2 + 1
                        j0 = max(4 * c + 1, jm)
                        coloff = (j0 - (4 * c + 1)) * 128
                        N = 512 - coloff
                        qcols = qf[:, p, c * 512 + coloff: (c + 1) * 512]
                        b.wait("tensor", "S", s_rel[sb])
                        b.wait("tensor", "V", z_rel[sb])
                        for hh in range(2):
                            b.op("tensor", lambda e,
                                 o=bank(SBK[sb] + hh, 0, N),
                                 l=kf[hh * 64:(hh + 1) * 64, p, ktloc * 128:(ktloc + 1) * 128],
                                 r=qcols[hh * 64:(hh + 1) * 64, :], tp=(hh * 64, 0):
                                 e.matmul(o, l, r, skip_group_check=True, start=True, stop=True, tile_position=tp),
                                 inc="T")
                        return b.cnt["T"]

                    def is_m(idx):
                        return 8 * c <= kts[idx] <= 8 * c + 7

                    gcnt, mcnt, pcnt = {}, {}, {}

                    def emit_eg(idx, qkT):
                        ktg = kts[idx]
                        sb = idx % 3
                        masked = is_m(idx)
                        j0 = max(4 * c + 1, ktg // 2 + 1)
                        coloff = (j0 - (4 * c + 1)) * 128
                        N = 512 - coloff
                        b.wait("scalar", "T", qkT)
                        b.wait("scalar", "V", e_rel[sb])
                        b.wait("scalar", "P", e_relP[sb])
                        b.op("scalar", lambda e, o=eb[:, 2 * sb:2 * sb + 2, 0:N],
                             i=ps8[:, SBK[sb]:SBK[sb] + 2, 0:N]:
                             e.activation(o, i, AF.Relu), inc="S")
                        if masked:
                            b.wait("scalar", "V", zw_rel[sb])
                            b.op("scalar", lambda e, o=zb[:, sb, :, :],
                                 i=ps8[:, SBK[sb]:SBK[sb] + 2, 0:128]:
                                 e.activation(o, i, AF.Identity, scale=-0.01,
                                              bias=consts[:, 1:2]), inc="S")
                        s_rel[sb] = b.cnt["S"]
                        b.wait("vector", "S", s_rel[sb])
                        b.wait("vector", "T", g_rel[sb])
                        gcnt[idx] = b.op("vector",
                                         lambda e, o=gb[:, 2 * sb:2 * sb + 2, 0:N],
                                         i=eb[:, 2 * sb:2 * sb + 2, 0:N]:
                                         e.scalar_tensor_tensor(o, i, 0.01, i,
                                                                ALU.add, ALU.mult),
                                         inc="V")
                        pcnt[idx] = 0
                        if masked:
                            midx = ktg % 2
                            zw_rel[sb] = b.op("vector", lambda e, o=zmb[:, sb, :, :],
                                              z=zb[:, sb, :, :], m=masksb[:, midx, :, :]:
                                              e.tensor_mul(o, z, m), inc="V")
                            b.wait("vector", "V", gcnt[idx])
                            mcnt[idx] = b.op("vector",
                                             lambda e, o=gb[:, 2 * sb:2 * sb + 2, 0:128],
                                             m=masksb[:, midx, :, :]:
                                             e.tensor_mul(o, o, m), inc="V")
                        e_rel[sb] = b.cnt["V"]

                    def emit_av(idx):
                        ktg = kts[idx]
                        ktloc = ktg - 16 * half
                        sb = idx % 3
                        jm = ktg // 2 + 1
                        j0 = max(4 * c + 1, jm)
                        coloff = (j0 - (4 * c + 1)) * 128
                        N = 512 - coloff
                        b.wait("tensor", "V", mcnt[idx] if is_m(idx) else gcnt[idx])
                        b.wait("tensor", "P", pcnt[idx])
                        if idx == 0:
                            b.wait("tensor", "V", av_free[avbk])
                            b.wait("tensor", "S", av_freeS[avbk])
                        for hh in range(2):
                            b.op("tensor", lambda e, o=ps[hh * 64:(hh + 1) * 64,
                                 avbk * 512 + coloff: (avbk + 1) * 512],
                                 l=vt[:, ktloc, (2 * p + hh) * 64:(2 * p + hh + 1) * 64],
                                 r=gb[:, 2 * sb + hh, 0:N], st=(idx == 0),
                                 tp=(0, hh * 64):
                                 e.matmul(o, l, r, skip_group_check=True, start=st, stop=True, tile_position=tp),
                                 inc="T")
                        if is_m(idx):
                            for hh in range(2):
                                b.op("tensor", lambda e, o=ps[hh * 64:(hh + 1) * 64,
                                     avbk * 512 + coloff: avbk * 512 + coloff + 128],
                                     l=vt[:, ktloc, (2 * p + hh) * 64:(2 * p + hh + 1) * 64],
                                     r=zmb[:, sb, hh, :], tp=(0, hh * 64):
                                     e.matmul(o, l, r, skip_group_check=True, start=False,
                                              stop=True, tile_position=tp), inc="T")
                        g_rel[sb] = b.cnt["T"]
                        if idx == 0:
                            for i4 in range(4):
                                j = 4 * c + 1 + i4
                                hj = 0 if j <= 9 else 1
                                if hj != half or j < 2:
                                    continue
                                for par in range(2):
                                    b.op("tensor", lambda e,
                                         o=ps[par * 64:(par + 1) * 64,
                                              avbk * 512 + i4 * 128: avbk * 512 + (i4 + 1) * 128],
                                         l=msnap[par * 64:(par + 1) * 64, j % 8,
                                                 p * 64:(p + 1) * 64],
                                         r=qf[par * 64:(par + 1) * 64, p,
                                              (c * 4 + i4) * 128:(c * 4 + i4 + 1) * 128],
                                         tp=(par * 64, par * 64):
                                         e.matmul(o, l, r, skip_group_check=True, start=False, stop=True,
                                                  tile_position=tp), inc="T")
                            g_rel[sb] = b.cnt["T"]

                    fences = {}
                    fences[0] = emit_qk(0)
                    if nk > 1:
                        fences[1] = emit_qk(1)
                    if nk > 2:
                        fences[2] = emit_qk(2)
                    for idx in range(nk):
                        emit_eg(idx, fences[idx])
                        if idx + 3 < nk:
                            fences[idx + 3] = emit_qk(idx + 3)
                        emit_av(idx)
                        if K_ILV and half == 0 and idx < 8:
                            b.wait("tensor", "V", kcast[hc])
                            if idx == 0:
                                b.wait("tensor", "V", av_free[idle_bk])
                                b.wait("tensor", "S", av_freeS[idle_bk])
                            b.op("tensor", lambda e, o=bank(idle_bk),
                                 l=wkvh[hc % 2][:, idx, :],
                                 r=lnxh[1][:, idx, tb2 * 512:(tb2 + 1) * 512],
                                 st=(idx == 0), sp=(idx == 7):
                                 e.matmul(o, l, r, skip_group_check=True, start=st,
                                          stop=sp), inc="T")
                    av_done = b.cnt["T"]
                    b.wait("vector", "T", tfence())
                    if K_ILV and half == 0:
                        if ke_dma[kblk[0] % 2]:
                            b.wait("vector", ke_dma[kblk[0] % 2], 16)
                        kv = b.op("vector", lambda e, o=kevac2[kblk[0] % 2][:, :],
                                  s=bank(idle_bk): e.tensor_copy(o, s), inc="V")
                        av_free[idle_bk] = kv
                        b.wait("sync", "V", kv)
                        ke_dma[kblk[0] % 2] = b.dma(
                            kf2_d[:, hc, tb2 * 512:(tb2 + 1) * 512],
                            kevac2[kblk[0] % 2][:, :], slot=f"ke{kblk[0] % 2}")
                        if tb2 == 3:
                            kwT[hc % 2] = b.cnt["T"]
                        kblk[0] += 1
                    chunk_final = (c <= 2 * half + 1)
                    for i4 in range(4):
                        j = 4 * c + 1 + i4
                        dst = avpart[:, p, (j - 1) * 128: j * 128]
                        src = ps[:, avbk * 512 + i4 * 128: avbk * 512 + (i4 + 1) * 128]
                        if not chunk_final:
                            b.op("vector", lambda e, o=dst, s=src: e.tensor_copy(o, s),
                                 inc="V")
                        elif c <= 1:
                            if j == 1:
                                b.op("vector", lambda e, o=dst, s=src: e.tensor_copy(o, s),
                                     inc="V")
                            else:
                                b.op("vector", lambda e, o=dst, s=src,
                                     pv=prefsnap[:, j, p:p + 1]:
                                     e.tensor_scalar(o, s, pv, None, ALU.add), inc="V")
                        else:
                            b.op("vector", lambda e, o=dst, s=src, pv=prefsnap[:, j, p:p + 1]:
                                 e.scalar_tensor_tensor(o, s, pv, o, ALU.add, ALU.add),
                                 inc="V")
                    av_free[avbk] = b.cnt["V"]
                    av_freeS[avbk] = b.cnt["S"]

        b.wait_all("tensor", p1_done)
        if upto == "sweep":
            dump_src.update(msnap=msnap.ap(), prefsnap=prefsnap.ap())
            return finish(nc, b, stack, dump_t, dump_src, b.barrier())
        emit_attention(0)
        h0_done = b.barrier()

        def p4_sweep(colbase, is_v):
            entry_T = b.cnt["T"]
            casts = []
            stages4 = [view(poolD, 16384, F32, [128, 1024]),
                       view(poolD, 20480, F32, [128, 1024])]
            for dc in range(DC):
                b.wait("sync", "V", casts[dc - 2] if dc >= 2 else b.cnt["V"])
                dd = b.dma(stages4[dc % 2][:, :],
                           W_attn[dc * 128:(dc + 1) * 128, colbase: colbase + 1024])
                b.wait("vector", dd, 16)
                b.wait("vector", "T", entry_T)
                casts.append(b.op("vector", lambda e, o=wkvb[:, dc, :],
                                  s=stages4[dc % 2][:, :]: e.tensor_copy(o, s), inc="V"))
            wready = b.cnt["V"]
            b.wait("tensor", "V", wready)
            if not is_v:
                for tb in range(4):
                    for hc in range(NP):
                        bk = 4 + hc % 2
                        b.wait("tensor", "V", evacV[bk])
                        for dc in range(DC):
                            b.op("tensor", lambda e, o=bank(bk),
                                 l=wkvb[:, dc, hc * 128:(hc + 1) * 128],
                                 r=lnxh[1][:, dc, tb * 512:(tb + 1) * 512],
                                 st=(dc == 0), sp=(dc == DC - 1):
                                 e.matmul(o, l, r, skip_group_check=True, start=st, stop=sp), inc="T")
                        b.wait("vector", "T", tfence())
                        evacV[bk] = b.op("vector", lambda e,
                                         o=kf[:, hc, tb * 512:(tb + 1) * 512], s=bank(bk):
                                         e.tensor_copy(o, s), inc="V")
            else:
                for tt in range(16):
                    for vc2 in range(2):
                        bk = 4 + (tt * 2 + vc2) % 2
                        b.wait("tensor", "V", evacV[bk])
                        for dc in range(DC):
                            b.op("tensor", lambda e, o=bank(bk),
                                 l=lnxh[1][:, dc, tt * 128:(tt + 1) * 128],
                                 r=wkvb[:, dc, vc2 * 512:(vc2 + 1) * 512],
                                 st=(dc == 0), sp=(dc == DC - 1):
                                 e.matmul(o, l, r, skip_group_check=True, start=st, stop=sp), inc="T")
                        b.wait("vector", "T", tfence())
                        evacV[bk] = b.op("vector", lambda e,
                                         o=vt[:, tt, vc2 * 512:(vc2 + 1) * 512],
                                         s=bank(bk): e.tensor_scalar_mul(o, s, VSCALE),
                                         inc="V")
                    emit_sweep_kt(16 + tt, 1, vt_ready_V=evacV[4 + (tt * 2 + 1) % 2])

        b.wait("sync", "T", h0_done["T"])
        b.wait("sync", "S", h0_done["S"])
        b.wait_all("vector", h0_done, sems=("T", "S"))
        if K_ILV:
            b.wait("sync", ke_dma[0], 16)
            b.wait("sync", ke_dma[1], 16)
            for i in range(4):
                kfload.append(b.dma(kf[:, :, 512 * i:512 * (i + 1)],
                                    kf2_d[:, :, 512 * i:512 * (i + 1)]))
            b.wait("vector", ke_dma[0], 16)
            b.wait("vector", ke_dma[1], 16)
        else:
            p4_sweep(D, is_v=False)
        p4_sweep(2 * D, is_v=True)
        p4_done = b.barrier()

        b.wait_all("tensor", p4_done)
        emit_attention(1)
        attn_done = b.barrier()

        dump_src.update(msnap=msnap.ap(), prefsnap=prefsnap.ap(), avpart=avpart)
        if upto == "attn":
            return finish(nc, b, stack, dump_t, dump_src, b.barrier())

        b.wait("sync", "T", attn_done["T"])
        b.wait("sync", "S", attn_done["S"])
        b.wait("sync", "V", attn_done["V"])
        casts = []
        for p in range(NP):
            if p >= 1:
                b.wait("sync", "V", casts[p - 1])
            dd = b.dma(wstage6[:, :], W_projd[p * 128:(p + 1) * 128, :])
            b.wait("vector", dd, 16)
            b.wait("vector", "T", attn_done["T"])
            casts.append(b.op("vector", lambda e, o=wprojb[:, p, :], s=wstage6[:, :]:
                              e.tensor_copy(o, s), inc="V"))
        wproj_V = b.cnt["V"]
        b.wait("tensor", "V", wproj_V)

        wstage6b = view(poolC, 49152, F32, [128, 1024])

        def stream_ffn(pairs):
            cc = [wproj_V, wproj_V]
            stages = [wstage6, wstage6b]
            for i, (dsl, ssl) in enumerate(pairs):
                n = prod(ssl.shape[1:])
                assert n <= 1024
                stage = stages[i % 2][:, 0:n]
                if len(ssl.shape) == 3:
                    stage = stage.rearrange("p (a c) -> p a c", a=ssl.shape[1])
                b.wait("scalar", "V", cc[-2])
                dd = b.dma2(stage, ssl)
                b.wait("vector", dd, 16)
                cc.append(b.op("vector", lambda e, o=dsl, s=stage: e.tensor_copy(o, s),
                               inc="V"))
            return b.cnt["V"]

        u1_src = U1d.rearrange("(dc p) r -> p dc r", p=128)
        v1_src = V1d.rearrange("(rc p) f -> p rc f", p=128)
        u2_src = U2d.rearrange("(fc p) r -> p fc r", p=128)
        v2_src = V2d.rearrange("(rc p) d -> p rc d", p=128)
        pairs = []
        for i in range(2):
            pairs.append((u1b[:, 4 * i:4 * (i + 1), :], u1_src[:, 4 * i:4 * (i + 1), :]))
        for rc in range(2):
            for sseg in range(4):
                pairs.append((v1b[:, rc, sseg * 1024:(sseg + 1) * 1024],
                              v1_src[:, rc, sseg * 1024:(sseg + 1) * 1024]))
        for i in range(8):
            pairs.append((u2b[:, 4 * i:4 * (i + 1), :], u2_src[:, 4 * i:4 * (i + 1), :]))
        for rc in range(2):
            pairs.append((v2b[:, rc, :], v2_src[:, rc, :]))

        evac2 = {0: 0, 1: 0}
        x_own_src = x_own_t.rearrange("(dc p) t -> p dc t", p=128)
        xdd = {}

        def xo_dma(c, h):
            xdd[(c, h)] = b.dma(xown[:, 4 * h:4 * h + 4, :],
                                x_own_src[:, 4 * h:4 * h + 4, c * 512:(c + 1) * 512])
        xo_dma(0, 0)
        xo_dma(0, 1)
        b.wait("tensor", "S", attn_done["S"])
        for c in range(4):
            for dcol in range(DC):
                bk = dcol % 2
                if dcol % 4 == 0:
                    b.wait("vector", xdd[(c, dcol // 4)], 16)
                b.wait("tensor", "V", evac2[bk])
                b.wait("tensor", "V", attn_done["V"])
                for p in range(NP):
                    b.op("tensor", lambda e, o=bank(bk), l=wprojb[:, p, dcol * 128:(dcol + 1) * 128],
                         r=avpart[:, p, c * 512:(c + 1) * 512], st=(p == 0), sp=(p == NP - 1):
                         e.matmul(o, l, r, skip_group_check=True, start=st, stop=sp), inc="T")
                b.wait("vector", "T", tfence())
                evac2[bk] = b.op("vector", lambda e, o=resid[:, dcol, c * 512:(c + 1) * 512],
                                 x=xown[:, dcol, :], s=bank(bk): e.tensor_add(o, x, s), inc="V")
                if dcol in (3, 7) and c + 1 < 4:
                    b.wait("sync", "V", b.cnt["V"])
                    xo_dma(c + 1, dcol // 4)
        wffn_V = stream_ffn(pairs)
        p6a_done = b.barrier()

        dump_src.update(resid=resid)
        if upto == "p6a":
            return finish(nc, b, stack, dump_t, dump_src, b.barrier())

        rows6 = (row6_mean, row6_var, l2_rstd, l2_nm)
        b.wait("vector", "T", p6a_done["T"])
        for c in range(4):
            src = resid[:, :, c * 512:(c + 1) * 512]
            rows_c = (row6_mean, row6_var,
                      l2_rstd[0:1, c * 512:(c + 1) * 512].rearrange("p c -> p c") if False
                      else l2_rstd, l2_nm)
            emit_ln_stats_mixed(src, rows_c, c * 512, raw6, sq6)
        p6b_done = b.barrier()

        b.wait("tensor", "V", wffn_V)

        ydma = []
        lnx2b_ = [lnx2b, view(poolC, 47104, BF16, [128, DC, 512])]
        ln_done = [0, 0, 0, 0]
        tmp6_ = [tmp6] + [view(poolD, k, F32, [128, 512]) for k in (2048, 4096, 6144)]
        ln2_prev = {}

        def emit_ln2_normalize(c):
            nonlocal ln2_prev
            ccols = slice(c * 512, (c + 1) * 512)
            b.wait("tensor", "S", b.cnt["S"])
            b.wait("tensor", "V", tail_ybufV[0])
            b.op("tensor", lambda e, o=bank(6), l=onesrow[0:1, :],
                 r=l2_rstd[0:1, c * 512:(c + 1) * 512]:
                 e.matmul(o, l, r, skip_group_check=True, start=True, stop=True))
            b.op("tensor", lambda e, o=bank(7), l=onesrow[0:1, :],
                 r=l2_nm[0:1, c * 512:(c + 1) * 512]:
                 e.matmul(o, l, r, skip_group_check=True, start=True, stop=True), inc="T")
            b.wait("vector", "T", tfence())
            m6 = {}
            a6 = {}

            def n6mul(dc):
                b.wait("vector", "V",
                       a6[dc - 4] if dc >= 4 else ln2_prev.get(dc + 4, 0))
                m6[dc] = b.op("vector", lambda e, o=tmp6_[dc % 4], s=resid[:, dc, ccols],
                              r=bank(6): e.tensor_mul(o, s, r), inc="V")
            n6mul(0)
            n6mul(1)
            for dc in range(DC):
                if dc + 2 < DC:
                    n6mul(dc + 2)
                b.wait("vector", "V", m6[dc])
                a6[dc] = b.op("vector", lambda e, o=lnx2b_[c % 2][:, dc, :],
                              t=tmp6_[dc % 4], n=bank(7): e.tensor_add(o, t, n), inc="V")
            ln2_prev = a6
            ln_done[c] = b.cnt["V"]

        b.wait("vector", "V", p6b_done["V"])
        pending_tail = [None]
        prev_ycV = [0]
        tail_ybufV = [0]
        emit_ln2_normalize(0)
        for c in range(4):
            ccols = slice(c * 512, (c + 1) * 512)
            b.wait("tensor", "V", ln_done[c])
            b.wait("tensor", "V", prev_ycV[0])
            for rc2 in range(2):
                for dc in range(DC):
                    b.op("tensor", lambda e, o=bank(rc2), l=u1b[:, dc, rc2 * 128:(rc2 + 1) * 128],
                         r=lnx2b_[c % 2][:, dc, :], st=(dc == 0), sp=(dc == DC - 1):
                         e.matmul(o, l, r, skip_group_check=True, start=st, stop=sp), inc="T")
            b.wait("vector", "T", tfence())
            p1V = b.op("vector", lambda e: e.tensor_copy(p1bv[:, 0, :], bank(0)))
            p1V = b.op("vector", lambda e: e.tensor_copy(p1bv[:, 1, :], bank(1)), inc="V")
            if pending_tail[0] is not None:
                pending_tail[0]()
                pending_tail[0] = None
            b.wait("tensor", "V", p1V)
            gelu_S = [0, 0]
            q2_T = [0, 0]
            hfence = {}

            def emit_h(fc):
                hbk = 2 + fc % 2
                b.wait("tensor", "S", gelu_S[fc % 2])
                for rc2 in range(2):
                    b.op("tensor", lambda e, o=bank(hbk), l=v1b[:, rc2, fc * 128:(fc + 1) * 128],
                         r=p1bv[:, rc2, :], st=(rc2 == 0), sp=(rc2 == 1):
                         e.matmul(o, l, r, skip_group_check=True, start=st, stop=sp), inc="T")
                return tfence()

            hfence[0] = emit_h(0)
            hfence[1] = emit_h(1)
            if c + 1 < 4:
                emit_ln2_normalize(c + 1)
            for fc in range(32):
                b.wait("scalar", "T", hfence[fc])
                b.wait("scalar", "T", q2_T[fc % 2])
                gelu_S[fc % 2] = b.op("scalar", lambda e, o=hb2[:, fc % 2, :],
                                      i=bank(2 + fc % 2), bb=b1s[:, fc:fc + 1]:
                                      e.activation(o, i, AF.Gelu_apprx_tanh, bias=bb), inc="S")
                if fc + 2 < 32:
                    hfence[fc + 2] = emit_h(fc + 2)
                b.wait("tensor", "S", gelu_S[fc % 2])
                for rc2 in range(2):
                    b.op("tensor", lambda e, o=bank(4 + rc2),
                         l=u2b[:, fc, rc2 * 128:(rc2 + 1) * 128],
                         r=hb2[:, fc % 2, :], st=(fc == 0), sp=(fc == 31):
                         e.matmul(o, l, r, skip_group_check=True, start=st, stop=sp), inc="T")
                q2_T[fc % 2] = b.cnt["T"]
            b.wait("vector", "T", tfence())
            q2V = b.op("vector", lambda e: e.tensor_copy(q2bv[:, 0, :], bank(4)))
            q2V = b.op("vector", lambda e: e.tensor_copy(q2bv[:, 1, :], bank(5)), inc="V")
            b.wait("tensor", "V", q2V)
            oev = {0: 0, 1: 0}
            for dcol in range(DC):
                obk = dcol % 2
                b.wait("tensor", "V", oev[obk])
                for rc2 in range(2):
                    b.op("tensor", lambda e, o=bank(obk), l=v2b[:, rc2, dcol * 128:(dcol + 1) * 128],
                         r=q2bv[:, rc2, :], st=(rc2 == 0), sp=(rc2 == 1):
                         e.matmul(o, l, r, skip_group_check=True, start=st, stop=sp), inc="T")
                b.wait("vector", "T", tfence())
                oev[obk] = b.op("vector", lambda e, o=y_cv[:, dcol, :], s=bank(obk),
                                bb=b2s[:, dcol:dcol + 1], rr=resid[:, dcol, ccols]:
                                e.scalar_tensor_tensor(o, s, bb, rr, ALU.add, ALU.add), inc="V")
            ycV = b.cnt["V"]
            prev_ycV[0] = ycV

            def make_tail(c=c, ycV=ycV):
                def tail():
                    tpos_done = ycV
                    if ydma:
                        b.wait("vector", ydma[-1], 16)
                    for tt4 in range(4):
                        b.wait("tensor", "V", tpos_done)
                        for dcol in range(DC):
                            tb_ = 6 + dcol // 4
                            b.op("tensor", lambda e,
                                 o=bank(tb_, (dcol % 4) * 128, (dcol % 4 + 1) * 128),
                                 i=y_cv[:, dcol, tt4 * 128:(tt4 + 1) * 128]:
                                 e.transpose(o, i, identf[:, :]), inc="T")
                        b.wait("vector", "T", tfence())
                        b.op("vector", lambda e, o=ybuf[:, tt4, 0:512], s=bank(6):
                             e.tensor_copy(o, s))
                        tpos_done = b.op("vector", lambda e, o=ybuf[:, tt4, 512:1024],
                                         s=bank(7): e.tensor_copy(o, s), inc="V")
                    tail_ybufV[0] = tpos_done
                    b.wait("sync", "V", tpos_done)
                    ydst = y_out.rearrange("(c tt p) d -> c p tt d", p=128, tt=4)[c]
                    ydma.append(b.dma(ydst, ybuf[:, :, :]))
                return tail
            pending_tail[0] = make_tail()

        pending_tail[0]()
        return finish(nc, b, stack, dump_t, dump_src, b.barrier())


def finish(nc, b, stack, dump_t, dump_src, counts):
    b.wait_all("sync", counts, sems=("T", "S", "V", "P"))
    for name, ap in dump_t.items():
        src = dump_src[name]
        flat = src
        nd = len(flat.shape)
        if nd == 3:
            flat = flat.rearrange("p a b -> p (a b)")
        elif nd == 4:
            flat = flat.rearrange("p a b c -> p (a b c)")
        b.dma(ap, flat)
    for i in range(b.dma_n):
        b.wait("sync", f"dma{i}", 16)
    block = stack.enter_context(nc.Block())
    b.replay(block)
    return nc


def _host_inputs(core, x, W_attn, W_proj, U1, V1, b1, U2, V2, b2):
    seq, cls = core // 2, core % 2
    xt = np.ascontiguousarray(x[seq].T)
    perm = np.arange(32)
    if cls == 1:
        perm = perm.reshape(16, 2)[:, ::-1].reshape(32)
    xt_p = np.ascontiguousarray(
        xt.reshape(D, 32, 128)[:, perm, :].reshape(D, T))
    own_tiles = perm[0::2]
    x_own = np.ascontiguousarray(
        xt.reshape(D, 32, 128)[:, own_tiles, :].reshape(D, OWNT))
    tril = np.tril(np.ones((128, 128), np.float32))
    trilT = tril.T
    m0 = trilT
    m1 = np.zeros((128, 128), np.float32) if cls == 0 else np.ones((128, 128), np.float32)
    masks = np.stack([np.stack([m0, m0]), np.stack([m1, m1])])
    masks = np.ascontiguousarray(masks.transpose(2, 0, 1, 3)).astype(ml_dtypes.bfloat16)
    return {
        "x_t": xt_p, "x_own_t": x_own,
        "W_attn": W_attn, "W_proj": W_proj, "U1": U1, "V1": V1,
        "U2": U2, "V2": V2,
        "b1l": np.ascontiguousarray(b1.reshape(DFF // 128, 128).T),
        "b2l": np.ascontiguousarray(b2.reshape(D // 128, 128).T),
        "masks": masks,
        "identb": np.eye(128, dtype=np.float32).astype(ml_dtypes.bfloat16),
        "identf": np.eye(128, dtype=np.float32),
        "onescol": np.ones((128, 1), np.float32).astype(ml_dtypes.bfloat16),
        "onesrow": np.ones((1, 128), np.float32),
    }, own_tiles


_NC_CACHE = {}


def kernel(L0_attn_x, L0_attn_W_attn, L0_attn_W_proj, L0_ffn_U1, L0_ffn_V1,
           L0_ffn_b1, L0_ffn_U2, L0_ffn_V2, L0_ffn_b2):
    x = np.asarray(L0_attn_x, dtype=np.float32)
    args = tuple(np.ascontiguousarray(np.asarray(a, dtype=np.float32)) for a in
                 (L0_attn_W_attn, L0_attn_W_proj, L0_ffn_U1, L0_ffn_V1,
                  L0_ffn_U2, L0_ffn_V2))
    b1 = np.asarray(L0_ffn_b1, np.float32)
    b2 = np.asarray(L0_ffn_b2, np.float32)
    if "nc" not in _NC_CACHE:
        _NC_CACHE["nc"] = build_program("full")
    nc = _NC_CACHE["nc"]
    in_maps = []
    owns = []
    for core in range(8):
        m, own_tiles = _host_inputs(core, x, args[0], args[1], args[2], args[3],
                                    b1, args[4], args[5], b2)
        in_maps.append(m)
        owns.append(own_tiles)
    res = run_bass_kernel_spmd(nc, in_maps, core_ids=list(range(8)))
    out = np.empty((4, T, D), np.float32)
    for core in range(8):
        seq = core // 2
        yc = res.results[core]["y"]
        rows = (owns[core][:, None] * 128 + np.arange(128)[None, :]).reshape(-1)
        out[seq, rows, :] = yc
    return out


# version 33
# speedup vs baseline: 1.0358x; 1.0022x over previous
import contextlib
import numpy as np
import ml_dtypes
import concourse.bass as bass
import concourse.mybir as mybir
from concourse.bass_utils import run_bass_kernel_spmd

F32 = mybir.dt.float32
BF16 = mybir.dt.bfloat16
U8 = mybir.dt.uint8
AF = mybir.ActivationFunctionType
ALU = mybir.AluOpType

D, T, HD, NP, DC, OWNT = 1024, 4096, 64, 8, 8, 2048
DFF, RANK = 4096, 256
SCALE_QK = 0.125
VSCALE = 1e-3
EPS = 1e-5


def prod(xs):
    r = 1
    for x in xs:
        r *= x
    return r


class Build:
    ENGINES = ("sync", "tensor", "scalar", "vector", "gpsimd")

    def __init__(self, nc):
        self.nc = nc
        self.streams = {e: [] for e in self.ENGINES}
        self.sem_handles = {}
        self.cnt = {"T": 0, "S": 0, "V": 0, "P": 0}
        self.last_wait = {e: {s: 0 for s in self.cnt} for e in self.ENGINES}

    N_DMA_SEMS = 96

    def init_sems(self, stack):
        for s in self.cnt:
            self.sem_handles[s] = stack.enter_context(self.nc.semaphore("sem_" + s))
        self.dma_sems = [stack.enter_context(self.nc.semaphore(f"dsem{i}"))
                         for i in range(self.N_DMA_SEMS)]
        self.dma_n = 0

    ENG_SEM = {"tensor": "T", "scalar": "S", "vector": "V", "gpsimd": "P"}

    def op(self, eng, fn, inc=None, ss=False):
        inc = self.ENG_SEM[eng]
        h = self.sem_handles[inc]
        selfw = 0
        if ss:
            v = self.cnt[inc]
            if v > 0 and self.last_wait[eng][inc] < v:
                selfw = v
                self.last_wait[eng][inc] = v
        self.cnt[inc] += 1

        def wrapped(e, fn=fn, h=h, w=selfw):
            i = fn(e)
            if w:
                i.wait_op(h, w, "sem-ge")
            i.then_inc(h, 1)
        self.streams[eng].append(wrapped)
        return self.cnt[inc]

    def _dma_on(self, queue, dst, src, slot=None):
        key = f"dma{self.dma_n}"
        self.dma_n += 1
        if not hasattr(self, "dma_slots"):
            self.dma_slots = {}
        if slot is None:
            slot = key
        if slot not in self.dma_slots:
            assert len(self.dma_slots) < self.N_DMA_SEMS, "DMA sem pool exhausted"
            self.dma_slots[slot] = [self.dma_sems[len(self.dma_slots)], 0]
        ent = self.dma_slots[slot]
        ent[1] += 16
        h = ent[0]
        self.sem_handles[key] = h
        self.cnt[key] = ent[1]
        for e_ in self.ENGINES:
            self.last_wait[e_][key] = 0
        self.streams[queue].append(
            lambda e, d=dst, s=src, h=h: e.dma_start(out=d, in_=s).then_inc(h, 16))
        return key

    def dma(self, dst, src, slot=None):
        return self._dma_on("sync", dst, src, slot)

    def dma2(self, dst, src, slot=None):
        return self._dma_on("scalar", dst, src, slot)

    def wait(self, eng, sem, val):
        if not val or val <= 0:
            return
        if sem.startswith("dma"):
            val = self.cnt[sem]
        assert val <= self.cnt[sem], f"{eng} waits future {sem}={val} (now {self.cnt[sem]})"
        if self.last_wait[eng][sem] >= val:
            return
        self.last_wait[eng][sem] = val
        h = self.sem_handles[sem]
        self.streams[eng].append(lambda e, h=h, v=val: e.wait_ge(h, v))

    def barrier(self):
        return dict(self.cnt)

    def wait_all(self, eng, counts, sems=("T", "S", "V")):
        for s in sems:
            self.wait(eng, s, counts[s])

    def replay(self, block):
        for name in self.ENGINES:
            ops = self.streams[name]

            def body(eng, ops=ops):
                for f in ops:
                    f(eng)
            body.__name__ = "_" + name
            getattr(block, name)(body)


def view(pool, byte_off, dtype, shape):
    esz = mybir.dt.size(dtype)
    assert byte_off % 32 == 0 and byte_off % esz == 0
    n = prod(shape[1:])
    ap = pool.bitcast(dtype).ap()[0:shape[0], byte_off // esz: byte_off // esz + n]
    if len(shape) > 2:
        names = [f"a{i}" for i in range(len(shape) - 1)]
        spec = "p (" + " ".join(names) + ") -> p " + " ".join(names)
        ap = ap.rearrange(spec, **{names[i]: shape[i + 1] for i in range(len(shape) - 1)})
    return ap


def build_program(upto="full", dumps=()):
    nc = bass.Bass("TRN2", target_bir_lowering=False)

    x_t = nc.dram_tensor("x_t", [D, T], F32, kind="ExternalInput").ap()
    x_own_t = nc.dram_tensor("x_own_t", [D, OWNT], F32, kind="ExternalInput").ap()
    W_attn = nc.dram_tensor("W_attn", [D, 3 * D], F32, kind="ExternalInput").ap()
    W_projd = nc.dram_tensor("W_proj", [D, D], F32, kind="ExternalInput").ap()
    U1d = nc.dram_tensor("U1", [D, RANK], F32, kind="ExternalInput").ap()
    V1d = nc.dram_tensor("V1", [RANK, DFF], F32, kind="ExternalInput").ap()
    U2d = nc.dram_tensor("U2", [DFF, RANK], F32, kind="ExternalInput").ap()
    V2d = nc.dram_tensor("V2", [RANK, D], F32, kind="ExternalInput").ap()
    b1d = nc.dram_tensor("b1l", [128, DFF // 128], F32, kind="ExternalInput").ap()
    b2d = nc.dram_tensor("b2l", [128, D // 128], F32, kind="ExternalInput").ap()
    masks_d = nc.dram_tensor("masks", [128, 2, 2, 128], BF16, kind="ExternalInput").ap()
    identb_d = nc.dram_tensor("identb", [128, 128], BF16, kind="ExternalInput").ap()
    identf_d = nc.dram_tensor("identf", [128, 128], F32, kind="ExternalInput").ap()
    onescol_d = nc.dram_tensor("onescol", [128, 1], BF16, kind="ExternalInput").ap()
    onesrow_d = nc.dram_tensor("onesrow", [1, 128], F32, kind="ExternalInput").ap()
    y_out = nc.dram_tensor("y", [OWNT, D], F32, kind="ExternalOutput").ap()
    kf2_d = nc.dram_tensor("kf2scratch", [128, NP, 2048], BF16, kind="Internal").ap()

    dump_specs = {
        "lnxb": (BF16, 0), "kf": (BF16, 1), "vt": (BF16, 2), "qf": (BF16, 3),
        "msnap": (BF16, 4), "prefsnap": (F32, 5), "avpart": (BF16, 6), "resid": (F32, 7),
        "wxb": (BF16, 8),
    }
    dump_t = {}

    with contextlib.ExitStack() as stack:
        E_ = stack.enter_context
        poolA = E_(nc.sbuf_tensor("poolA", [128, 65536], U8))
        poolB = E_(nc.sbuf_tensor("poolB", [128, 32768], U8))
        poolC = E_(nc.sbuf_tensor("poolC", [128, 65536], U8))
        poolD = E_(nc.sbuf_tensor("poolD", [128, 28672], U8))
        identb = E_(nc.sbuf_tensor("sb_identb", [128, 128], BF16))
        identf = E_(nc.sbuf_tensor("sb_identf", [128, 128], F32))
        onescol = E_(nc.sbuf_tensor("sb_onescol", [128, 1], BF16))
        onesrow = E_(nc.sbuf_tensor("sb_onesrow", [1, 128], F32))
        masksb = E_(nc.sbuf_tensor("sb_masksb", [128, 2, 2, 128], BF16))
        consts = E_(nc.sbuf_tensor("consts", [128, 3], F32))
        msnap = E_(nc.sbuf_tensor("msnap", [128, 8, 512], BF16))
        prefsnap = E_(nc.sbuf_tensor("prefsnap", [128, 17, 8], F32))
        b1s = E_(nc.sbuf_tensor("sb_b1s", [128, DFF // 128], F32))
        msum_t = E_(nc.sbuf_tensor("sb_msum", [128, 512], F32))
        prefsum_t = E_(nc.sbuf_tensor("sb_prefsum", [128, 8], F32))
        b2s = E_(nc.sbuf_tensor("sb_b2s", [128, D // 128], F32))
        psT = E_(nc.psum_tensor("ps", [128, 4096], F32))

        b = Build(nc)
        b.init_sems(stack)

        def vsync():
            b.wait("vector", "V", b.cnt["V"])

        def ssync():
            b.wait("scalar", "S", b.cnt["S"])

        def tfence():
            return b.cnt["T"]

        lnxh = [view(poolC, 0, BF16, [128, DC, 2048]),
                view(poolC, 32768, BF16, [128, DC, 2048])]
        kf = view(poolA, 0, BF16, [128, NP, 2048])
        vt = view(poolA, 32768, BF16, [128, 16, 1024])
        resid = view(poolA, 0, F32, [128, DC, OWNT])
        qf = view(poolB, 0, BF16, [128, NP, OWNT])
        v1b = view(poolB, 0, BF16, [128, 2, DFF])
        u2b = view(poolB, 16384, BF16, [128, 32, RANK])
        avpart = view(poolC, 0, BF16, [128, NP, OWNT])
        x2stage = view(poolC, 32768, F32, [128, DC, 512])
        tmp4 = view(poolC, 57344, F32, [128, 512])
        ln2blk = view(poolC, 49152, BF16, [128, DC, 512])
        xown = view(poolC, 32768, F32, [128, DC, 512])
        ybuf = view(poolC, 0, F32, [128, 4, D])
        y_cv = view(poolC, 16384, F32, [128, DC, 512])
        lnx2b = view(poolC, 32768, BF16, [128, DC, 512])
        p1bv = view(poolC, 40960, BF16, [128, 2, 512])
        hb2 = view(poolC, 43008, BF16, [128, 2, 512])
        q2bv = view(poolC, 45056, BF16, [128, 2, 512])
        xstage2_ = [view(poolD, 0, F32, [128, DC, 512]),
                    view(poolA, 0, F32, [128, DC, 512])]
        rawb = view(poolD, 16384, BF16, [128, 512])
        sqb = view(poolD, 17408, BF16, [128, 512])
        rows0 = [view(poolD, 18432 + 2048 * i, F32, [1, 512]) for i in range(4)]
        r_rows = view(poolB, 0, F32, [1, T])
        nm_rows = view(poolB, 16384, F32, [1, T])
        tmp32 = view(poolD, 26624, F32, [128, 512])
        wstage = view(poolD, 0, F32, [128, 2, 1024])
        wxb = view(poolD, 8192, BF16, [128, DC, 1024])
        eb = view(poolD, 0, BF16, [128, 6, 512])
        gb = view(poolD, 6144, BF16, [128, 6, 512])
        ab = view(poolD, 12288, BF16, [128, 6, 512])
        zb = view(poolD, 18432, BF16, [128, 3, 2, 128])
        zmb = view(poolD, 12288, BF16, [128, 3, 2, 128])
        abh = view(poolD, 13824, BF16, [128, 2, 128])
        K_ILV = False
        kstage4 = view(poolD, 20480, F32, [128, 8, 128])
        wkvh = [view(poolD, 14336, BF16, [128, 8, 128]),
                view(poolD, 16384, BF16, [128, 8, 128])]
        kevac2 = [view(poolD, 24576, BF16, [128, 512]),
                  view(poolD, 25600, BF16, [128, 512])]
        ab2 = [view(poolD, 14336, BF16, [128, 2, 512]),
               view(poolD, 16384, BF16, [128, 2, 512])]
        ktb2_ = [view(poolD, 24576, BF16, [128, 1024]),
                 view(poolD, 26624, BF16, [128, 1024])]
        wkvb = view(poolD, 0, BF16, [128, DC, 1024])
        wstage4 = view(poolD, 16384, F32, [128, 512])
        raw4 = view(poolD, 18432, BF16, [128, 512])
        sq4 = view(poolD, 19456, BF16, [128, 512])
        rows4 = [view(poolD, 20480 + 2048 * i, F32, [1, 512]) for i in range(4)]
        wprojb = view(poolD, 0, BF16, [128, NP, 1024])
        wstage6 = view(poolD, 16384, F32, [128, 1024])
        l2_rstd = view(poolD, 20480, F32, [1, 2048])
        l2_nm = view(poolD, 8192, F32, [1, 2048])
        row6_mean = view(poolD, 2048, F32, [1, 512])
        row6_var = view(poolD, 4096, F32, [1, 512])
        raw6 = view(poolD, 0, BF16, [128, 512])
        sq6 = view(poolD, 1024, BF16, [128, 512])
        tmp6 = view(poolD, 0, F32, [128, 512])
        u1b = view(poolC, 55296, BF16, [128, DC, RANK])
        v2b = view(poolC, 59392, BF16, [128, 2, D])

        msum_sb = msum_t.ap()
        prefsum_sb = prefsum_t.ap()
        ps = psT.ap()
        psbf = psT.bitcast(BF16).ap()

        def bank(i, lo=0, hi=512):
            return ps[:, i * 512 + lo: i * 512 + hi]
        ps8 = ps.rearrange("p (k c) -> p k c", k=8)
        tp_ps = psbf[:, 0:1024]

        for name in dumps:
            dt, _ = dump_specs[name]
            shp = {"lnxb": DC * T, "kf": NP * 2048, "vt": 16 * 1024, "qf": NP * OWNT,
                   "msnap": 8 * 512, "prefsnap": 17 * 8, "avpart": NP * OWNT,
                   "resid": DC * OWNT, "wxb": DC * 1024}[name]
            dump_t[name] = nc.dram_tensor("dump_" + name, [128, shp], dt,
                                          kind="ExternalOutput").ap()
        dump_src = {}

        d_misc_keys = []
        for dst, src in ((identb[:, :], identb_d[:, :]), (identf[:, :], identf_d[:, :]),
                         (onescol[:, :], onescol_d[:, :]), (onesrow[:, :], onesrow_d[:, :]),
                         (masksb[:, :, :, :], masks_d[:, :, :, :]),
                         (b1s[:, :], b1d[:, :]), (b2s[:, :], b2d[:, :])):
            d_misc_keys.append(b.dma(dst, src))

        def wait_misc(eng):
            for k in d_misc_keys:
                b.wait(eng, k, 16)
        b.op("vector", lambda e: e.memset(consts[:, 0:1], EPS))
        b.op("vector", lambda e: e.memset(consts[:, 1:2], 1.0))
        cV = b.op("vector", lambda e: e.memset(consts[:, 2:3], 1.0), inc="V")
        onescol_f = consts[:, 2:3]
        eps_ap = consts[:, 0:1]

        def emit_ln_stats(src_f32, rows, co, raw, sq, src_ready_d):
            rm = rows[0][0:1, co:co + 512]
            rv = rows[1][0:1, co:co + 512]
            rr = rows[2][0:1, co:co + 512]
            rn = rows[3][0:1, co:co + 512]
            b.wait("vector", "d", src_ready_d)
            b.wait("vector", "V", cV)
            b.wait("vector", "T", b.cnt["T"])
            raw2 = [raw, view(poolD, 6144, BF16, [128, 512])]
            sq2 = [sq, view(poolD, 7168, BF16, [128, 512])]
            t_done = {}
            for dc in range(DC):
                if dc >= 2:
                    b.wait("vector", "T", t_done[dc - 2])
                b.op("vector", lambda e, a=raw2[dc % 2], s=src_f32[:, dc, :]:
                     e.tensor_copy(a, s))
                b.op("vector", lambda e, a=sq2[dc % 2], r=raw2[dc % 2]:
                     e.tensor_mul(a, r, r), inc="V", ss=True)
                b.wait("tensor", "V", b.cnt["V"])
                b.op("tensor", lambda e, o=bank(0)[0:1, :], l=onescol[:, :], r=raw2[dc % 2],
                     st=(dc == 0), sp=(dc == DC - 1): e.matmul(o, l, r, skip_group_check=True, start=st, stop=sp))
                b.op("tensor", lambda e, o=bank(1)[0:1, :], l=onescol[:, :], r=sq2[dc % 2],
                     st=(dc == 0), sp=(dc == DC - 1): e.matmul(o, l, r, skip_group_check=True, start=st, stop=sp),
                     inc="T")
                t_done[dc] = b.cnt["T"]
            b.wait("scalar", "T", tfence())
            b.op("scalar", lambda e, o=rm, i=bank(0)[0:1, :]: e.mul(o, i, 1.0 / D))
            b.op("scalar", lambda e, o=rv, i=bank(1)[0:1, :]: e.mul(o, i, 1.0 / D),
                 inc="S")
            b.wait("vector", "S", b.cnt["S"])
            b.op("vector", lambda e, o=rn, m=rm: e.tensor_mul(o, m, m))
            b.op("vector", lambda e, o=rv, v=rv,
                 m=rn: e.tensor_sub(o, v, m), inc="V", ss=True)
            b.wait("scalar", "V", b.cnt["V"])
            b.op("scalar", lambda e, o=rr, i=rv:
                 e.activation(o, i, AF.Ln, bias=eps_ap[0:1, :]))
            b.op("scalar", lambda e, o=rr, i=rr:
                 e.activation(o, i, AF.Exp, scale=-0.5), inc="S", ss=True)
            b.wait("vector", "S", b.cnt["S"])
            b.op("vector", lambda e, o=rn, m=rm,
                 r=rr: e.scalar_tensor_tensor(o, m, -1.0, r, ALU.mult, ALU.mult),
                 inc="V")

        def emit_ln_normalize(src_f32, dst_slices, rows, co, tmpn):
            rr = rows[2][0:1, co:co + 512]
            rn = rows[3][0:1, co:co + 512]
            b.wait("tensor", "V", b.cnt["V"])
            b.op("tensor", lambda e, o=bank(2), l=onesrow[0:1, :], r=rr:
                 e.matmul(o, l, r, skip_group_check=True, start=True, stop=True))
            b.op("tensor", lambda e, o=bank(3), l=onesrow[0:1, :], r=rn:
                 e.matmul(o, l, r, skip_group_check=True, start=True, stop=True), inc="T")
            b.wait("vector", "T", tfence())
            for dc in range(DC):
                b.op("vector", lambda e, o=tmpn, s=src_f32[:, dc, :], r=bank(2):
                     e.tensor_mul(o, s, r), ss=True)
                b.op("vector", lambda e, o=dst_slices[dc], t=tmpn, n=bank(3):
                     e.tensor_add(o, t, n), inc="V", ss=True)
            return b.cnt["V"]

        def emit_ln_stats_mixed(src_f32, rows, co_rs, raw, sq):
            rm = rows[0][0:1, 0:512]
            rv = rows[1][0:1, 0:512]
            rr = rows[2][0:1, co_rs:co_rs + 512]
            rn = rows[3][0:1, co_rs:co_rs + 512]
            b.wait("vector", "V", cV)
            t_entry = b.cnt["T"]
            raw2_ = [raw, view(poolD, 6144, BF16, [128, 512])]
            sq2_ = [sq, view(poolD, 7168, BF16, [128, 512])]
            t_done6 = {}
            c6 = {}

            def p6cpy(dc):
                b.wait("vector", "T", t_done6[dc - 2] if dc >= 2 else t_entry)
                c6[dc] = b.op("vector", lambda e, a=raw2_[dc % 2], s=src_f32[:, dc, :]:
                              e.tensor_copy(a, s), inc="V")
            p6cpy(0)
            for dc in range(DC):
                if dc + 1 < DC:
                    p6cpy(dc + 1)
                b.wait("vector", "V", c6[dc])
                sq_v = b.op("vector", lambda e, a=sq2_[dc % 2], r=raw2_[dc % 2]:
                            e.tensor_mul(a, r, r), inc="V")
                b.wait("tensor", "V", sq_v)
                b.op("tensor", lambda e, o=bank(0)[0:1, :], l=onescol[:, :], r=raw2_[dc % 2],
                     st=(dc == 0), sp=(dc == DC - 1): e.matmul(o, l, r, skip_group_check=True, start=st, stop=sp))
                b.op("tensor", lambda e, o=bank(1)[0:1, :], l=onescol[:, :], r=sq2_[dc % 2],
                     st=(dc == 0), sp=(dc == DC - 1): e.matmul(o, l, r, skip_group_check=True, start=st, stop=sp),
                     inc="T")
                t_done6[dc] = b.cnt["T"]
            b.wait("scalar", "T", tfence())
            b.op("scalar", lambda e, o=rm, i=bank(0)[0:1, :]: e.mul(o, i, 1.0 / D))
            b.op("scalar", lambda e, o=rv, i=bank(1)[0:1, :]: e.mul(o, i, 1.0 / D), inc="S")
            b.wait("vector", "S", b.cnt["S"])
            b.op("vector", lambda e, o=rn, m=rm: e.tensor_mul(o, m, m))
            b.op("vector", lambda e, o=rv, v=rv, m=rn: e.tensor_sub(o, v, m), inc="V",
                 ss=True)
            b.wait("scalar", "V", b.cnt["V"])
            b.op("scalar", lambda e, o=rr, i=rv:
                 e.activation(o, i, AF.Ln, bias=eps_ap[0:1, :]))
            b.op("scalar", lambda e, o=rr, i=rr:
                 e.activation(o, i, AF.Exp, scale=-0.5), inc="S", ss=True)
            b.wait("vector", "S", b.cnt["S"])
            b.op("vector", lambda e, o=rn, m=rm, r=rr:
                 e.scalar_tensor_tensor(o, m, -1.0, r, ALU.mult, ALU.mult), inc="V")

        x_src = x_t.rearrange("(dc p) t -> p dc t", p=128)
        wait_misc("tensor")
        wait_misc("vector")
        wait_misc("scalar")
        wait_misc("gpsimd")
        relVs = [0, 0]
        relTs = [0, 0]
        for tb in range(8):
            xs = xstage2_[tb % 2]
            b.wait("sync", "V", relVs[tb % 2])
            b.wait("sync", "T", relTs[tb % 2])
            dd = b.dma(xs[:, :, :], x_src[:, :, tb * 512:(tb + 1) * 512])
            co = tb * 512
            rm = rows0[0][0:1, 0:512]
            rv = rows0[1][0:1, 0:512]
            rr = r_rows[0:1, co:co + 512]
            rn = nm_rows[0:1, co:co + 512]
            b.wait("vector", dd, 16)
            b.wait("vector", "V", cV)
            b.wait("vector", "T", b.cnt["T"])
            rawx = [rawb, view(poolD, 26624, BF16, [128, 512]),
                    view(poolD, 22528, BF16, [128, 512]),
                    view(poolD, 24576, BF16, [128, 512])]
            sqx = [sqb, view(poolD, 27648, BF16, [128, 512]),
                   view(poolD, 23552, BF16, [128, 512]),
                   view(poolD, 25600, BF16, [128, 512])]
            t_done = {}
            ccnt = {}

            def p1cpy(dc):
                if dc >= 4:
                    b.wait("vector", "T", t_done[dc - 4])
                ccnt[dc] = b.op("vector", lambda e, a=rawx[dc % 4], x_=xs[:, dc, :]:
                                e.tensor_copy(a, x_), inc="V")
            p1cpy(0)
            p1cpy(1)
            for dc in range(DC):
                if dc + 2 < DC:
                    p1cpy(dc + 2)
                b.wait("vector", "V", ccnt[dc])
                sq_v = b.op("vector", lambda e, a=sqx[dc % 4], r=rawx[dc % 4]:
                            e.tensor_mul(a, r, r), inc="V")
                b.wait("tensor", "V", sq_v)
                b.op("tensor", lambda e, o=bank(0)[0:1, :], l=onescol[:, :], r=rawx[dc % 4],
                     st=(dc == 0), sp=(dc == DC - 1): e.matmul(o, l, r, skip_group_check=True, start=st, stop=sp))
                b.op("tensor", lambda e, o=bank(1)[0:1, :], l=onescol[:, :], r=sqx[dc % 4],
                     st=(dc == 0), sp=(dc == DC - 1): e.matmul(o, l, r, skip_group_check=True, start=st, stop=sp),
                     inc="T")
                t_done[dc] = b.cnt["T"]
            relVs[tb % 2] = b.cnt["V"]
            relTs[tb % 2] = b.cnt["T"]
            b.wait("scalar", "T", tfence())
            b.op("scalar", lambda e, o=rm, i=bank(0)[0:1, :]: e.mul(o, i, 1.0 / D))
            b.op("scalar", lambda e, o=rv, i=bank(1)[0:1, :]: e.mul(o, i, 1.0 / D), inc="S")
            b.wait("vector", "S", b.cnt["S"])
            b.op("vector", lambda e, o=rn, m=rm: e.tensor_mul(o, m, m))
            b.op("vector", lambda e, o=rv, v=rv, m=rn: e.tensor_sub(o, v, m), inc="V",
                 ss=True)
            b.wait("scalar", "V", b.cnt["V"])
            b.op("scalar", lambda e, o=rr, i=rv:
                 e.activation(o, i, AF.Ln, bias=eps_ap[0:1, :]))
            b.op("scalar", lambda e, o=rr, i=rr:
                 e.activation(o, i, AF.Exp, scale=-0.5), inc="S", ss=True)
            b.wait("vector", "S", b.cnt["S"])
            b.op("vector", lambda e, o=rn, m=rm, r=rr:
                 e.scalar_tensor_tensor(o, m, -1.0, r, ALU.mult, ALU.mult), inc="V")
        relVs = [b.cnt["V"], b.cnt["V"]]
        bc_free = 0
        p2_prev = {}
        for tb in range(8):
            xs = xstage2_[tb % 2]
            b.wait("sync", "V", relVs[tb % 2])
            dd = b.dma(xs[:, :, :], x_src[:, :, tb * 512:(tb + 1) * 512])
            co = tb * 512
            bka, bkb = (2, 3) if tb % 2 == 0 else (4, 5)
            b.wait("tensor", "V", bc_free)
            b.op("tensor", lambda e, o=bank(bka), l=onesrow[0:1, :], r=r_rows[0:1, co:co + 512]:
                 e.matmul(o, l, r, skip_group_check=True, start=True, stop=True))
            b.op("tensor", lambda e, o=bank(bkb), l=onesrow[0:1, :], r=nm_rows[0:1, co:co + 512]:
                 e.matmul(o, l, r, skip_group_check=True, start=True, stop=True), inc="T")
            tb_T = tfence()
            b.wait("vector", "T", tb_T)
            b.wait("vector", dd, 16)
            tmp2 = [tmp32, view(poolD, 16384, F32, [128, 512]),
                    view(poolD, 22528, F32, [128, 512]),
                    view(poolD, 24576, F32, [128, 512])]
            mcnt2 = {}
            acnt2 = {}

            def p2mul(dc):
                b.wait("vector", "V",
                       acnt2[dc - 4] if dc >= 4 else p2_prev.get(dc + 4, 0))
                mcnt2[dc] = b.op("vector", lambda e, o=tmp2[dc % 4], s=xs[:, dc, :],
                                 r=bank(bka): e.tensor_mul(o, s, r), inc="V")
            p2mul(0)
            p2mul(1)
            for dc in range(DC):
                if dc + 2 < DC:
                    p2mul(dc + 2)
                b.wait("vector", "V", mcnt2[dc])
                acnt2[dc] = b.op("vector", lambda e,
                                 o=lnxh[tb // 4][:, dc, (tb % 4) * 512:(tb % 4 + 1) * 512],
                                 t=tmp2[dc % 4], n=bank(bkb): e.tensor_add(o, t, n),
                                 inc="V")
            p2_prev = acnt2
            relVs[tb % 2] = b.cnt["V"]
            if tb % 2 == 1:
                bc_free = b.cnt["V"]
        p0_done = b.barrier()

        s_rel = [0, 0, 0]
        z_rel = [0, 0, 0]
        e_rel = [0, 0, 0]
        e_relP = [0, 0, 0]
        ab_rel = [0, 0, 0]
        zw_rel = [0, 0, 0]
        g_rel = [0, 0, 0]
        SBK = (0, 2, 6)
        av_free = {4: 0, 5: 0}
        av_freeS = {4: 0, 5: 0}
        sweep_state = {"tpV0": 0, "tpV1": 0, "snapV": 0, "mprefT": 0}

        def emit_sweep_kt(ktg, half, vt_ready_V=0):
            if True:
                ktloc = ktg - 16 * half
                b.wait("tensor", "V", vt_ready_V)
                par2 = ktloc % 2
                tpx = psbf[:, par2 * 1024:(par2 + 1) * 1024]
                ktb = ktb2_[par2]
                b.wait("tensor", "V", sweep_state["tpV" + str(par2)])
                b.wait("tensor", "V", sweep_state["snapV"])
                if K_ILV and half == 1:
                    b.wait("tensor", kfload[ktloc // 4], 16)
                for p in range(NP):
                    b.op("tensor", lambda e, o=tpx[:, p * 128:(p + 1) * 128],
                         i=kf[:, p, ktloc * 128:(ktloc + 1) * 128]:
                         e.transpose(o, i, identb[:, :]), inc="T")
                b.wait("vector", "T", tfence())
                sweep_state["tpV" + str(par2)] = b.op(
                    "vector", lambda e, k=ktb, t=tpx: e.tensor_scalar_mul(k[:, :], t, -0.01),
                    inc="V")
                b.wait("tensor", "V", sweep_state["tpV" + str(par2)])
                for p in range(NP):
                    for par in range(2):
                        b.op("tensor", lambda e,
                             o=ps[par * 64:(par + 1) * 64, 6 * 512 + p * 64: 6 * 512 + (p + 1) * 64],
                             l=ktb[:, p * 128 + par * 64: p * 128 + (par + 1) * 64],
                             r=vt[:, ktloc, (2 * p + par) * 64:(2 * p + par + 1) * 64],
                             st=(ktg % 16 == 0 and p == 0), tp=(0, par * 64):
                             e.matmul(o, l, r, skip_group_check=True, start=st, stop=True, tile_position=tp), inc="T")
                for dc in range(DC):
                    b.op("tensor", lambda e, o=bank(7, 2 * dc, 2 * dc + 1),
                         l=vt[:, ktloc, dc * 128:(dc + 1) * 128], r=onescol[:, :],
                         st=(ktg % 16 == 0 and dc == 0): e.matmul(o, l, r, skip_group_check=True, start=st, stop=True),
                         inc="T")
                sweep_state["mprefT"] = b.cnt["T"]
                if ktg % 2 == 1:
                    j = (ktg + 3) // 2
                    pref_src = bank(7, 0, 16).rearrange("p (a b) -> p a b", b=2)[:, :, 0]
                    if 2 <= j <= 16:
                        b.wait("vector", "T", tfence())
                        if half == 0:
                            b.op("vector", lambda e, o=msnap[:, j % 8, :], s=bank(6):
                                 e.tensor_copy(o, s))
                            sweep_state["snapV"] = b.op(
                                "vector", lambda e, o=prefsnap[:, j, :], s=pref_src:
                                e.tensor_copy(o, s), inc="V")
                        else:
                            b.op("vector", lambda e, o=msnap[:, j % 8, :], s=bank(6),
                                 m=msum_sb[:, :]: e.tensor_add(o, s, m))
                            sweep_state["snapV"] = b.op(
                                "vector", lambda e, o=prefsnap[:, j, :], s=pref_src,
                                m=prefsum_sb[:, :]: e.tensor_add(o, s, m), inc="V")
                    if ktg == 15:
                        b.wait("vector", "T", tfence())
                        b.op("vector", lambda e: e.tensor_copy(msum_sb[:, :], bank(6)))
                        sweep_state["snapV"] = b.op(
                            "vector", lambda e, o=prefsum_sb[:, :], s=pref_src:
                            e.tensor_copy(o, s), inc="V")

        b.wait("sync", "V", p0_done["V"])
        b.wait("vector", "T", p0_done["T"])
        b.wait("vector", "S", p0_done["S"])

        w_src2 = W_attn.rearrange("(a p) c -> p a c", p=128)

        def stream_w(colbase, dst):
            b.wait("vector", "T", b.cnt["T"])
            casts = []
            for dp in range(DC):
                b.wait("sync", "V", casts[dp - 2] if dp >= 2 else b.cnt["V"])
                dd = b.dma(wstage[:, dp % 2, :],
                           w_src2[:, dp, colbase:colbase + 1024], slot=f"w{dp % 2}")
                b.wait("vector", dd, 16)
                casts.append(b.op("vector", lambda e, o=dst[:, dp, :],
                                  s=wstage[:, dp % 2, :]: e.tensor_copy(o, s), inc="V"))
            return b.cnt["V"]

        evacV = {4: 0, 5: 0}

        def proj_loop(tiles, emit_mms, emit_evac):
            for i, tile in enumerate(tiles):
                bk = 4 + i % 2
                b.wait("tensor", "V", evacV[bk])
                emit_mms(bk, tile)
                b.wait("vector", "T", tfence())
                evacV[bk] = emit_evac(bk, tile)

        wV = stream_w(D, wxb)
        b.wait("tensor", "V", wV)

        def k_mms(bk, tile):
            hc, tb2 = tile
            for dc in range(DC):
                b.op("tensor", lambda e, o=bank(bk), l=wxb[:, dc, hc * 128:(hc + 1) * 128],
                     r=lnxh[0][:, dc, tb2 * 512:(tb2 + 1) * 512], st=(dc == 0),
                     sp=(dc == DC - 1): e.matmul(o, l, r, skip_group_check=True, start=st, stop=sp), inc="T")

        def k_evac(bk, tile):
            hc, tb2 = tile
            return b.op("vector", lambda e, o=kf[:, hc, tb2 * 512:(tb2 + 1) * 512],
                        s=bank(bk): e.tensor_copy(o, s), inc="V")
        proj_loop([(hc, tb2) for hc in range(NP) for tb2 in range(4)], k_mms, k_evac)

        if upto == "k":
            dump_src.update(kf=kf, wxb=wxb)
            return finish(nc, b, stack, dump_t, dump_src, b.barrier())

        wV = stream_w(2 * D, wxb)
        b.wait("tensor", "V", wV)

        def v_mms(bk, tile):
            tt, vc2 = tile
            for dc in range(DC):
                b.op("tensor", lambda e, o=bank(bk), l=lnxh[0][:, dc, tt * 128:(tt + 1) * 128],
                     r=wxb[:, dc, vc2 * 512:(vc2 + 1) * 512], st=(dc == 0),
                     sp=(dc == DC - 1): e.matmul(o, l, r, skip_group_check=True, start=st, stop=sp), inc="T")

        def v_evac(bk, tile):
            tt, vc2 = tile
            return b.op("vector", lambda e, o=vt[:, tt, vc2 * 512:(vc2 + 1) * 512],
                        s=bank(bk): e.tensor_scalar_mul(o, s, VSCALE), inc="V")
        proj_loop([(tt, vc2) for tt in range(16) for vc2 in range(2)], v_mms, v_evac)

        if upto == "kv":
            dump_src.update(kf=kf, vt=vt)
            return finish(nc, b, stack, dump_t, dump_src, b.barrier())

        wV = stream_w(0, wxb)
        b.wait("tensor", "V", wV)
        lnx_gh = [h.rearrange("p dc (g c) -> p dc g c", c=128) for h in lnxh]

        def q_mms(bk, tile):
            hc, qb4 = tile
            for dc in range(DC):
                rhs = lnx_gh[qb4 // 2][:, dc, 0::2, :][:, (qb4 % 2) * 4:(qb4 % 2 + 1) * 4, :]
                b.op("tensor", lambda e, o=bank(bk), l=wxb[:, dc, hc * 128:(hc + 1) * 128],
                     r=rhs, st=(dc == 0), sp=(dc == DC - 1):
                     e.matmul(o, l, r, skip_group_check=True, start=st, stop=sp), inc="T")

        def q_evac(bk, tile):
            hc, qb4 = tile
            return b.op("vector", lambda e, o=qf[:, hc, qb4 * 512:(qb4 + 1) * 512],
                        s=bank(bk): e.tensor_scalar_mul(o, s, SCALE_QK), inc="V")

        q_tiles = [(hc, qb4) for hc in range(NP) for qb4 in range(4)]
        for i, tile in enumerate(q_tiles):
            bk = 4 + i % 2
            b.wait("tensor", "V", evacV[bk])
            q_mms(bk, tile)
            qT = tfence()
            if i < 16:
                emit_sweep_kt(i, 0)
            b.wait("vector", "T", qT)
            evacV[bk] = q_evac(bk, tile)
        p1_done = b.barrier()

        dump_src.update(lnxb=lnxh[0], kf=kf, vt=vt, qf=qf)
        if upto == "p1":
            return finish(nc, b, stack, dump_t, dump_src, b.barrier())

        def kts_of(c, half):
            lo, hi = 16 * half, 16 * half + 16
            return list(range(lo, min(8 * c + 8, hi)))

        kblk = [0]
        kcast = {}
        kwT = [0, 0]
        ke_dma = {0: 0, 1: 0}
        kfload = []

        def emit_attention(half):
            b.wait("tensor", "V", b.cnt["V"])
            if K_ILV and half == 0:
                b.wait("sync", "T", p1_done["T"])
            if K_ILV and half == 1:
                for k in kfload:
                    b.wait("tensor", k, 16)
            for c in range(4):
                kts = kts_of(c, half)
                if not kts:
                    continue
                for p in range(NP):
                    avbk = 4 + p % 2
                    av_done = 0
                    nk = len(kts)
                    hc, tb2 = kblk[0] // 4, kblk[0] % 4
                    idle_bk = 4 + (1 - p % 2)
                    if K_ILV and half == 0 and tb2 == 0:
                        b.wait("sync", "V", kcast[hc - 1] if hc > 0 else p1_done["V"])
                        kdd = b.dma(kstage4[:, :, :],
                                    w_src2[:, :, D + hc * 128: D + (hc + 1) * 128],
                                    slot="wk")
                        b.wait("vector", kdd, 16)
                        b.wait("vector", "T", kwT[hc % 2])
                        kcast[hc] = b.op("vector", lambda e, o=wkvh[hc % 2][:, :, :],
                                         s=kstage4[:, :, :]: e.tensor_copy(o, s),
                                         inc="V")

                    def emit_qk(idx):
                        ktg = kts[idx]
                        ktloc = ktg - 16 * half
                        sb = idx % 3
                        jm = ktg // 2 + 1
                        j0 = max(4 * c + 1, jm)
                        coloff = (j0 - (4 * c + 1)) * 128
                        N = 512 - coloff
                        qcols = qf[:, p, c * 512 + coloff: (c + 1) * 512]
                        b.wait("tensor", "S", s_rel[sb])
                        b.wait("tensor", "V", z_rel[sb])
                        for hh in range(2):
                            b.op("tensor", lambda e,
                                 o=bank(SBK[sb] + hh, 0, N),
                                 l=kf[hh * 64:(hh + 1) * 64, p, ktloc * 128:(ktloc + 1) * 128],
                                 r=qcols[hh * 64:(hh + 1) * 64, :], tp=(hh * 64, 0):
                                 e.matmul(o, l, r, skip_group_check=True, start=True, stop=True, tile_position=tp),
                                 inc="T")
                        return b.cnt["T"]

                    def is_m(idx):
                        return 8 * c <= kts[idx] <= 8 * c + 7

                    gcnt, mcnt, pcnt = {}, {}, {}

                    def emit_eg(idx, qkT):
                        ktg = kts[idx]
                        sb = idx % 3
                        masked = is_m(idx)
                        j0 = max(4 * c + 1, ktg // 2 + 1)
                        coloff = (j0 - (4 * c + 1)) * 128
                        N = 512 - coloff
                        b.wait("scalar", "T", qkT)
                        b.wait("scalar", "V", e_rel[sb])
                        b.wait("scalar", "P", e_relP[sb])
                        b.op("scalar", lambda e, o=eb[:, 2 * sb:2 * sb + 2, 0:N],
                             i=ps8[:, SBK[sb]:SBK[sb] + 2, 0:N]:
                             e.activation(o, i, AF.Relu), inc="S")
                        if masked:
                            b.wait("scalar", "V", zw_rel[sb])
                            b.op("scalar", lambda e, o=zb[:, sb, :, :],
                                 i=ps8[:, SBK[sb]:SBK[sb] + 2, 0:128]:
                                 e.activation(o, i, AF.Identity, scale=-0.01,
                                              bias=consts[:, 1:2]), inc="S")
                        s_rel[sb] = b.cnt["S"]
                        b.wait("vector", "S", s_rel[sb])
                        b.wait("vector", "T", g_rel[sb])
                        gcnt[idx] = b.op("vector",
                                         lambda e, o=gb[:, 2 * sb:2 * sb + 2, 0:N],
                                         i=eb[:, 2 * sb:2 * sb + 2, 0:N]:
                                         e.scalar_tensor_tensor(o, i, 0.01, i,
                                                                ALU.add, ALU.mult),
                                         inc="V")
                        pcnt[idx] = 0
                        if masked:
                            midx = ktg % 2
                            zw_rel[sb] = b.op("vector", lambda e, o=zmb[:, sb, :, :],
                                              z=zb[:, sb, :, :], m=masksb[:, midx, :, :]:
                                              e.tensor_mul(o, z, m), inc="V")
                            b.wait("vector", "V", gcnt[idx])
                            mcnt[idx] = b.op("vector",
                                             lambda e, o=gb[:, 2 * sb:2 * sb + 2, 0:128],
                                             m=masksb[:, midx, :, :]:
                                             e.tensor_mul(o, o, m), inc="V")
                        e_rel[sb] = b.cnt["V"]

                    def emit_av(idx):
                        ktg = kts[idx]
                        ktloc = ktg - 16 * half
                        sb = idx % 3
                        jm = ktg // 2 + 1
                        j0 = max(4 * c + 1, jm)
                        coloff = (j0 - (4 * c + 1)) * 128
                        N = 512 - coloff
                        b.wait("tensor", "V", mcnt[idx] if is_m(idx) else gcnt[idx])
                        b.wait("tensor", "P", pcnt[idx])
                        if idx == 0:
                            b.wait("tensor", "V", av_free[avbk])
                            b.wait("tensor", "S", av_freeS[avbk])
                        for hh in range(2):
                            b.op("tensor", lambda e, o=ps[hh * 64:(hh + 1) * 64,
                                 avbk * 512 + coloff: (avbk + 1) * 512],
                                 l=vt[:, ktloc, (2 * p + hh) * 64:(2 * p + hh + 1) * 64],
                                 r=gb[:, 2 * sb + hh, 0:N], st=(idx == 0),
                                 tp=(0, hh * 64):
                                 e.matmul(o, l, r, skip_group_check=True, start=st, stop=True, tile_position=tp),
                                 inc="T")
                        if is_m(idx):
                            for hh in range(2):
                                b.op("tensor", lambda e, o=ps[hh * 64:(hh + 1) * 64,
                                     avbk * 512 + coloff: avbk * 512 + coloff + 128],
                                     l=vt[:, ktloc, (2 * p + hh) * 64:(2 * p + hh + 1) * 64],
                                     r=zmb[:, sb, hh, :], tp=(0, hh * 64):
                                     e.matmul(o, l, r, skip_group_check=True, start=False,
                                              stop=True, tile_position=tp), inc="T")
                        g_rel[sb] = b.cnt["T"]
                        if idx == 0:
                            for i4 in range(4):
                                j = 4 * c + 1 + i4
                                hj = 0 if j <= 9 else 1
                                if hj != half or j < 2:
                                    continue
                                for par in range(2):
                                    b.op("tensor", lambda e,
                                         o=ps[par * 64:(par + 1) * 64,
                                              avbk * 512 + i4 * 128: avbk * 512 + (i4 + 1) * 128],
                                         l=msnap[par * 64:(par + 1) * 64, j % 8,
                                                 p * 64:(p + 1) * 64],
                                         r=qf[par * 64:(par + 1) * 64, p,
                                              (c * 4 + i4) * 128:(c * 4 + i4 + 1) * 128],
                                         tp=(par * 64, par * 64):
                                         e.matmul(o, l, r, skip_group_check=True, start=False, stop=True,
                                                  tile_position=tp), inc="T")
                            g_rel[sb] = b.cnt["T"]

                    fences = {}
                    fences[0] = emit_qk(0)
                    if nk > 1:
                        fences[1] = emit_qk(1)
                    if nk > 2:
                        fences[2] = emit_qk(2)
                    for idx in range(nk):
                        emit_eg(idx, fences[idx])
                        if idx + 3 < nk:
                            fences[idx + 3] = emit_qk(idx + 3)
                        emit_av(idx)
                        if K_ILV and half == 0 and idx < 8:
                            b.wait("tensor", "V", kcast[hc])
                            if idx == 0:
                                b.wait("tensor", "V", av_free[idle_bk])
                                b.wait("tensor", "S", av_freeS[idle_bk])
                            b.op("tensor", lambda e, o=bank(idle_bk),
                                 l=wkvh[hc % 2][:, idx, :],
                                 r=lnxh[1][:, idx, tb2 * 512:(tb2 + 1) * 512],
                                 st=(idx == 0), sp=(idx == 7):
                                 e.matmul(o, l, r, skip_group_check=True, start=st,
                                          stop=sp), inc="T")
                    av_done = b.cnt["T"]
                    b.wait("vector", "T", tfence())
                    if K_ILV and half == 0:
                        if ke_dma[kblk[0] % 2]:
                            b.wait("vector", ke_dma[kblk[0] % 2], 16)
                        kv = b.op("vector", lambda e, o=kevac2[kblk[0] % 2][:, :],
                                  s=bank(idle_bk): e.tensor_copy(o, s), inc="V")
                        av_free[idle_bk] = kv
                        b.wait("sync", "V", kv)
                        ke_dma[kblk[0] % 2] = b.dma(
                            kf2_d[:, hc, tb2 * 512:(tb2 + 1) * 512],
                            kevac2[kblk[0] % 2][:, :], slot=f"ke{kblk[0] % 2}")
                        if tb2 == 3:
                            kwT[hc % 2] = b.cnt["T"]
                        kblk[0] += 1
                    chunk_final = (c <= 2 * half + 1)
                    for i4 in range(4):
                        j = 4 * c + 1 + i4
                        dst = avpart[:, p, (j - 1) * 128: j * 128]
                        src = ps[:, avbk * 512 + i4 * 128: avbk * 512 + (i4 + 1) * 128]
                        if not chunk_final:
                            b.op("vector", lambda e, o=dst, s=src: e.tensor_copy(o, s),
                                 inc="V")
                        elif c <= 1:
                            if j == 1:
                                b.op("vector", lambda e, o=dst, s=src: e.tensor_copy(o, s),
                                     inc="V")
                            else:
                                b.op("vector", lambda e, o=dst, s=src,
                                     pv=prefsnap[:, j, p:p + 1]:
                                     e.tensor_scalar(o, s, pv, None, ALU.add), inc="V")
                        else:
                            b.op("vector", lambda e, o=dst, s=src, pv=prefsnap[:, j, p:p + 1]:
                                 e.scalar_tensor_tensor(o, s, pv, o, ALU.add, ALU.add),
                                 inc="V")
                    av_free[avbk] = b.cnt["V"]
                    av_freeS[avbk] = b.cnt["S"]

        b.wait_all("tensor", p1_done)
        if upto == "sweep":
            dump_src.update(msnap=msnap.ap(), prefsnap=prefsnap.ap())
            return finish(nc, b, stack, dump_t, dump_src, b.barrier())
        emit_attention(0)
        h0_done = b.barrier()

        def p4_sweep(colbase, is_v):
            entry_T = b.cnt["T"]
            casts = []
            stages4 = [view(poolD, 16384, F32, [128, 1024]),
                       view(poolD, 20480, F32, [128, 1024])]
            for dc in range(DC):
                b.wait("sync", "V", casts[dc - 2] if dc >= 2 else b.cnt["V"])
                dd = b.dma(stages4[dc % 2][:, :],
                           W_attn[dc * 128:(dc + 1) * 128, colbase: colbase + 1024])
                b.wait("vector", dd, 16)
                b.wait("vector", "T", entry_T)
                casts.append(b.op("vector", lambda e, o=wkvb[:, dc, :],
                                  s=stages4[dc % 2][:, :]: e.tensor_copy(o, s), inc="V"))
            wready = b.cnt["V"]
            b.wait("tensor", "V", wready)
            if not is_v:
                for tb in range(4):
                    for hc in range(NP):
                        bk = 4 + hc % 2
                        b.wait("tensor", "V", evacV[bk])
                        for dc in range(DC):
                            b.op("tensor", lambda e, o=bank(bk),
                                 l=wkvb[:, dc, hc * 128:(hc + 1) * 128],
                                 r=lnxh[1][:, dc, tb * 512:(tb + 1) * 512],
                                 st=(dc == 0), sp=(dc == DC - 1):
                                 e.matmul(o, l, r, skip_group_check=True, start=st, stop=sp), inc="T")
                        b.wait("vector", "T", tfence())
                        evacV[bk] = b.op("vector", lambda e,
                                         o=kf[:, hc, tb * 512:(tb + 1) * 512], s=bank(bk):
                                         e.tensor_copy(o, s), inc="V")
            else:
                for tt in range(16):
                    for vc2 in range(2):
                        bk = 4 + (tt * 2 + vc2) % 2
                        b.wait("tensor", "V", evacV[bk])
                        for dc in range(DC):
                            b.op("tensor", lambda e, o=bank(bk),
                                 l=lnxh[1][:, dc, tt * 128:(tt + 1) * 128],
                                 r=wkvb[:, dc, vc2 * 512:(vc2 + 1) * 512],
                                 st=(dc == 0), sp=(dc == DC - 1):
                                 e.matmul(o, l, r, skip_group_check=True, start=st, stop=sp), inc="T")
                        b.wait("vector", "T", tfence())
                        evacV[bk] = b.op("vector", lambda e,
                                         o=vt[:, tt, vc2 * 512:(vc2 + 1) * 512],
                                         s=bank(bk): e.tensor_scalar_mul(o, s, VSCALE),
                                         inc="V")
                    emit_sweep_kt(16 + tt, 1, vt_ready_V=evacV[4 + (tt * 2 + 1) % 2])

        b.wait("sync", "T", h0_done["T"])
        b.wait("sync", "S", h0_done["S"])
        b.wait_all("vector", h0_done, sems=("T", "S"))
        if K_ILV:
            b.wait("sync", ke_dma[0], 16)
            b.wait("sync", ke_dma[1], 16)
            for i in range(4):
                kfload.append(b.dma(kf[:, :, 512 * i:512 * (i + 1)],
                                    kf2_d[:, :, 512 * i:512 * (i + 1)]))
            b.wait("vector", ke_dma[0], 16)
            b.wait("vector", ke_dma[1], 16)
        else:
            p4_sweep(D, is_v=False)
        p4_sweep(2 * D, is_v=True)
        p4_done = b.barrier()

        b.wait_all("tensor", p4_done)
        emit_attention(1)
        attn_done = b.barrier()

        dump_src.update(msnap=msnap.ap(), prefsnap=prefsnap.ap(), avpart=avpart)
        if upto == "attn":
            return finish(nc, b, stack, dump_t, dump_src, b.barrier())

        b.wait("sync", "T", attn_done["T"])
        b.wait("sync", "S", attn_done["S"])
        b.wait("sync", "V", attn_done["V"])
        casts = []
        for p in range(NP):
            if p >= 1:
                b.wait("sync", "V", casts[p - 1])
            dd = b.dma(wstage6[:, :], W_projd[p * 128:(p + 1) * 128, :])
            b.wait("vector", dd, 16)
            b.wait("vector", "T", attn_done["T"])
            casts.append(b.op("vector", lambda e, o=wprojb[:, p, :], s=wstage6[:, :]:
                              e.tensor_copy(o, s), inc="V"))
        wproj_V = b.cnt["V"]
        b.wait("tensor", "V", wproj_V)

        wstage6b = view(poolC, 49152, F32, [128, 1024])

        def stream_ffn(pairs):
            cc = [wproj_V, wproj_V]
            stages = [wstage6, wstage6b]
            for i, (dsl, ssl) in enumerate(pairs):
                n = prod(ssl.shape[1:])
                assert n <= 1024
                stage = stages[i % 2][:, 0:n]
                if len(ssl.shape) == 3:
                    stage = stage.rearrange("p (a c) -> p a c", a=ssl.shape[1])
                b.wait("scalar", "V", cc[-2])
                dd = b.dma2(stage, ssl)
                b.wait("vector", dd, 16)
                cc.append(b.op("vector", lambda e, o=dsl, s=stage: e.tensor_copy(o, s),
                               inc="V"))
            return b.cnt["V"]

        u1_src = U1d.rearrange("(dc p) r -> p dc r", p=128)
        v1_src = V1d.rearrange("(rc p) f -> p rc f", p=128)
        u2_src = U2d.rearrange("(fc p) r -> p fc r", p=128)
        v2_src = V2d.rearrange("(rc p) d -> p rc d", p=128)
        pairs = []
        for i in range(2):
            pairs.append((u1b[:, 4 * i:4 * (i + 1), :], u1_src[:, 4 * i:4 * (i + 1), :]))
        for rc in range(2):
            for sseg in range(4):
                pairs.append((v1b[:, rc, sseg * 1024:(sseg + 1) * 1024],
                              v1_src[:, rc, sseg * 1024:(sseg + 1) * 1024]))
        for i in range(8):
            pairs.append((u2b[:, 4 * i:4 * (i + 1), :], u2_src[:, 4 * i:4 * (i + 1), :]))
        for rc in range(2):
            pairs.append((v2b[:, rc, :], v2_src[:, rc, :]))

        evac2 = {0: 0, 1: 0}
        x_own_src = x_own_t.rearrange("(dc p) t -> p dc t", p=128)
        xdd = {}

        def xo_dma(c, h):
            xdd[(c, h)] = b.dma(xown[:, 4 * h:4 * h + 4, :],
                                x_own_src[:, 4 * h:4 * h + 4, c * 512:(c + 1) * 512])
        xo_dma(0, 0)
        xo_dma(0, 1)
        b.wait("tensor", "S", attn_done["S"])
        for c in range(4):
            for dcol in range(DC):
                bk = dcol % 2
                if dcol % 4 == 0:
                    b.wait("vector", xdd[(c, dcol // 4)], 16)
                b.wait("tensor", "V", evac2[bk])
                b.wait("tensor", "V", attn_done["V"])
                for p in range(NP):
                    b.op("tensor", lambda e, o=bank(bk), l=wprojb[:, p, dcol * 128:(dcol + 1) * 128],
                         r=avpart[:, p, c * 512:(c + 1) * 512], st=(p == 0), sp=(p == NP - 1):
                         e.matmul(o, l, r, skip_group_check=True, start=st, stop=sp), inc="T")
                b.wait("vector", "T", tfence())
                evac2[bk] = b.op("vector", lambda e, o=resid[:, dcol, c * 512:(c + 1) * 512],
                                 x=xown[:, dcol, :], s=bank(bk): e.tensor_add(o, x, s), inc="V")
                if dcol in (3, 7) and c + 1 < 4:
                    b.wait("sync", "V", b.cnt["V"])
                    xo_dma(c + 1, dcol // 4)
        wffn_V = stream_ffn(pairs)
        p6a_done = b.barrier()

        dump_src.update(resid=resid)
        if upto == "p6a":
            return finish(nc, b, stack, dump_t, dump_src, b.barrier())

        rows6 = (row6_mean, row6_var, l2_rstd, l2_nm)
        b.wait("vector", "T", p6a_done["T"])
        for c in range(4):
            src = resid[:, :, c * 512:(c + 1) * 512]
            rows_c = (row6_mean, row6_var,
                      l2_rstd[0:1, c * 512:(c + 1) * 512].rearrange("p c -> p c") if False
                      else l2_rstd, l2_nm)
            emit_ln_stats_mixed(src, rows_c, c * 512, raw6, sq6)
        p6b_done = b.barrier()

        b.wait("tensor", "V", wffn_V)

        ydma = []
        lnx2b_ = [lnx2b, view(poolC, 47104, BF16, [128, DC, 512])]
        ln_done = [0, 0, 0, 0]
        tmp6_ = [tmp6] + [view(poolD, k, F32, [128, 512]) for k in (2048, 4096, 6144)]
        ln2_prev = {}

        def emit_ln2_normalize(c):
            nonlocal ln2_prev
            ccols = slice(c * 512, (c + 1) * 512)
            b.wait("tensor", "S", b.cnt["S"])
            b.wait("tensor", "V", tail_ybufV[0])
            b.op("tensor", lambda e, o=bank(6), l=onesrow[0:1, :],
                 r=l2_rstd[0:1, c * 512:(c + 1) * 512]:
                 e.matmul(o, l, r, skip_group_check=True, start=True, stop=True))
            b.op("tensor", lambda e, o=bank(7), l=onesrow[0:1, :],
                 r=l2_nm[0:1, c * 512:(c + 1) * 512]:
                 e.matmul(o, l, r, skip_group_check=True, start=True, stop=True), inc="T")
            b.wait("vector", "T", tfence())
            m6 = {}
            a6 = {}

            def n6mul(dc):
                b.wait("vector", "V",
                       a6[dc - 4] if dc >= 4 else ln2_prev.get(dc + 4, 0))
                m6[dc] = b.op("vector", lambda e, o=tmp6_[dc % 4], s=resid[:, dc, ccols],
                              r=bank(6): e.tensor_mul(o, s, r), inc="V")
            n6mul(0)
            n6mul(1)
            for dc in range(DC):
                if dc + 2 < DC:
                    n6mul(dc + 2)
                b.wait("vector", "V", m6[dc])
                a6[dc] = b.op("vector", lambda e, o=lnx2b_[c % 2][:, dc, :],
                              t=tmp6_[dc % 4], n=bank(7): e.tensor_add(o, t, n), inc="V")
            ln2_prev = a6
            ln_done[c] = b.cnt["V"]

        b.wait("vector", "V", p6b_done["V"])
        pending_tail = [None]
        prev_ycV = [0]
        tail_ybufV = [0]
        emit_ln2_normalize(0)
        for c in range(4):
            ccols = slice(c * 512, (c + 1) * 512)
            b.wait("tensor", "V", ln_done[c])
            b.wait("tensor", "V", prev_ycV[0])
            for rc2 in range(2):
                for dc in range(DC):
                    b.op("tensor", lambda e, o=bank(rc2), l=u1b[:, dc, rc2 * 128:(rc2 + 1) * 128],
                         r=lnx2b_[c % 2][:, dc, :], st=(dc == 0), sp=(dc == DC - 1):
                         e.matmul(o, l, r, skip_group_check=True, start=st, stop=sp), inc="T")
            b.wait("vector", "T", tfence())
            p1V = b.op("vector", lambda e: e.tensor_copy(p1bv[:, 0, :], bank(0)))
            p1V = b.op("vector", lambda e: e.tensor_copy(p1bv[:, 1, :], bank(1)), inc="V")
            if pending_tail[0] is not None:
                pending_tail[0]()
                pending_tail[0] = None
            b.wait("tensor", "V", p1V)
            gelu_S = [0, 0]
            q2_T = [0, 0]
            hfence = {}

            def emit_h(fc):
                hbk = 2 + fc % 2
                b.wait("tensor", "S", gelu_S[fc % 2])
                for rc2 in range(2):
                    b.op("tensor", lambda e, o=bank(hbk), l=v1b[:, rc2, fc * 128:(fc + 1) * 128],
                         r=p1bv[:, rc2, :], st=(rc2 == 0), sp=(rc2 == 1):
                         e.matmul(o, l, r, skip_group_check=True, start=st, stop=sp), inc="T")
                return tfence()

            hfence[0] = emit_h(0)
            hfence[1] = emit_h(1)
            if c + 1 < 4:
                emit_ln2_normalize(c + 1)
            for fc in range(32):
                b.wait("scalar", "T", hfence[fc])
                b.wait("scalar", "T", q2_T[fc % 2])
                gelu_S[fc % 2] = b.op("scalar", lambda e, o=hb2[:, fc % 2, :],
                                      i=bank(2 + fc % 2), bb=b1s[:, fc:fc + 1]:
                                      e.activation(o, i, AF.Gelu_apprx_tanh, bias=bb), inc="S")
                if fc + 2 < 32:
                    hfence[fc + 2] = emit_h(fc + 2)
                b.wait("tensor", "S", gelu_S[fc % 2])
                for rc2 in range(2):
                    b.op("tensor", lambda e, o=bank(4 + rc2),
                         l=u2b[:, fc, rc2 * 128:(rc2 + 1) * 128],
                         r=hb2[:, fc % 2, :], st=(fc == 0), sp=(fc == 31):
                         e.matmul(o, l, r, skip_group_check=True, start=st, stop=sp), inc="T")
                q2_T[fc % 2] = b.cnt["T"]
            b.wait("vector", "T", tfence())
            q2V = b.op("vector", lambda e: e.tensor_copy(q2bv[:, 0, :], bank(4)))
            q2V = b.op("vector", lambda e: e.tensor_copy(q2bv[:, 1, :], bank(5)), inc="V")
            b.wait("tensor", "V", q2V)
            oev = {0: 0, 1: 0}
            for dcol in range(DC):
                obk = dcol % 2
                b.wait("tensor", "V", oev[obk])
                for rc2 in range(2):
                    b.op("tensor", lambda e, o=bank(obk), l=v2b[:, rc2, dcol * 128:(dcol + 1) * 128],
                         r=q2bv[:, rc2, :], st=(rc2 == 0), sp=(rc2 == 1):
                         e.matmul(o, l, r, skip_group_check=True, start=st, stop=sp), inc="T")
                b.wait("vector", "T", tfence())
                oev[obk] = b.op("vector", lambda e, o=y_cv[:, dcol, :], s=bank(obk),
                                bb=b2s[:, dcol:dcol + 1], rr=resid[:, dcol, ccols]:
                                e.scalar_tensor_tensor(o, s, bb, rr, ALU.add, ALU.add), inc="V")
            ycV = b.cnt["V"]
            prev_ycV[0] = ycV

            def make_tail(c=c, ycV=ycV):
                def tail():
                    tpos_done = ycV
                    if ydma:
                        b.wait("vector", ydma[-1], 16)
                    for tt4 in range(4):
                        b.wait("tensor", "V", tpos_done)
                        for dcol in range(DC):
                            tb_ = 6 + dcol // 4
                            b.op("tensor", lambda e,
                                 o=bank(tb_, (dcol % 4) * 128, (dcol % 4 + 1) * 128),
                                 i=y_cv[:, dcol, tt4 * 128:(tt4 + 1) * 128]:
                                 e.transpose(o, i, identf[:, :]), inc="T")
                        b.wait("vector", "T", tfence())
                        b.op("vector", lambda e, o=ybuf[:, tt4, 0:512], s=bank(6):
                             e.tensor_copy(o, s))
                        tpos_done = b.op("vector", lambda e, o=ybuf[:, tt4, 512:1024],
                                         s=bank(7): e.tensor_copy(o, s), inc="V")
                    tail_ybufV[0] = tpos_done
                    b.wait("sync", "V", tpos_done)
                    ydst = y_out.rearrange("(c tt p) d -> c p tt d", p=128, tt=4)[c]
                    ydma.append(b.dma(ydst, ybuf[:, :, :]))
                return tail
            pending_tail[0] = make_tail()

        pending_tail[0]()
        return finish(nc, b, stack, dump_t, dump_src, b.barrier())


def finish(nc, b, stack, dump_t, dump_src, counts):
    b.wait_all("sync", counts, sems=("T", "S", "V", "P"))
    for name, ap in dump_t.items():
        src = dump_src[name]
        flat = src
        nd = len(flat.shape)
        if nd == 3:
            flat = flat.rearrange("p a b -> p (a b)")
        elif nd == 4:
            flat = flat.rearrange("p a b c -> p (a b c)")
        b.dma(ap, flat)
    for i in range(b.dma_n):
        b.wait("sync", f"dma{i}", 16)
    block = stack.enter_context(nc.Block())
    b.replay(block)
    return nc


def _host_inputs(core, x, W_attn, W_proj, U1, V1, b1, U2, V2, b2):
    seq, cls = core // 2, core % 2
    xt = np.ascontiguousarray(x[seq].T)
    perm = np.arange(32)
    if cls == 1:
        perm = perm.reshape(16, 2)[:, ::-1].reshape(32)
    xt_p = np.ascontiguousarray(
        xt.reshape(D, 32, 128)[:, perm, :].reshape(D, T))
    own_tiles = perm[0::2]
    x_own = np.ascontiguousarray(
        xt.reshape(D, 32, 128)[:, own_tiles, :].reshape(D, OWNT))
    tril = np.tril(np.ones((128, 128), np.float32))
    trilT = tril.T
    m0 = trilT
    m1 = np.zeros((128, 128), np.float32) if cls == 0 else np.ones((128, 128), np.float32)
    masks = np.stack([np.stack([m0, m0]), np.stack([m1, m1])])
    masks = np.ascontiguousarray(masks.transpose(2, 0, 1, 3)).astype(ml_dtypes.bfloat16)
    return {
        "x_t": xt_p, "x_own_t": x_own,
        "W_attn": W_attn, "W_proj": W_proj, "U1": U1, "V1": V1,
        "U2": U2, "V2": V2,
        "b1l": np.ascontiguousarray(b1.reshape(DFF // 128, 128).T),
        "b2l": np.ascontiguousarray(b2.reshape(D // 128, 128).T),
        "masks": masks,
        "identb": np.eye(128, dtype=np.float32).astype(ml_dtypes.bfloat16),
        "identf": np.eye(128, dtype=np.float32),
        "onescol": np.ones((128, 1), np.float32).astype(ml_dtypes.bfloat16),
        "onesrow": np.ones((1, 128), np.float32),
    }, own_tiles


_NC_CACHE = {}


def kernel(L0_attn_x, L0_attn_W_attn, L0_attn_W_proj, L0_ffn_U1, L0_ffn_V1,
           L0_ffn_b1, L0_ffn_U2, L0_ffn_V2, L0_ffn_b2):
    x = np.asarray(L0_attn_x, dtype=np.float32)
    args = tuple(np.ascontiguousarray(np.asarray(a, dtype=np.float32)) for a in
                 (L0_attn_W_attn, L0_attn_W_proj, L0_ffn_U1, L0_ffn_V1,
                  L0_ffn_U2, L0_ffn_V2))
    b1 = np.asarray(L0_ffn_b1, np.float32)
    b2 = np.asarray(L0_ffn_b2, np.float32)
    if "nc" not in _NC_CACHE:
        _NC_CACHE["nc"] = build_program("full")
    nc = _NC_CACHE["nc"]
    in_maps = []
    owns = []
    for core in range(8):
        m, own_tiles = _host_inputs(core, x, args[0], args[1], args[2], args[3],
                                    b1, args[4], args[5], b2)
        in_maps.append(m)
        owns.append(own_tiles)
    res = run_bass_kernel_spmd(nc, in_maps, core_ids=list(range(8)))
    out = np.empty((4, T, D), np.float32)
    for core in range(8):
        seq = core // 2
        yc = res.results[core]["y"]
        rows = (owns[core][:, None] * 128 + np.arange(128)[None, :]).reshape(-1)
        out[seq, rows, :] = yc
    return out
